# version 08
# baseline (speedup 1.0000x reference)
import math
from contextlib import ExitStack

import numpy as np
import concourse.bass as bass
import concourse.mybir as mybir
from concourse.bass_utils import run_bass_kernel_spmd

F32 = mybir.dt.float32
BF16 = mybir.dt.bfloat16
I8 = mybir.dt.int8
I32 = mybir.dt.int32
AF = mybir.ActivationFunctionType
ALU = mybir.AluOpType
AX = mybir.AxisListType

D = 2048
NCH = D // 128
SEQ = 4096
TOK = 1024
HD = 128
H_PER = 4
EPS_DN = 1e-6
EPS_LN = 1e-5
INV_D = 1.0 / D


class Ctx:

    ENGINES = ("tensor", "vector", "scalar", "gpsimd", "sync")

    def __init__(self, nc, es):
        self.nc = nc
        self.es = es
        self.streams = {e: [] for e in self.ENGINES}
        self._h = {}
        self._n = {}
        for nm in ("ld", "mm", "vec", "scl", "gp", "sy", "st", "cc"):
            self._h[nm] = es.enter_context(nc.semaphore(nm))
            self._n[nm] = 0

    def dsem(self, nm):
        if nm not in self._h:
            self._h[nm] = self.es.enter_context(self.nc.semaphore(nm))
            self._n[nm] = 0
        return nm

    def val(self, nm):
        return self._n[nm]

    CHAIN = {"vector": "vec", "scalar": "scl", "gpsimd": "gp"}

    def op(self, eng, fn, inc=None, inck=1, chain=True):
        own = self.CHAIN.get(eng) if chain else None
        if own is not None and (inc is None or inc == own):
            h = self._h[own]
            prev = self._n[own]
            self.streams[eng].append(
                lambda e, fn=fn, h=h, p=prev: (e.wait_ge(h, p) if p else None,
                                               fn(e).then_inc(h, 1))[1])
            self._n[own] += 1
            return self._n[own]
        if inc is None:
            self.streams[eng].append(fn)
            return None
        h = self._h[inc]
        if own is not None:
            ho = self._h[own]
            prev = self._n[own]
            self.streams[eng].append(
                lambda e, fn=fn, h=h, k=inck, ho=ho, p=prev: (
                    e.wait_ge(ho, p) if p else None, fn(e).then_inc(h, k))[1])
        else:
            self.streams[eng].append(lambda e, fn=fn, h=h, k=inck: fn(e).then_inc(h, k))
        self._n[inc] += inck
        return self._n[inc]

    def wait(self, eng, sem, val=None):
        v = self._n[sem] if val is None else val
        if v <= 0:
            return
        h = self._h[sem]
        self.streams[eng].append(lambda e, h=h, v=v: e.wait_ge(h, v))

    def barrier(self):
        vals = {nm: v for nm, v in self._n.items() if v > 0}
        for eng in self.ENGINES:
            for sem, v in vals.items():
                self.wait(eng, sem, v)

    def mm(self, o, l, r, start, stop, inc=None):
        return self.op("tensor",
                       lambda t, o=o, l=l, r=r, a=start, b=stop: t.matmul(o, l, r, start=a, stop=b),
                       inc=inc)

    def trans(self, o, i, ident, inc=None):
        return self.op("tensor", lambda t, o=o, i=i, d=ident: t.transpose(o, i, d), inc=inc)

    def tt(self, eng, o, a, b, alu, inc=None):
        return self.op(eng, lambda v, o=o, a=a, b=b, u=alu: v.tensor_tensor(o, a, b, u), inc=inc)

    def ts(self, eng, o, a, s1, s2, op0, op1, inc=None):
        return self.op(eng, lambda v, o=o, a=a, s1=s1, s2=s2, p0=op0, p1=op1:
                       v.tensor_scalar(o, a, s1, s2, p0, p1), inc=inc)

    def stt(self, eng, o, a, s, b, op0, op1, inc=None):
        return self.op(eng, lambda v, o=o, a=a, s=s, b=b, p0=op0, p1=op1:
                       v.scalar_tensor_tensor(o, a, s, b, p0, p1), inc=inc)

    def act(self, o, a, fname, bias=None, scale=None, accum=None, inc=None):
        def f(s, o=o, a=a, fn=fname, bi=bias, sc=scale, ac=accum):
            kw = {}
            if bi is not None:
                kw["bias"] = bi
            if sc is not None:
                kw["scale"] = sc
            if ac is not None:
                kw["accum_out"] = ac
            return s.activation(o, a, fn, **kw)
        return self.op("scalar", f, inc=inc)

    def copy(self, eng, o, a, inc=None):
        return self.op(eng, lambda v, o=o, a=a: v.tensor_copy(o, a), inc=inc)

    def dma(self, eng, o, a, inc="ld", transpose=False):
        if inc not in self._h:
            self.dsem(inc)
        if transpose:
            return self.op(eng, lambda e, o=o, a=a: e.dma_start(out=o, in_=a, transpose=True),
                           inc=inc, inck=16, chain=False)
        return self.op(eng, lambda e, o=o, a=a: e.dma_start(out=o, in_=a), inc=inc, inck=16,
                       chain=False)

    def finalize(self):
        blk = self.es.enter_context(self.nc.Block())
        streams = self.streams

        @blk.tensor
        def _(t):
            for f in streams["tensor"]:
                f(t)

        @blk.vector
        def _(v):
            for f in streams["vector"]:
                f(v)

        @blk.scalar
        def _(s):
            for f in streams["scalar"]:
                f(s)

        @blk.gpsimd
        def _(g):
            for f in streams["gpsimd"]:
                f(g)

        @blk.sync
        def _(sy):
            for f in streams["sync"]:
                f(sy)


def sbt(ctx, es, name, shape, dtype):
    return es.enter_context(ctx.nc.sbuf_tensor(name, shape, dtype))


def pst(ctx, es, name, shape, dtype=F32):
    elems = 512 if dtype == F32 else 1024
    shape = [shape[0], max(shape[1], elems)]
    return es.enter_context(ctx.nc.psum_tensor(name, shape, dtype))


CONST_SPECS = {
    "ident_bf": ([128, 128], BF16),
    "ident_f32": ([128, 128], F32),
    "ones_row_bf": ([1, 128], BF16),
    "ones_row_f32": ([1, 128], F32),
    "ones_col_bf": ([128, 1], BF16),
    "mask_sL": ([128, 128], F32),
    "mask_iU": ([128, 128], BF16),
    "mask_bd": ([128, 128], F32),
    "mask_of": ([128, 128], F32),
    "diagmask": ([128, 4 * 512], BF16),
}


def host_consts():
    import ml_dtypes
    ii = np.arange(128)
    c = {}
    c["ident_bf"] = np.eye(128, dtype=np.float32)
    c["ident_f32"] = np.eye(128, dtype=np.float32)
    c["ones_row_bf"] = np.ones((1, 128), np.float32)
    c["ones_row_f32"] = np.ones((1, 128), np.float32)
    c["ones_col_bf"] = np.ones((128, 1), np.float32)
    c["mask_sL"] = (ii[:, None] > ii[None, :]).astype(np.float32)
    c["mask_iU"] = (ii[:, None] <= ii[None, :]).astype(np.float32)
    c["mask_bd"] = (((ii[:, None] // 32) == (ii[None, :] // 32))
                    & (ii[:, None] > ii[None, :])).astype(np.float32)
    c["mask_of"] = ((ii[:, None] // 32) > (ii[None, :] // 32)).astype(np.float32)
    dm = np.zeros((128, 4 * 512), np.float32)
    for j in range(4):
        dm[:, j * 512:(j + 1) * 512] = ((j * 128 + ii[:, None]) <= np.arange(512)[None, :])
    c["diagmask"] = dm
    out = {}
    for k, (shape, dt) in CONST_SPECS.items():
        a = np.ascontiguousarray(c[k].reshape(shape))
        out[k] = a.astype(ml_dtypes.bfloat16) if dt == BF16 else a.astype(np.float32)
    return out


def emit_load_consts(ctx, es, dr):
    C = {}
    for k, (shape, dt) in CONST_SPECS.items():
        C[k] = sbt(ctx, es, "c_" + k, shape, dt)
        ctx.dma("gpsimd", C[k][:], dr[k])
    return C


def emit_ln_fm(ctx, es, C, r_fm, T, tag, sink):
    with ExitStack() as ph:
        sq = sbt(ctx, ph, f"lnsq{tag}", [128, 2 * 512], BF16)
        rows = sbt(ctx, ph, f"lnrw{tag}", [1, 6 * 512], F32)
        p_sum = pst(ctx, ph, f"lnps{tag}", [1, 512])
        p_sq = pst(ctx, ph, f"lnpq{tag}", [1, 512])
        p_a = pst(ctx, ph, f"lnpa{tag}", [128, 512])
        p_b = pst(ctx, ph, f"lnpb{tag}", [128, 512])

        def R(i):
            return rows[:, i * 512:(i + 1) * 512]

        applied = 0
        for g in range(T // 512):
            rsl = [r_fm[:, k * T + g * 512:k * T + g * 512 + 512] for k in range(NCH)]
            ctx.wait("tensor", "vec", applied)
            for k in range(NCH):
                ctx.mm(p_sum[:], C["ones_col_bf"][:], rsl[k], k == 0, k == NCH - 1, inc="mm")
            mmv = {}
            for k in range(NCH):
                if k >= 2:
                    ctx.wait("scalar", "mm", mmv[k - 2])
                sqs = sq[:, (k % 2) * 512:(k % 2) * 512 + 512]
                sv = ctx.act(sqs, rsl[k], AF.Square, inc="scl")
                ctx.wait("tensor", "scl", sv)
                mmv[k] = ctx.mm(p_sq[:], C["ones_col_bf"][:], sqs, k == 0, k == NCH - 1, inc="mm")
            ctx.wait("vector", "mm")
            ctx.ts("vector", R(0), p_sum[:], INV_D, 1.0, ALU.mult, ALU.mult)
            ctx.ts("vector", R(1), p_sq[:], INV_D, 1.0, ALU.mult, ALU.mult)
            ctx.tt("vector", R(4), R(0), R(0), ALU.mult)
            ctx.tt("vector", R(1), R(1), R(4), ALU.subtract)
            vv = ctx.ts("vector", R(1), R(1), 1.0, EPS_LN, ALU.mult, ALU.add, inc="vec")
            ctx.wait("scalar", "vec", vv)
            sv = ctx.act(R(4), R(1), AF.Sqrt, inc="scl")
            ctx.wait("vector", "scl", sv)
            ctx.op("vector", lambda v, o=R(2), a=R(4): v.reciprocal(o, a))
            ctx.tt("vector", R(3), R(0), R(2), ALU.mult)
            vv = ctx.ts("vector", R(3), R(3), -1.0, 1.0, ALU.mult, ALU.mult, inc="vec")
            ctx.wait("tensor", "vec", vv)
            ctx.mm(p_a[:], C["ones_row_f32"][:], R(2), True, True)
            bcv = ctx.mm(p_b[:], C["ones_row_f32"][:], R(3), True, True, inc="mm")
            ctx.wait("vector", "mm", bcv)
            for k in range(NCH):
                sink(g, k, rsl[k], p_a[:], p_b[:])
            applied = ctx.val("vec")


def ln_sink_simple(ctx, out_fm, T):
    def sink(g, k, rsl, a, b):
        osl = out_fm[:, k * T + g * 512:k * T + g * 512 + 512]
        ctx.tt("vector", osl, rsl, a, ALU.mult)
        ctx.tt("vector", osl, osl, b, ALU.add, inc="vec")
    return sink


def emit_l1ffn(ctx, es, C, x_fm, wg8, wu8, wd8, sg_t, su_t, sd_t, out_dram):
    with ExitStack() as ph:
        r_fm = sbt(ctx, ph, "f_r", [128, NCH * TOK], BF16)
        inner = ExitStack()
        fbuf = sbt(ctx, inner, "f_F", [128, 32 * 512], BF16)
        wgt = sbt(ctx, inner, "f_wg", [128, 16 * 512], BF16)
        wut = sbt(ctx, inner, "f_wu", [128, 16 * 512], BF16)
        wdt = sbt(ctx, inner, "f_wd", [128, 32 * 512], BF16)
        gtmp = sbt(ctx, inner, "f_gt", [128, 2 * 512], BF16)
        p_g = [pst(ctx, inner, f"f_pg{i}", [128, 512]) for i in range(2)]
        p_u = [pst(ctx, inner, f"f_pu{i}", [128, 512]) for i in range(2)]
        p_d = [pst(ctx, inner, f"f_pd{i}", [128, 512]) for i in range(2)]

        def colslice_dma(dst, src, c0, ncol, row0, nrow, sem):
            o = dst[:].rearrange("p (c n) -> p c n", n=ncol)
            i = src[row0:row0 + nrow, c0:c0 + ncol].rearrange("(c p) n -> p c n", p=128)
            return ctx.dma("gpsimd", o, i, inc=sem)

        for th in range(2):
            tok0 = th * 512
            xsl = [x_fm[:, k * TOK + tok0:k * TOK + tok0 + 512] for k in range(NCH)]
            for hh in range(2):
                ctx.wait("vector", "mm")
                prev_mm = ctx.val("mm")
                fdv = {0: 0, 1: 0}
                sdv = {0: 0, 1: 0}
                for hs in range(8):
                    ctx.wait("gpsimd", "mm", prev_mm)
                    gv_ = colslice_dma(wgt, wg8, hh * 4096 + hs * 512, 512, 0, D, "s_wg")
                    uv_ = colslice_dma(wut, wu8, hh * 4096 + hs * 512, 512, 0, D, "s_wu")
                    ctx.wait("tensor", "s_wg", gv_)
                    ctx.wait("tensor", "s_wu", uv_)
                    for hc in range(4):
                        sl_ = hc % 2
                        hci = hh * 32 + hs * 4 + hc
                        fi = hs * 4 + hc
                        fsl = fbuf[:, fi * 512:fi * 512 + 512]
                        gsl = gtmp[:, sl_ * 512:sl_ * 512 + 512]
                        ctx.wait("tensor", "vec", fdv[sl_])
                        ctx.wait("tensor", "scl", sdv[sl_])
                        mg = mu = None
                        for k in range(NCH):
                            mg = ctx.mm(p_g[sl_][:], wgt[:, (k * 4 + hc) * 128:(k * 4 + hc) * 128 + 128],
                                        xsl[k], k == 0, k == NCH - 1,
                                        inc="mm" if k == NCH - 1 else None)
                        for k in range(NCH):
                            mu = ctx.mm(p_u[sl_][:], wut[:, (k * 4 + hc) * 128:(k * 4 + hc) * 128 + 128],
                                        xsl[k], k == 0, k == NCH - 1,
                                        inc="mm" if k == NCH - 1 else None)
                        prev_mm = mu
                        ctx.wait("vector", "mm", mg)
                        gv = ctx.ts("vector", gsl, p_g[sl_][:], sg_t[:, hci:hci + 1], 1.0,
                                    ALU.mult, ALU.mult, inc="vec")
                        ctx.wait("scalar", "vec", gv)
                        sv = ctx.act(gsl, gsl, AF.Silu, inc="scl")
                        sdv[sl_] = sv
                        ctx.wait("vector", "mm", mu)
                        ctx.ts("vector", fsl, p_u[sl_][:], su_t[:, hci:hci + 1], 1.0,
                               ALU.mult, ALU.mult)
                        ctx.wait("vector", "scl", sv)
                        fdv[sl_] = ctx.tt("vector", fsl, fsl, gsl, ALU.mult, inc="vec")
                ctx.wait("vector", "scl")
                fdone = ctx.val("vec")
                prev_dmm = ctx.val("mm")
                ddv = {0: 0, 1: 0}
                for ds in range(4):
                    ctx.wait("gpsimd", "mm", prev_dmm)
                    ldv = colslice_dma(wdt, wd8, ds * 512, 512, hh * 4096, 4096, "s_wd")
                    ctx.wait("tensor", "s_wd", ldv)
                    ctx.wait("tensor", "vec", fdone)
                    for dc in range(4):
                        sl_ = dc % 2
                        dci = ds * 4 + dc
                        pd = p_d[sl_]
                        gsl = gtmp[:, sl_ * 512:sl_ * 512 + 512]
                        ctx.wait("tensor", "vec", ddv[sl_])
                        mv = None
                        for hc in range(32):
                            mv = ctx.mm(pd[:], wdt[:, (hc * 4 + dc) * 128:(hc * 4 + dc) * 128 + 128],
                                        fbuf[:, hc * 512:hc * 512 + 512], hc == 0, hc == 31,
                                        inc="mm" if hc == 31 else None)
                        prev_dmm = mv
                        ctx.wait("vector", "mm", mv)
                        rsl = r_fm[:, dci * TOK + tok0:dci * TOK + tok0 + 512]
                        ctx.ts("vector", gsl, pd[:], sd_t[:, dci:dci + 1], 1.0,
                               ALU.mult, ALU.mult)
                        if hh == 0:
                            ddv[sl_] = ctx.tt("vector", rsl, gsl, xsl[dci], ALU.add, inc="vec")
                        else:
                            ddv[sl_] = ctx.tt("vector", rsl, rsl, gsl, ALU.add, inc="vec")
        ctx.barrier()
        inner.close()
        y32 = sbt(ctx, ph, "f_y32", [128, NCH * 512], F32)
        ytm = sbt(ctx, ph, "f_ytm", [128, 2048], F32)
        p_t = [pst(ctx, ph, f"f_pt{i}", [128, 512]) for i in range(2)]

        def sink(g, k, rsl, a, b):
            osl = y32[:, k * 512:k * 512 + 512]
            ctx.tt("vector", osl, rsl, a, ALU.mult)
            ctx.tt("vector", osl, osl, b, ALU.add, inc="vec")
            if k == NCH - 1:
                _store_group(ctx, C, y32, ytm, p_t, out_dram, g)

        emit_ln_fm(ctx, ph, C, r_fm, TOK, "f", sink)
        ctx.wait("sync", "st")


def _store_group(ctx, C, y32, ytm, p_t, out_dram, g):
    yv = ctx.val("vec")
    ctx.wait("tensor", "vec", yv)
    evv = {}
    for tc in range(4):
        stv = ctx.val("st")
        for k in range(NCH):
            i = tc * NCH + k
            pt = p_t[i % 2]
            if i >= 2:
                ctx.wait("tensor", "vec", evv[i - 2])
            mv = ctx.trans(pt[:, 0:128], y32[:, k * 512 + tc * 128:k * 512 + tc * 128 + 128],
                           C["ident_f32"][:], inc="mm")
            ctx.wait("vector", "mm", mv)
            if k == 0 and stv > 0:
                ctx.wait("vector", "st", stv)
            evv[i] = ctx.copy("vector", ytm[:, k * 128:k * 128 + 128], pt[:, 0:128], inc="vec")
        ev = ctx.val("vec")
        ctx.wait("sync", "vec", ev)
        row0 = g * 512 + tc * 128
        ctx.dma("sync", out_dram[row0:row0 + 128, :], ytm[:], inc="st")


def kernel(**inputs):
    raise NotImplementedError("full build assembled incrementally; see scratch drivers")


def emit_moe(ctx, es, C, x_fm, gate_w_s, ltmask, wg_dr, wu_dr, wd_dr,
             swg_dr, swu_dr, swd_dr, dbg_comb=None, r_fm=None):
    if r_fm is None:
        r_fm = sbt(ctx, es, "m_r", [128, NCH * TOK], BF16)
    with ExitStack() as ph:
        comb_t = sbt(ctx, ph, "m_combt", [8, 8 * 128], F32)
        comb_r = sbt(ctx, ph, "m_combr", [1, 8 * 8 * 128], BF16)
        inner = ExitStack()
        with ExitStack() as rp:
            p_r = pst(ctx, rp, "m_pr", [128, 512])
            p_tr = pst(ctx, rp, "m_ptr", [128, 512])
            sc = sbt(ctx, rp, "m_sc", [128, 8], F32)
            rk = sbt(ctx, rp, "m_rk", [128, 8], F32)
            t8 = sbt(ctx, rp, "m_t8", [128, 8], F32)
            mrow = sbt(ctx, rp, "m_mrow", [128, 2], F32)
            for tc in range(8):
                xch = [x_fm[:, k * TOK + tc * 128:k * TOK + tc * 128 + 128] for k in range(NCH)]
                ctx.wait("tensor", "vec")
                ctx.wait("tensor", "scl")
                mv = None
                for k in range(NCH):
                    mv = ctx.mm(p_r[:, 0:8], xch[k], gate_w_s[:, k * 8:k * 8 + 8],
                                k == 0, k == NCH - 1, inc="mm" if k == NCH - 1 else None)
                ctx.wait("vector", "mm", mv)
                ctx.op("vector", lambda v, o=mrow[:, 0:1], a=p_r[:, 0:8]:
                       v.reduce_max(o, a, axis=AX.X))
                ctx.ts("vector", mrow[:, 1:2], mrow[:, 0:1], -1.0, 1.0, ALU.mult, ALU.mult,
                       inc="vec")
                ctx.wait("scalar", "vec")
                sv = ctx.act(sc[:], p_r[:, 0:8], AF.Exp, bias=mrow[:, 1:2], inc="scl")
                ctx.wait("vector", "scl", sv)
                ctx.op("vector", lambda v, o=mrow[:, 0:1], a=sc[:]:
                       v.reduce_sum(o, a, axis=AX.X))
                ctx.op("vector", lambda v, o=mrow[:, 1:2], a=mrow[:, 0:1]: v.reciprocal(o, a))
                ctx.ts("vector", sc[:], sc[:], mrow[:, 1:2], 1.0, ALU.mult, ALU.mult)
                ctx.ts("vector", rk[:], sc[:], sc[:, 0:1], 0.0, ALU.is_lt, ALU.add)
                for j in range(1, 8):
                    ctx.ts("vector", t8[:], sc[:], sc[:, j:j + 1], 1.0, ALU.is_lt, ALU.mult)
                    ctx.tt("vector", rk[:], rk[:], t8[:], ALU.add)
                for j in range(7):
                    ctx.ts("vector", t8[:], sc[:], sc[:, j:j + 1], 1.0, ALU.is_equal, ALU.mult)
                    ctx.tt("vector", t8[:], t8[:], ltmask[:, j * 8:j * 8 + 8], ALU.mult)
                    ctx.tt("vector", rk[:], rk[:], t8[:], ALU.add)
                ctx.ts("vector", rk[:], rk[:], 2.0, 1.0, ALU.is_lt, ALU.mult)
                cv = ctx.tt("vector", sc[:], sc[:], rk[:], ALU.mult, inc="vec")
                ctx.wait("tensor", "vec", cv)
                tv = ctx.trans(p_tr[0:8, 0:128], sc[:], C["ident_f32"][:], inc="mm")
                ctx.wait("vector", "mm", tv)
                ctx.copy("vector", comb_t[:, tc * 128:tc * 128 + 128], p_tr[0:8, 0:128],
                         inc="vec")
            ctx.wait("gpsimd", "vec")
            for e_ in range(8):
                ctx.dma("gpsimd", comb_r[:, e_ * 1024:(e_ + 1) * 1024],
                        comb_t[e_:e_ + 1, :], inc="m_cr")
            ctx.barrier()
            ctx.wait("tensor", "m_cr")
            if dbg_comb is not None:
                ctx.dma("gpsimd", dbg_comb, comb_r[:], inc="m_dbg")
                ctx.wait("gpsimd", "m_dbg")
        wgt = sbt(ctx, inner, "m_wg", [128, 2 * 16 * 512], BF16)
        wut = sbt(ctx, inner, "m_wu", [128, 2 * 16 * 512], BF16)
        wdt = sbt(ctx, inner, "m_wd", [128, 2 * 11 * 512], BF16)
        gbuf = sbt(ctx, inner, "m_G", [128, 11 * TOK], BF16)
        gtmp = sbt(ctx, inner, "m_gt", [128, 2 * 512], BF16)
        p_g = [pst(ctx, inner, f"m_pg{i}", [128, 512]) for i in range(2)]
        p_u = [pst(ctx, inner, f"m_pu{i}", [128, 512]) for i in range(2)]
        p_cb = [pst(ctx, inner, f"m_pcb{i}", [128, 512]) for i in range(2)]
        p_d = [pst(ctx, inner, f"m_pd{i}", [128, 512]) for i in range(2)]

        def colslice(dst, slot, slotsz, src, r0, nr, c0, ncol, sem):
            o = dst[:, slot * slotsz:slot * slotsz + (nr // 128) * ncol]
            o = o.rearrange("p (c n) -> p c n", n=ncol)
            i = src[r0:r0 + nr, c0:c0 + ncol].rearrange("(c p) n -> p c n", p=128)
            return ctx.dma("gpsimd", o, i, inc=sem)

        for e in range(10):
            routed = e < 8
            Hh = 1408
            nhc = Hh // 128
            supers = [(0, 512), (512, 512), (1024, 384)]
            if routed:
                g_src = wg_dr[e * 2048:(e + 1) * 2048, :]
                u_src = wu_dr[e * 2048:(e + 1) * 2048, :]
                d_src = wd_dr[e * 1408:(e + 1) * 1408, :]
            else:
                sh = (e - 8) * 1408
                g_src = swg_dr[:, sh:sh + 1408]
                u_src = swu_dr[:, sh:sh + 1408]
                d_src = swd_dr[sh:sh + 1408, :]
            if routed:
                ctx.wait("tensor", "vec")
                for half in range(2):
                    ctx.mm(p_cb[half][:], C["ones_row_bf"][:],
                           comb_r[:, e * 1024 + half * 512:e * 1024 + half * 512 + 512],
                           True, True, inc="mm")
            mm_slot = {0: ctx.val("mm"), 1: ctx.val("mm")}
            gdv = {0: ctx.val("vec"), 1: ctx.val("vec")}
            sdv = {0: ctx.val("scl"), 1: ctx.val("scl")}
            for si, (h0, w) in enumerate(supers):
                slot = si % 2
                ctx.wait("gpsimd", "mm", mm_slot[slot])
                g_ld = colslice(wgt, slot, 16 * 512, g_src, 0, 2048, h0, w, f"m_swg{slot}")
                u_ld = colslice(wut, slot, 16 * 512, u_src, 0, 2048, h0, w, f"m_swu{slot}")
                ctx.wait("tensor", f"m_swg{slot}", g_ld)
                ctx.wait("tensor", f"m_swu{slot}", u_ld)
                for hc in range(w // 128):
                    gi = h0 // 128 + hc
                    for half in range(2):
                        sl_ = (hc * 2 + half) % 2
                        xsl = [x_fm[:, k * TOK + half * 512:k * TOK + half * 512 + 512]
                               for k in range(NCH)]
                        gsl = gbuf[:, gi * TOK + half * 512:gi * TOK + half * 512 + 512]
                        gts = gtmp[:, sl_ * 512:sl_ * 512 + 512]
                        ctx.wait("tensor", "vec", gdv[sl_])
                        ctx.wait("tensor", "scl", sdv[sl_])
                        mg = mu = None
                        for k in range(NCH):
                            off = slot * 16 * 512 + (k * (w // 128) + hc) * 128
                            mg = ctx.mm(p_g[sl_][:], wgt[:, off:off + 128], xsl[k],
                                        k == 0, k == NCH - 1,
                                        inc="mm" if k == NCH - 1 else None)
                        for k in range(NCH):
                            off = slot * 16 * 512 + (k * (w // 128) + hc) * 128
                            mu = ctx.mm(p_u[sl_][:], wut[:, off:off + 128], xsl[k],
                                        k == 0, k == NCH - 1,
                                        inc="mm" if k == NCH - 1 else None)
                        mm_slot[slot] = mu
                        ctx.wait("scalar", "mm", mg)
                        sv = ctx.act(gts, p_g[sl_][:], AF.Silu, inc="scl")
                        sdv[sl_] = sv
                        ctx.wait("vector", "mm", mu)
                        ctx.wait("vector", "scl", sv)
                        if routed:
                            ctx.tt("vector", gsl, gts, p_u[sl_][:], ALU.mult)
                            gdv[sl_] = ctx.tt("vector", gsl, gsl, p_cb[half][:], ALU.mult,
                                              inc="vec")
                        else:
                            gdv[sl_] = ctx.tt("vector", gsl, gts, p_u[sl_][:], ALU.mult,
                                              inc="vec")
            gdone = ctx.val("vec")
            dn_slot = {0: ctx.val("mm"), 1: ctx.val("mm")}
            pdv = {0: ctx.val("vec"), 1: ctx.val("vec")}
            dslotsz = 11 * 512
            for ds in range(4):
                slot = ds % 2
                ctx.wait("gpsimd", "mm", dn_slot[slot])
                d_ld = colslice(wdt, slot, dslotsz, d_src, 0, Hh, ds * 512, 512, f"m_swd{slot}")
                ctx.wait("tensor", f"m_swd{slot}", d_ld)
                ctx.wait("tensor", "vec", gdone)
                for dc in range(4):
                    dci = ds * 4 + dc
                    for half in range(2):
                        sl_ = (dc * 2 + half) % 2
                        pd = p_d[sl_]
                        ctx.wait("tensor", "vec", pdv[sl_])
                        mv = None
                        for hc in range(nhc):
                            off = slot * dslotsz + hc * 512 + dc * 128
                            mv = ctx.mm(pd[:], wdt[:, off:off + 128],
                                        gbuf[:, hc * TOK + half * 512:hc * TOK + half * 512 + 512],
                                        hc == 0, hc == nhc - 1,
                                        inc="mm" if hc == nhc - 1 else None)
                        dn_slot[slot] = mv
                        ctx.wait("vector", "mm", mv)
                        rsl = r_fm[:, dci * TOK + half * 512:dci * TOK + half * 512 + 512]
                        if e == 0:
                            pdv[sl_] = ctx.tt("vector", rsl, pd[:],
                                              x_fm[:, dci * TOK + half * 512:dci * TOK + half * 512 + 512],
                                              ALU.add, inc="vec")
                        else:
                            pdv[sl_] = ctx.tt("vector", rsl, rsl, pd[:], ALU.add, inc="vec")
        ctx.barrier()
        inner.close()
        emit_ln_fm(ctx, ph, C, r_fm, TOK, "m", ln_sink_simple(ctx, r_fm, TOK))
        ctx.barrier()
    return r_fm


DN_GROUPS = 8
DN_CHUNKS = 4
DN_HORNER = 3
MHA_STAGE = 4


def emit_mha(ctx, es, C, xsrc, wq_dr, wk_dr, wv_dr, cc3_dr):
    RT2 = 1.0 / math.sqrt(128.0)
    with ExitStack() as ph:
        wq_s = sbt(ctx, ph, "a_wq", [128, 16 * 512], BF16)
        wk_s = sbt(ctx, ph, "a_wk", [128, 16 * 512], BF16)
        wv_s = sbt(ctx, ph, "a_wv", [128, 16 * 512], BF16)
        q_fm = sbt(ctx, ph, "a_qfm", [128, 4 * 512], BF16)
        k_fm = sbt(ctx, ph, "a_kfm", [128, 4 * SEQ], BF16)
        v_tm = sbt(ctx, ph, "a_vtm", [128, 4 * SEQ], BF16)
        xg = sbt(ctx, ph, "a_xg", [128, 2 * 16 * 512], BF16)
        cosx = sbt(ctx, ph, "a_cos", [128, 32 * 128], BF16)
        sinx = sbt(ctx, ph, "a_sin", [128, 32 * 128], BF16)
        qrp = sbt(ctx, ph, "a_qrp", [128, 512], BF16)
        krp = sbt(ctx, ph, "a_krp", [128, 512], BF16)
        tmpab = sbt(ctx, ph, "a_tab", [128, 256], BF16)
        sqb = sbt(ctx, ph, "a_sqb", [128, 512], BF16)
        ptb = sbt(ctx, ph, "a_ptb", [128, 2 * 512], BF16)
        exps = sbt(ctx, ph, "a_exps", [128, 2 * 512], F32)
        obuf = sbt(ctx, ph, "a_obuf", [128, 512], BF16)
        mb_sb = sbt(ctx, ph, "a_mbsb", [128, 512], BF16)
        rows = sbt(ctx, ph, "a_rows", [1, 3 * 512], F32)
        qnr = sbt(ctx, ph, "a_qnr", [1, 4 * 512], F32)
        runmax = sbt(ctx, ph, "a_rmx", [1, 8], F32)
        lrow = sbt(ctx, ph, "a_lrow", [1, 512], BF16)
        p_q = pst(ctx, ph, "a_pq", [128, 512])
        p_k = pst(ctx, ph, "a_pk", [128, 512])
        p_v = pst(ctx, ph, "a_pv", [128, 512])
        p_st = pst(ctx, ph, "a_pst", [128, 512])
        p_mb = pst(ctx, ph, "a_pmb", [128, 512])
        p_o = pst(ctx, ph, "a_po", [128, 512])
        p_l = pst(ctx, ph, "a_pl", [1, 512])

        def R(i):
            return rows[:, i * 512:(i + 1) * 512]

        ctx.dma("gpsimd", wq_s[:].rearrange("p (c n) -> p c n", n=512),
                wq_dr.rearrange("(c p) n -> p c n", p=128), inc="a_w")
        ctx.dma("gpsimd", wk_s[:].rearrange("p (c n) -> p c n", n=512),
                wk_dr.rearrange("(c p) n -> p c n", p=128), inc="a_w")
        wv_v = ctx.dma("gpsimd", wv_s[:].rearrange("p (c n) -> p c n", n=512),
                       wv_dr.rearrange("(c p) n -> p c n", p=128), inc="a_w")
        with ExitStack() as tp:
            itr = sbt(ctx, tp, "a_itr", [1, 128], F32)
            thr = sbt(ctx, tp, "a_thr", [1, 64], F32)
            ang = sbt(ctx, tp, "a_ang", [128, 64], F32)
            red = sbt(ctx, tp, "a_red", [128, 64], F32)
            ki = sbt(ctx, tp, "a_ki", [128, 64], I32)

            gv = ctx.op("gpsimd", lambda g, o=thr[:]: g.iota(
                o, pattern=[[1, 64]], base=0, channel_multiplier=0,
                allow_small_or_imprecise_dtypes=True), inc="gp")
            ctx.wait("scalar", "gp", gv)
            sv = ctx.act(thr[:], thr[:], AF.Exp, scale=float(-math.log(10000.0) / 64.0),
                         inc="scl")
            ctx.wait("tensor", "gp", gv)
            ctx.wait("tensor", "scl", sv)
            ang_mv = 0
            for tc in range(32):
                if ang_mv:
                    ctx.wait("gpsimd", "mm", ang_mv)
                gv = ctx.op("gpsimd", lambda g, o=itr[:], b=tc * 128: g.iota(
                    o, pattern=[[1, 128]], base=b, channel_multiplier=0,
                    allow_small_or_imprecise_dtypes=True), inc="gp")
                ctx.wait("tensor", "gp", gv)
                mv = ctx.mm(p_st[:, 0:64], itr[:], thr[:],
                            True, True, inc="mm")
                ang_mv = mv
                ctx.wait("vector", "mm", mv)
                for half, shift in ((0, 0.0), (1, 0.5 * math.pi)):
                    ctx.ts("vector", ang[:], p_st[:, 0:64], 1.0, shift, ALU.mult, ALU.add)
                    ctx.ts("vector", red[:], ang[:], float(1.0 / (2 * math.pi)), 1.0,
                           ALU.mult, ALU.mult)
                    ctx.copy("vector", ki[:], red[:])
                    ctx.copy("vector", red[:], ki[:])
                    ctx.stt("vector", ang[:], red[:], float(-2 * math.pi), ang[:],
                            ALU.mult, ALU.add)
                    ctx.ts("vector", red[:], ang[:], float(math.pi), 1.0, ALU.is_gt, ALU.mult)
                    vv = ctx.stt("vector", ang[:], red[:], float(-2 * math.pi), ang[:],
                                 ALU.mult, ALU.add, inc="vec")
                    ctx.wait("scalar", "vec", vv)
                    dst = sinx if half == 0 else cosx
                    s0 = ctx.act(dst[:, tc * 128:tc * 128 + 127:2], ang[:], AF.Sin,
                                 inc="scl")
                    s1 = ctx.act(dst[:, tc * 128 + 1:tc * 128 + 128:2], ang[:], AF.Sin,
                                 inc="scl")
                    ctx.wait("vector", "scl", s1)
                ctx.wait("tensor", "vec")
        ctx.wait("tensor", "a_w", wv_v)
        ctx.wait("vector", "a_w", wv_v)

        ctx.op("vector", lambda v, o=runmax[:]: v.memset(o, 0.0), inc="vec")

        xg_mm = {0: 0, 1: 0}
        for g in range(8 if MHA_STAGE >= 2 else 0):
            slot = g % 2
            ctx.wait("gpsimd", "mm", xg_mm[slot])
            xv = ctx.dma("gpsimd",
                         xg[:, slot * 8192:slot * 8192 + 8192].rearrange(
                             "p (c n) -> p c n", n=512),
                         xsrc(g).rearrange("(c p) n -> p c n", p=128), inc=f"a_x{slot}")
            ctx.wait("tensor", f"a_x{slot}", xv)
            for tc4 in range(4):
                tc = g * 4 + tc4
                xsl = [xg[:, slot * 8192 + k * 512 + tc4 * 128:
                          slot * 8192 + k * 512 + tc4 * 128 + 128] for k in range(NCH)]
                ctx.wait("tensor", "vec")
                mq = mk = mvv = None
                for k in range(NCH):
                    mq = ctx.mm(p_q[:], xsl[k], wq_s[:, k * 512:k * 512 + 512],
                                k == 0, k == NCH - 1, inc="mm" if k == NCH - 1 else None)
                for k in range(NCH):
                    mk = ctx.mm(p_k[:], xsl[k], wk_s[:, k * 512:k * 512 + 512],
                                k == 0, k == NCH - 1, inc="mm" if k == NCH - 1 else None)
                for k in range(NCH):
                    mvv = ctx.mm(p_v[:], xsl[k], wv_s[:, k * 512:k * 512 + 512],
                                 k == 0, k == NCH - 1, inc="mm" if k == NCH - 1 else None)
                xg_mm[slot] = mvv
                ctx.wait("vector", "mm", mvv)
                ctx.copy("vector", v_tm[:, tc * 512:tc * 512 + 512], p_v[:], inc="vec")
                if MHA_STAGE < 2.3:
                    continue
                ctx.wait("vector", "mm")
                cose = cosx[:, tc * 128:tc * 128 + 128]
                sine = sinx[:, tc * 128:tc * 128 + 128]
                for src_ps, dstbuf in ((p_q, qrp), (p_k, krp)):
                    for h in range(4):
                        psl = src_ps[:, h * 128:h * 128 + 128]
                        dsl = dstbuf[:, h * 128:h * 128 + 128]
                        ctx.tt("vector", tmpab[:, 0:128], psl, cose, ALU.mult)
                        ctx.tt("vector", tmpab[:, 128:256], psl, sine, ALU.mult)
                        ctx.tt("vector", dstbuf[:, h * 128:h * 128 + 127:2],
                               tmpab[:, 0:127:2], tmpab[:, 129:256:2], ALU.subtract)
                        ctx.tt("vector", dstbuf[:, h * 128 + 1:h * 128 + 128:2],
                               tmpab[:, 1:128:2], tmpab[:, 128:255:2], ALU.add,
                               inc="vec")
                if MHA_STAGE < 2.6:
                    continue
                ctx.wait("tensor", "vec")
                ptr_v = [p_q[:].bitcast(BF16), p_k[:].bitcast(BF16)]
                evv = {}
                for i, (srcbuf, dstfm) in enumerate(((qrp, q_fm), (krp, k_fm))):
                    for h in range(4):
                        j = i * 4 + h
                        if j >= 2:
                            ctx.wait("tensor", "vec", evv[j - 2])
                        tv = ctx.trans(ptr_v[j % 2][:, 0:128],
                                       srcbuf[:, h * 128:h * 128 + 128],
                                       C["ident_bf"][:], inc="mm")
                        ctx.wait("vector", "mm", tv)
                        dcol = (h * 512 + tc4 * 128) if i == 0 else (h * SEQ + tc * 128)
                        evv[j] = ctx.copy(
                            "vector",
                            dstfm[:, dcol:dcol + 128],
                            ptr_v[j % 2][:, 0:128], inc="vec")
            if MHA_STAGE < 3:
                continue
            for h in range(4):
                qw = h * 512
                kw = h * SEQ + g * 512
                kv = ctx.tt("vector", sqb[:], k_fm[:, kw:kw + 512], k_fm[:, kw:kw + 512],
                            ALU.mult, inc="vec")
                ctx.wait("tensor", "vec", kv)
                mv = ctx.mm(p_l[:], C["ones_col_bf"][:], sqb[:], True, True, inc="mm")
                ctx.wait("vector", "mm", mv)
                ctx.op("vector", lambda v, o=R(0)[:, 0:1], a=p_l[:, 0:512]:
                       v.reduce_max(o, a, axis=AX.X))
                ctx.tt("vector", runmax[:, h:h + 1], runmax[:, h:h + 1], R(0)[:, 0:1],
                       ALU.max, inc="vec")
                qv = ctx.tt("vector", sqb[:], q_fm[:, qw:qw + 512], q_fm[:, qw:qw + 512],
                            ALU.mult, inc="vec")
                ctx.wait("tensor", "vec", qv)
                mv = ctx.mm(p_l[:], C["ones_col_bf"][:], sqb[:], True, True, inc="mm")
                ctx.wait("scalar", "mm", mv)
                sv = ctx.act(R(1), p_l[:, 0:512], AF.Ln, inc="scl")
                sv = ctx.act(qnr[:, h * 512:h * 512 + 512], R(1), AF.Exp, scale=0.5,
                             inc="scl")
                ctx.wait("vector", "scl", sv)
            if MHA_STAGE < 4:
                continue
            sv0 = ctx.val("vec")
            ctx.wait("scalar", "vec", sv0)
            sv = ctx.act(runmax[:, 4:8], runmax[:, 0:4], AF.Ln, inc="scl")
            sv = ctx.act(runmax[:, 4:8], runmax[:, 4:8], AF.Exp, scale=0.5, inc="scl")
            ctx.wait("vector", "scl", sv)
            for h in range(4):
                qw = h * 512
                mrv = ctx.ts("vector", R(2), qnr[:, h * 512:h * 512 + 512],
                             runmax[:, 4 + h:5 + h], RT2, ALU.mult, ALU.mult, inc="vec")
                ctx.wait("tensor", "vec", mrv)
                mbv = ctx.mm(p_mb[:], C["ones_row_f32"][:], R(2), True, True, inc="mm")
                ctx.wait("vector", "mm", mbv)
                ctx.copy("vector", mb_sb[:], p_mb[:], inc="vec")
                nk = 4 * (g + 1)
                lv = None
                ptv = {0: 0, 1: 0}
                svv = {0: 0, 1: 0}
                for kc in range(nk):
                    ps = kc % 2
                    stv = ctx.mm(p_st[:], k_fm[:, h * SEQ + kc * 128:h * SEQ + kc * 128 + 128],
                                 q_fm[:, qw:qw + 512], True, True, inc="mm")
                    ctx.wait("vector", "mm", stv)
                    if ptv[ps]:
                        ctx.wait("vector", "mm", ptv[ps])
                    if svv[ps]:
                        ctx.wait("vector", "scl", svv[ps])
                    ev = ctx.stt("vector", exps[:, ps * 512:ps * 512 + 512], p_st[:],
                                 RT2, mb_sb[:], ALU.mult, ALU.subtract, inc="vec")
                    ctx.wait("scalar", "vec", ev)
                    sv = ctx.act(ptb[:, ps * 512:ps * 512 + 512],
                                 exps[:, ps * 512:ps * 512 + 512], AF.Exp, inc="scl")
                    svv[ps] = sv
                    dj = kc - 4 * g
                    if dj >= 0:
                        ctx.wait("vector", "scl", sv)
                        sv = ctx.tt("vector", ptb[:, ps * 512:ps * 512 + 512],
                                    ptb[:, ps * 512:ps * 512 + 512],
                                    C["diagmask"][:, dj * 512:dj * 512 + 512],
                                    ALU.mult, inc="vec")
                        ctx.wait("tensor", "vec", sv)
                    else:
                        ctx.wait("tensor", "scl", sv)
                    pvv = ctx.mm(p_o[:], v_tm[:, (kc * 4 + h) * 128:(kc * 4 + h) * 128 + 128],
                                 ptb[:, ps * 512:ps * 512 + 512], kc == 0, kc == nk - 1,
                                 inc="mm")
                    lv = ctx.mm(p_l[:], C["ones_col_bf"][:], ptb[:, ps * 512:ps * 512 + 512],
                                kc == 0, kc == nk - 1, inc="mm")
                    ptv[ps] = lv
                ctx.wait("vector", "mm", lv)
                ov = ctx.copy("vector", obuf[:], p_o[:], inc="vec")
                lvv = ctx.copy("vector", lrow[:], p_l[:, 0:512], inc="vec")
                ctx.wait("sync", "vec", lvv)
                r0 = (g // 2) * 516
                c0 = (g % 2) * 512
                ctx.dma("sync", cc3_dr[r0 + h * 128:r0 + h * 128 + 128, c0:c0 + 512],
                        obuf[:], inc="a_st")
                sov = ctx.dma("sync", cc3_dr[r0 + 512 + h:r0 + 513 + h, c0:c0 + 512],
                              lrow[:], inc="a_st")
                ctx.wait("vector", "a_st", sov)
        ctx.barrier()


def emit_deltanet(ctx, es, C, xsrc, wq_dr, wk_dr, wv_dr, wb_dr, wg1_dr, wg2_dr,
                  gnf_dr, cc1_dr):
    with ExitStack() as ph:
        wq_s = sbt(ctx, ph, "d_wq", [128, 16 * 512], BF16)
        wk_s = sbt(ctx, ph, "d_wk", [128, 16 * 512], BF16)
        wv_s = sbt(ctx, ph, "d_wv", [128, 16 * 512], BF16)
        wb_s = sbt(ctx, ph, "d_wb", [128, 16 * 4], BF16)
        wg1_s = sbt(ctx, ph, "d_wg1", [128, 16 * 128], BF16)
        wg2_s = sbt(ctx, ph, "d_wg2", [128, 512], BF16)
        gnf = sbt(ctx, ph, "d_gnf", [128, 4], F32)
        xg = sbt(ctx, ph, "d_xg", [128, 2 * 16 * 512], BF16)
        qf = sbt(ctx, ph, "d_qf", [128, 2 * 2048], BF16)
        kf = sbt(ctx, ph, "d_kf", [128, 2 * 2048], BF16)
        kt = sbt(ctx, ph, "d_kt", [128, 2 * 2048], BF16)
        vt = sbt(ctx, ph, "d_vt", [128, 2 * 2048], BF16)
        gate = sbt(ctx, ph, "d_gate", [128, 2 * 2048], BF16)
        xg1 = sbt(ctx, ph, "d_xg1", [128, 2 * 512], BF16)
        bt = sbt(ctx, ph, "d_bt", [128, 2 * 16], F32)
        ksum = sbt(ctx, ph, "d_ksum", [128, 2 * 16], F32)
        qg_all = sbt(ctx, ph, "d_qga", [128, 2 * 2048], BF16)
        ktm_g = sbt(ctx, ph, "d_ktmg", [128, 2048], F32)
        ogb = sbt(ctx, ph, "d_ogb", [128, 2048], BF16)
        sgt = sbt(ctx, ph, "d_sgt", [128, 512], F32)
        rowb = sbt(ctx, ph, "d_rowb", [1, 2048], F32)
        nb = sbt(ctx, ph, "d_nb", [128, 128], BF16)
        ntb = sbt(ctx, ph, "d_ntb", [128, 128], BF16)
        mqb = sbt(ctx, ph, "d_mqb", [128, 128], BF16)
        spw = sbt(ctx, ph, "d_spw", [128, 128], BF16)
        spwt = sbt(ctx, ph, "d_spwt", [128, 128], BF16)
        ptb2 = sbt(ctx, ph, "d_pt", [128, 2 * 128], BF16)
        rb = sbt(ctx, ph, "d_rb", [128, 128], BF16)
        fb = sbt(ctx, ph, "d_fb", [128, 128], BF16)
        ub = sbt(ctx, ph, "d_ub", [128, 128], BF16)
        s32 = sbt(ctx, ph, "d_s32", [128, 2 * 4 * 128], F32)
        sbf = sbt(ctx, ph, "d_sbf", [128, 2 * 4 * 128], BF16)
        pA = pst(ctx, ph, "d_pA", [128, 512])
        pB = pst(ctx, ph, "d_pB", [128, 512])
        pC = pst(ctx, ph, "d_pC", [128, 512])
        pD = pst(ctx, ph, "d_pD", [128, 512])
        pE = pst(ctx, ph, "d_pE", [128, 512])
        pF = pst(ctx, ph, "d_pF", [128, 512])
        pG = pst(ctx, ph, "d_pG", [1, 512])
        pH = pst(ctx, ph, "d_pH", [128, 512])

        for dst, src, n in ((wq_s, wq_dr, 512), (wk_s, wk_dr, 512), (wv_s, wv_dr, 512),
                            (wb_s, wb_dr, 4), (wg1_s, wg1_dr, 128)):
            ctx.dma("gpsimd", dst[:].rearrange("p (c n) -> p c n", n=n),
                    src.rearrange("(c p) n -> p c n", p=128), inc="d_w")
        ctx.dma("gpsimd", wg2_s[:], wg2_dr, inc="d_w")
        wv_v = ctx.dma("gpsimd", gnf[:], gnf_dr, inc="d_w")
        for e in ("tensor", "vector", "scalar"):
            ctx.wait(e, "d_w", wv_v)

        xg_mm = {0: 0, 1: 0}
        for g in range(DN_GROUPS):
            slot = g % 2
            so = slot * 2048
            ctx.wait("gpsimd", "mm", xg_mm[slot])
            xv = ctx.dma("gpsimd",
                         xg[:, slot * 8192:slot * 8192 + 8192].rearrange(
                             "p (c n) -> p c n", n=512),
                         xsrc(g).rearrange("(c p) n -> p c n", p=128), inc=f"d_x{slot}")
            ctx.wait("tensor", f"d_x{slot}", xv)
            for tc4 in range(4):
                xsl = [xg[:, slot * 8192 + k * 512 + tc4 * 128:
                          slot * 8192 + k * 512 + tc4 * 128 + 128] for k in range(NCH)]
                ctx.wait("tensor", "vec")
                ctx.wait("tensor", "scl")
                mvq = mv1 = mvv_ = mvb = mv2 = None
                for k in range(NCH):
                    mvq = ctx.mm(pA[:], xsl[k], wq_s[:, k * 512:k * 512 + 512],
                                 k == 0, k == NCH - 1, inc="mm" if k == NCH - 1 else None)
                for k in range(NCH):
                    mv1 = ctx.mm(pB[:], xsl[k], wk_s[:, k * 512:k * 512 + 512],
                                 k == 0, k == NCH - 1, inc="mm" if k == NCH - 1 else None)
                for k in range(NCH):
                    mvv_ = ctx.mm(pC[:], xsl[k], wv_s[:, k * 512:k * 512 + 512],
                                  k == 0, k == NCH - 1, inc="mm" if k == NCH - 1 else None)
                for k in range(NCH):
                    mvb = ctx.mm(pD[:, 0:4], xsl[k], wb_s[:, k * 4:k * 4 + 4],
                                 k == 0, k == NCH - 1, inc="mm" if k == NCH - 1 else None)
                for k in range(NCH):
                    mv2 = ctx.mm(pE[:, 0:128], wg1_s[:, k * 128:k * 128 + 128], xsl[k],
                                 k == 0, k == NCH - 1, inc="mm" if k == NCH - 1 else None)
                ctx.wait("scalar", "mm", mvq)
                ctx.wait("scalar", "vec")
                sv = ctx.act(sgt[:], pA[:], AF.Sigmoid, inc="scl")
                ctx.wait("vector", "scl", sv)
                qtv = ctx.tt("vector", qg_all[:, so + tc4 * 512:so + tc4 * 512 + 512],
                             pA[:], sgt[:], ALU.mult, inc="vec")
                ctx.wait("scalar", "mm", mv1)
                ctx.wait("scalar", "vec", qtv)
                sv = ctx.act(sgt[:], pB[:], AF.Sigmoid, inc="scl")
                ctx.wait("vector", "scl", sv)
                ctx.tt("vector", ktm_g[:, tc4 * 512:tc4 * 512 + 512], pB[:], sgt[:],
                       ALU.mult, inc="vec")
                ctx.wait("scalar", "mm", mvb)
                bv = ctx.act(bt[:, slot * 16 + tc4 * 4:slot * 16 + tc4 * 4 + 4],
                             pD[:, 0:4], AF.Sigmoid, inc="scl")
                ctx.wait("vector", "scl", bv)
                ctx.ts("vector", bt[:, slot * 16 + tc4 * 4:slot * 16 + tc4 * 4 + 4],
                       bt[:, slot * 16 + tc4 * 4:slot * 16 + tc4 * 4 + 4], 2.0, 1.0,
                       ALU.mult, ALU.mult, inc="vec")
                ctx.wait("vector", "mm", mvv_)
                ctx.copy("vector", vt[:, so + tc4 * 512:so + tc4 * 512 + 512], pC[:],
                         inc="vec")
                ctx.wait("vector", "mm", mv2)
                xv1 = ctx.copy("vector", xg1[:, slot * 512 + tc4 * 128:
                                             slot * 512 + tc4 * 128 + 128],
                               pE[:, 0:128], inc="vec")
                ctx.wait("tensor", "vec", xv1)
                gmv = None
                for h in range(4):
                    gmv = ctx.mm(pF[:, h * 128:h * 128 + 128],
                                 wg2_s[:, h * 128:h * 128 + 128],
                                 xg1[:, slot * 512 + tc4 * 128:slot * 512 + tc4 * 128 + 128],
                                 True, True, inc="mm" if h == 3 else None)
                ctx.wait("scalar", "mm", gmv)
                ctx.act(gate[:, so + tc4 * 512:so + tc4 * 512 + 512], pF[:],
                        AF.Sigmoid, inc="scl")
                for h in range(4):
                    ctx.tt("vector", sgt[:, 0:128],
                           ktm_g[:, tc4 * 512 + h * 128:tc4 * 512 + h * 128 + 128],
                           ktm_g[:, tc4 * 512 + h * 128:tc4 * 512 + h * 128 + 128],
                           ALU.mult)
                    ctx.op("vector", lambda v,
                           o=ksum[:, slot * 16 + tc4 * 4 + h:slot * 16 + tc4 * 4 + h + 1],
                           a=sgt[:, 0:128]: v.reduce_sum(o, a, axis=AX.X), inc="vec")
            vv = ctx.ts("vector", ksum[:, slot * 16:slot * 16 + 16],
                        ksum[:, slot * 16:slot * 16 + 16], 1.0, EPS_DN, ALU.mult,
                        ALU.add, inc="vec")
            ctx.wait("scalar", "vec", vv)
            sv = ctx.act(ksum[:, slot * 16:slot * 16 + 16],
                         ksum[:, slot * 16:slot * 16 + 16], AF.Sqrt, inc="scl")
            ctx.wait("vector", "scl", sv)
            ctx.op("vector", lambda v, o=ksum[:, slot * 16:slot * 16 + 16],
                   a=ksum[:, slot * 16:slot * 16 + 16]: v.reciprocal(o, a), inc="vec")
            pv = [pA[:].bitcast(BF16), pB[:].bitcast(BF16)]
            evv = {}
            ctx.wait("tensor", "vec")
            for tc4 in range(4):
                for h in range(4):
                    i = tc4 * 4 + h
                    ctx.ts("vector", kt[:, so + i * 128:so + i * 128 + 128],
                           ktm_g[:, tc4 * 512 + h * 128:tc4 * 512 + h * 128 + 128],
                           ksum[:, slot * 16 + i:slot * 16 + i + 1], 1.0,
                           ALU.mult, ALU.mult, inc="vec")
            ktv = ctx.val("vec")
            ctx.wait("tensor", "vec", ktv)
            for j in range(32):
                srcb = qg_all if j < 16 else kt
                dstb = qf if j < 16 else kf
                i = j % 16
                if j >= 2:
                    ctx.wait("tensor", "vec", evv[j - 2])
                tv = ctx.trans(pv[j % 2][:, 0:128], srcb[:, so + i * 128:so + i * 128 + 128],
                               C["ident_bf"][:], inc="mm")
                ctx.wait("vector", "mm", tv)
                evv[j] = ctx.copy("vector", dstb[:, so + i * 128:so + i * 128 + 128],
                                  pv[j % 2][:, 0:128], inc="vec")
            ctx.wait("vector", "scl")
            for tc4 in range(DN_CHUNKS):
                for h in range(4):
                    i = tc4 * 4 + h
                    first = (g == 0 and tc4 == 0)
                    kfs = kf[:, so + i * 128:so + i * 128 + 128]
                    qfs = qf[:, so + i * 128:so + i * 128 + 128]
                    kts = kt[:, so + i * 128:so + i * 128 + 128]
                    vts = vt[:, so + i * 128:so + i * 128 + 128]
                    bcol = bt[:, slot * 16 + i:slot * 16 + i + 1]
                    par = (g * 4 + tc4) % 2
                    sold32 = s32[:, par * 512 + h * 128:par * 512 + h * 128 + 128]
                    snew32 = s32[:, (1 - par) * 512 + h * 128:(1 - par) * 512 + h * 128 + 128]
                    soldbf = sbf[:, par * 512 + h * 128:par * 512 + h * 128 + 128]
                    snewbf = sbf[:, (1 - par) * 512 + h * 128:(1 - par) * 512 + h * 128 + 128]
                    ctx.wait("tensor", "vec")
                    ctx.mm(pA[:, 0:128], kfs, kfs, True, True)
                    m1 = ctx.mm(pA[:, 128:256], kfs, qfs, True, True, inc="mm")
                    ctx.wait("vector", "mm", m1)
                    ctx.tt("vector", sgt[:, 0:128], pA[:, 0:128], C["mask_bd"][:], ALU.mult)
                    ctx.ts("vector", nb[:], sgt[:, 0:128], bcol, -1.0, ALU.mult, ALU.mult)
                    ctx.tt("vector", sgt[:, 0:128], pA[:, 0:128], C["mask_of"][:], ALU.mult)
                    ctx.ts("vector", fb[:], sgt[:, 0:128], bcol, 1.0, ALU.mult, ALU.mult)
                    nv = ctx.tt("vector", mqb[:], pA[:, 128:256], C["mask_iU"][:],
                                ALU.mult, inc="vec")
                    ctx.wait("tensor", "vec", nv)
                    tv = ctx.trans(pA[:].bitcast(BF16)[:, 0:128], nb[:], C["ident_bf"][:],
                                   inc="mm")
                    ctx.wait("vector", "mm", tv)
                    ctx.copy("vector", spwt[:], pA[:].bitcast(BF16)[:, 0:128])
                    ctx.tt("vector", ptb2[:, 0:128], spwt[:], C["ident_bf"][:], ALU.add)
                    iv = ctx.copy("vector", spw[:], nb[:], inc="vec")
                    ctx.wait("tensor", "vec", iv)
                    for lvl in range(1, 5):
                        sq2 = ctx.mm(pB[:, 0:128], spwt[:], spw[:], True, True, inc="mm")
                        if lvl < 4:
                            sq2 = ctx.mm(pB[:, 128:256], spw[:], spwt[:], True, True,
                                         inc="mm")
                        ctx.wait("vector", "mm", sq2)
                        ctx.copy("vector", spw[:], pB[:, 0:128])
                        uv = ctx.copy("vector", spwt[:], pB[:, 128:256], inc="vec") \
                            if lvl < 4 else \
                            ctx.copy("vector", spw[:, 0:1], pB[:, 0:1], inc="vec")
                        ctx.wait("tensor", "vec", uv)
                        pu = ctx.mm(pC[:, 0:128], spw[:], ptb2[:, 0:128], True, True,
                                    inc="mm")
                        ctx.wait("vector", "mm", pu)
                        uv = ctx.tt("vector", ptb2[:, 0:128], pC[:, 0:128],
                                    ptb2[:, 0:128], ALU.add, inc="vec")
                        ctx.wait("tensor", "vec", uv)
                    pts = ptb2[:, 0:128]
                    ev_ = ctx.mm(pC[:, 128:256], fb[:], pts, True, True, inc="mm")
                    ctx.wait("vector", "mm", ev_)
                    ctx.copy("vector", ntb[:], pC[:, 128:256], inc="vec")
                    if not first:
                        pp = ctx.mm(pD[:, 0:128], kfs, soldbf, True, True, inc="mm")
                        ctx.wait("vector", "mm", pp)
                        ctx.stt("vector", sgt[:, 0:128], pD[:, 0:128], -1.0, vts,
                                ALU.mult, ALU.add)
                        rv = ctx.ts("vector", rb[:], sgt[:, 0:128], bcol, 1.0,
                                    ALU.mult, ALU.mult, inc="vec")
                    else:
                        rv = ctx.ts("vector", rb[:], vts, bcol, 1.0, ALU.mult, ALU.mult,
                                    inc="vec")
                    ctx.wait("tensor", "vec", rv)
                    zv = ctx.mm(pD[:, 128:256], pts, rb[:], True, True, inc="mm")
                    ctx.wait("vector", "mm", zv)
                    zv = ctx.copy("vector", ptb2[:, 128:256], pD[:, 128:256], inc="vec")
                    ctx.wait("tensor", "vec", zv)
                    usrc = ptb2[:, 128:256]
                    for it in range(3 - DN_HORNER, 3):
                        udst = ub[:] if it == 2 else (rb[:] if it % 2 == 0 else ub[:])
                        hv = ctx.mm(pD[:, 256:384], ntb[:], usrc, True, True, inc="mm")
                        ctx.wait("vector", "mm", hv)
                        hv = ctx.stt("vector", udst, pD[:, 256:384], -1.0,
                                     ptb2[:, 128:256], ALU.mult, ALU.add, inc="vec")
                        ctx.wait("tensor", "vec", hv)
                        usrc = udst
                    sv_ = ctx.mm(pD[:, 384:512], kts, ub[:], True, True, inc="mm")
                    ctx.wait("vector", "mm", sv_)
                    if not first:
                        ctx.tt("vector", snew32[:], pD[:, 384:512], sold32[:], ALU.add)
                    else:
                        ctx.copy("vector", snew32[:], pD[:, 384:512])
                    s2v = ctx.copy("vector", snewbf[:], snew32[:], inc="vec")
                    if not first:
                        ctx.mm(pE[:, 0:128], soldbf, qfs, True, False)
                        ov = ctx.mm(pE[:, 0:128], ub[:], mqb[:], False, True, inc="mm")
                    else:
                        ov = ctx.mm(pE[:, 0:128], ub[:], mqb[:], True, True, inc="mm")
                    ctx.wait("vector", "mm", ov)
                    ctx.tt("vector", ogb[:, i * 128:i * 128 + 128], pE[:, 0:128],
                           gate[:, so + i * 128:so + i * 128 + 128], ALU.mult, inc="vec")
                    gv = ctx.tt("vector", sgt[:, 0:128].bitcast(BF16)[:, 0:128],
                                ogb[:, i * 128:i * 128 + 128], ogb[:, i * 128:i * 128 + 128],
                                ALU.mult, inc="vec")
                    ctx.wait("tensor", "vec", gv)
                    rmv = ctx.mm(pG[0:1, 0:128], C["ones_col_bf"][:],
                                 sgt[:, 0:128].bitcast(BF16)[:, 0:128],
                                 True, True, inc="mm")
                    ctx.wait("vector", "mm", rmv)
                    ctx.op("vector", lambda v, o=rowb[:, i * 128:i * 128 + 128],
                           a=pG[0:1, 0:128]: v.tensor_copy(o, a), inc="vec")
            if DN_CHUNKS < 4:
                continue
            vv = ctx.val("vec")
            ctx.wait("scalar", "vec", vv)
            sv = ctx.ts("vector", rowb[:], rowb[:], 1.0 / 128.0, EPS_DN, ALU.mult,
                        ALU.add, inc="vec")
            ctx.wait("scalar", "vec", sv)
            sv = ctx.act(rowb[:], rowb[:], AF.Sqrt, inc="scl")
            ctx.wait("vector", "scl", sv)
            rv = ctx.op("vector", lambda v, o=rowb[:], a=rowb[:]: v.reciprocal(o, a),
                        inc="vec")
            ctx.wait("tensor", "vec", rv)
            for i in range(16):
                h = i % 4
                bv = ctx.mm(pH[:, 0:128], C["ones_row_f32"][:],
                            rowb[:, i * 128:i * 128 + 128], True, True, inc="mm")
                ctx.wait("vector", "mm", bv)
                ctx.tt("vector", ogb[:, i * 128:i * 128 + 128],
                       ogb[:, i * 128:i * 128 + 128], pH[:, 0:128], ALU.mult)
                ov = ctx.ts("vector", ogb[:, i * 128:i * 128 + 128],
                            ogb[:, i * 128:i * 128 + 128], gnf[:, h:h + 1], 1.0,
                            ALU.mult, ALU.mult, inc="vec")
                ctx.wait("tensor", "vec", ov)
            ovv = ctx.val("vec")
            ctx.wait("sync", "vec", ovv)
            for h in range(4):
                r0 = (g // 2) * 512 + h * 128
                c0 = (g % 2) * 512
                sv_ = ctx.dma("sync",
                              cc1_dr[r0:r0 + 128, c0:c0 + 512]
                              .rearrange("q (a n) -> q a n", a=4),
                              ogb[:].rearrange("p (a c n) -> p a c n", a=4, c=4)[:, :, h, :],
                              inc="d_og")
            ctx.wait("vector", "d_og")
            xg_mm[slot] = ctx.val("mm")
        ctx.barrier()


def emit_wo_ln(ctx, es, out_es, C, og_dr, resid_ap, wo_dr, l_norm, sel, tag):
    BLK = 516 if l_norm else 512
    r_fm = sbt(ctx, out_es, f"w_r{tag}", [128, NCH * TOK], BF16)
    with ExitStack() as ph:
        og_s = sbt(ctx, ph, f"w_og{tag}", [128, NCH * TOK], BF16)
        og4 = sbt(ctx, ph, f"w_og4{tag}", [128, 4 * TOK], BF16)
        res_s = sbt(ctx, ph, f"w_res{tag}", [128, NCH * TOK], BF16)
        wot = sbt(ctx, ph, f"w_wo{tag}", [128, 2 * 16 * 512], BF16)
        p_w = [pst(ctx, ph, f"w_pw{tag}{i}", [128, 512]) for i in range(2)]
        p_rl = pst(ctx, ph, f"w_prl{tag}", [128, 512])
        ldk = {}
        for k in range(NCH):
            r_, lc = k // 4, k % 4
            for w in range(4):
                row = r_ * 4 * BLK + w * BLK + lc * 128
                ldk[(k, w)] = ctx.dma("gpsimd", og4[:, w * TOK:(w + 1) * TOK],
                                      og_dr[row:row + 128, :], inc=f"w_g{tag}{w}")
            for w in range(4):
                ctx.wait("vector", f"w_g{tag}{w}", ldk[(k, w)])
            osl = og_s[:, k * TOK:(k + 1) * TOK]
            ctx.ts("vector", osl, og4[:, 0:TOK], sel[:, 0:1], 1.0, ALU.mult, ALU.mult)
            for w in range(1, 4):
                ctx.stt("vector", osl, og4[:, w * TOK:(w + 1) * TOK], sel[:, w:w + 1],
                        osl, ALU.mult, ALU.add)
            gv = ctx.op("vector", lambda v, o=osl[:, 0:1], a=osl[:, 0:1]:
                        v.tensor_copy(o, a), inc="vec")
            ctx.wait("gpsimd", "vec", gv)
        for k in range(NCH):
            ov = ctx.dma("gpsimd", res_s[:, k * TOK:(k + 1) * TOK], resid_ap(k),
                         inc=f"w_ld{tag}")
        ogv = ctx.val("vec")
        if l_norm:
            lr16 = sbt(ctx, ph, f"w_lr16{tag}", [16, 4 * TOK], F32)
            lr0 = sbt(ctx, ph, f"w_lr0{tag}", [1, 16 * TOK], BF16)
            for w in range(4):
                for r_ in range(4):
                    row = r_ * 4 * BLK + w * BLK + 512
                    ov = ctx.dma("gpsimd", lr16[r_ * 4:r_ * 4 + 4, w * TOK:(w + 1) * TOK],
                                 og_dr[row:row + 4, :], inc=f"w_ll{tag}")
            ctx.wait("vector", f"w_ll{tag}", ov)
            ctx.ts("vector", lr16[:, 0:TOK], lr16[:, 0:TOK], sel[0:16, 0:1], 1.0,
                   ALU.mult, ALU.mult)
            for w in range(1, 4):
                ctx.stt("vector", lr16[:, 0:TOK], lr16[:, w * TOK:(w + 1) * TOK],
                        sel[0:16, w:w + 1], lr16[:, 0:TOK], ALU.mult, ALU.add)
            rv = ctx.op("vector", lambda v, o=lr16[:, 0:TOK], a=lr16[:, 0:TOK]:
                        v.reciprocal(o, a), inc="vec")
            ctx.wait("gpsimd", "vec", rv)
            for k in range(16):
                ov = ctx.dma("gpsimd", lr0[:, k * TOK:(k + 1) * TOK],
                             lr16[k:k + 1, 0:TOK], inc=f"w_lr{tag}")
            ctx.wait("tensor", f"w_lr{tag}", ov)
            ctx.wait("tensor", f"w_ld{tag}")
            ctx.wait("tensor", "vec", ogv)
            for k in range(NCH):
                for th in range(2):
                    bv = ctx.mm(p_rl[:], C["ones_row_bf"][:],
                                lr0[:, k * TOK + th * 512:k * TOK + th * 512 + 512],
                                True, True, inc="mm")
                    ctx.wait("vector", "mm", bv)
                    osl = og_s[:, k * TOK + th * 512:k * TOK + th * 512 + 512]
                    nv = ctx.tt("vector", osl, osl, p_rl[:], ALU.mult, inc="vec")
                    ctx.wait("tensor", "vec", nv)
        else:
            ctx.wait("tensor", f"w_ld{tag}")
            ctx.wait("tensor", "vec", ogv)
        wo_mm = {0: 0, 1: 0}
        for ds in range(4):
            slot = ds % 2
            ctx.wait("gpsimd", "mm", wo_mm[slot])
            o = wot[:, slot * 8192:slot * 8192 + 8192].rearrange("p (c n) -> p c n", n=512)
            i = wo_dr[:, ds * 512:ds * 512 + 512].rearrange("(c p) n -> p c n", p=128)
            wv = ctx.dma("gpsimd", o, i, inc=f"w_w{tag}{slot}")
            ctx.wait("tensor", f"w_w{tag}{slot}", wv)
            for dc in range(4):
                dci = ds * 4 + dc
                for th in range(2):
                    pw = p_w[th]
                    mv = None
                    for k in range(NCH):
                        off = slot * 8192 + (k * 4 + dc) * 128
                        mv = ctx.mm(pw[:], wot[:, off:off + 128],
                                    og_s[:, k * TOK + th * 512:k * TOK + th * 512 + 512],
                                    k == 0, k == NCH - 1,
                                    inc="mm" if k == NCH - 1 else None)
                    wo_mm[slot] = mv
                    ctx.wait("vector", "mm", mv)
                    rv = ctx.tt("vector", r_fm[:, dci * TOK + th * 512:dci * TOK + th * 512 + 512],
                                pw[:], res_s[:, dci * TOK + th * 512:dci * TOK + th * 512 + 512],
                                ALU.add, inc="vec")
                    ctx.wait("tensor", "vec", rv)
        ctx.barrier()
        emit_ln_fm(ctx, ph, C, r_fm, TOK, f"w{tag}", ln_sink_simple(ctx, r_fm, TOK))
        ctx.barrier()
    return r_fm



def build_full():
    nc = bass.Bass("TRN2", target_bir_lowering=False)
    dr = {}

    def ein(name, shape, dt):
        dr[name] = nc.dram_tensor(name, shape, dt, kind="ExternalInput").ap()
        return dr[name]

    for k, (shape, dt) in CONST_SPECS.items():
        ein(k, shape, dt)
    ein("x_fm", [D, SEQ], F32)
    ein("x_res", [D, TOK], F32)
    for nm in ("wq0", "wk0", "wv0"):
        ein(nm, [D, 512], F32)
    ein("wb0", [D, 4], F32)
    ein("wg1t", [D, 128], F32)
    ein("wg2sl", [128, 512], F32)
    ein("gnf", [128, 4], F32)
    ein("wo0sh", [256, D], F32)
    ein("gate_w", [D, 8], F32)
    ein("ltmask", [128, 64], F32)
    ein("wg_sh", [D, 1408], F32)
    ein("wu_sh", [D, 1408], F32)
    ein("wd_sh", [1408, D], F32)
    ein("swg_sh", [256, 2816], F32)
    ein("swu_sh", [256, 2816], F32)
    ein("swd_sh", [352, D], F32)
    for nm in ("wq1", "wk1", "wv1"):
        ein(nm, [D, 512], F32)
    ein("wo1sh", [256, D], F32)
    ein("wg8sh", [256, 8192], I8)
    ein("wu8sh", [256, 8192], I8)
    ein("wd8sh", [1024, D], I8)
    ein("sel", [128, 4], F32)
    ein("sg_t", [128, 64], F32)
    ein("su_t", [128, 64], F32)
    ein("sd_t", [128, 16], F32)
    out = nc.dram_tensor("out", [TOK, D], F32, kind="ExternalOutput").ap()

    ib = {}

    def mk_ag(name, in_shape, out_rows, dt):
        ib[name + "_b"] = nc.dram_tensor(name + "_b", list(in_shape), dt)
        ib[name + "_o"] = nc.dram_tensor(name + "_o", [out_rows, in_shape[1]], dt)

    mk_ag("wo0", (256, D), D, F32)
    mk_ag("wo1", (256, D), D, F32)
    mk_ag("wg", (D, 1408), 8 * D, F32)
    mk_ag("wu", (D, 1408), 8 * D, F32)
    mk_ag("wd", (1408, D), 8 * 1408, F32)
    mk_ag("swg", (256, 2816), D, F32)
    mk_ag("swu", (256, 2816), D, F32)
    mk_ag("swd", (352, D), 2816, F32)
    mk_ag("w8g", (256, 8192), D, I8)
    mk_ag("w8u", (256, 8192), D, I8)
    mk_ag("w8d", (1024, D), 8192, I8)
    cc1_in = nc.dram_tensor("cc1_in", [2048, TOK], BF16)
    cc1_out = nc.dram_tensor("cc1_out", [4 * 2048, TOK], BF16)
    cc2_in = nc.dram_tensor("cc2_in", [D, TOK], BF16)
    cc2_out = nc.dram_tensor("cc2_out", [4 * D, TOK], BF16)
    cc3_in = nc.dram_tensor("cc3_in", [4 * 516, TOK], BF16)
    cc3_out = nc.dram_tensor("cc3_out", [16 * 516, TOK], BF16)
    x2sav = nc.dram_tensor("x2sav", [D, TOK], BF16)

    G8 = [list(range(8))]
    G4 = [[0, 1, 2, 3], [4, 5, 6, 7]]

    es = ExitStack()
    ctx = Ctx(nc, es)
    C = emit_load_consts(ctx, es, dr)
    gws = sbt(ctx, es, "gws", [128, NCH * 8], BF16)
    ltm = sbt(ctx, es, "ltm", [128, 64], F32)
    sel = sbt(ctx, es, "sel_s", [128, 4], F32)
    ctx.dma("gpsimd", sel[:], dr["sel"])
    ctx.dma("gpsimd", gws[:].rearrange("p (c t) -> p c t", t=8),
            dr["gate_w"].rearrange("(c p) t -> p c t", p=128))
    ctx.dma("gpsimd", ltm[:], dr["ltmask"])

    ag_jobs = [("wo0", "wo0sh"), ("wg", "wg_sh"), ("wu", "wu_sh"), ("wd", "wd_sh"),
               ("swg", "swg_sh"), ("swu", "swu_sh"), ("swd", "swd_sh"),
               ("wo1", "wo1sh"), ("w8g", "wg8sh"), ("w8u", "wu8sh"), ("w8d", "wd8sh")]
    for idx, (name, srcn) in enumerate(ag_jobs):
        bv = ctx.dma("sync", ib[name + "_b"][:, :], dr[srcn], inc=f"agb{idx}")
        ctx.wait("gpsimd", f"agb{idx}", bv)
        ctx.op("gpsimd", lambda g, i=ib[name + "_b"], o=ib[name + "_o"]:
               g.collective_compute("AllGather", ALU.bypass, replica_groups=G8,
                                    ins=[i.ap().opt()], outs=[o.ap().opt()]),
               inc="cc", chain=False)
    cc_w = ctx.val("cc")

    emit_deltanet(ctx, es, C, lambda g: dr["x_fm"][:, g * 512:(g + 1) * 512],
                  dr["wq0"], dr["wk0"], dr["wv0"], dr["wb0"], dr["wg1t"],
                  dr["wg2sl"], dr["gnf"], cc1_in[:, :])
    ctx.wait("gpsimd", "d_og")
    ctx.op("gpsimd", lambda g: g.collective_compute(
        "AllGather", ALU.bypass, replica_groups=G4,
        ins=[cc1_in.ap().opt()], outs=[cc1_out.ap().opt()]), inc="cc", chain=False)
    ctx.wait("gpsimd", "cc")

    es_x2 = ExitStack()
    x2buf = sbt(ctx, es_x2, "x2buf", [128, NCH * TOK], BF16)
    es_x1 = ExitStack()
    x1 = emit_wo_ln(ctx, es, es_x1, C, cc1_out.ap(),
                    lambda k: dr["x_res"][k * 128:(k + 1) * 128, :],
                    ib["wo0_o"].ap(), False, sel, "0")

    x2 = emit_moe(ctx, es_x2, C, x1, gws, ltm, ib["wg_o"].ap(), ib["wu_o"].ap(),
                  ib["wd_o"].ap(), ib["swg_o"].ap(), ib["swu_o"].ap(), ib["swd_o"].ap(),
                  r_fm=x2buf)
    es_x1.close()
    x2v = ctx.dma("gpsimd", cc2_in[:, :].rearrange("(c p) t -> p c t", p=128),
                  x2[:].rearrange("p (c t) -> p c t", t=TOK), inc="x2s")
    sv2 = ctx.dma("gpsimd", x2sav[:, :].rearrange("(c p) t -> p c t", p=128),
                  x2[:].rearrange("p (c t) -> p c t", t=TOK), inc="x2p")
    ctx.wait("gpsimd", "x2s", x2v)
    ctx.op("gpsimd", lambda g: g.collective_compute(
        "AllGather", ALU.bypass, replica_groups=G4,
        ins=[cc2_in.ap().opt()], outs=[cc2_out.ap().opt()]), inc="cc", chain=False)
    ctx.wait("gpsimd", "cc")
    ctx.wait("gpsimd", "x2p", sv2)
    ctx.barrier()
    es_x2.close()

    emit_mha(ctx, es, C,
             lambda g: cc2_out.ap()[(g // 2) * D:(g // 2) * D + D,
                                    (g % 2) * 512:(g % 2) * 512 + 512],
             dr["wq1"], dr["wk1"], dr["wv1"], cc3_in[:, :])
    ctx.wait("gpsimd", "a_st")
    ctx.op("gpsimd", lambda g: g.collective_compute(
        "AllGather", ALU.bypass, replica_groups=G4,
        ins=[cc3_in.ap().opt()], outs=[cc3_out.ap().opt()]), inc="cc", chain=False)
    ctx.wait("gpsimd", "cc")

    sgt = sbt(ctx, es, "sgt_s", [128, 64], F32)
    sut = sbt(ctx, es, "sut_s", [128, 64], F32)
    sdt = sbt(ctx, es, "sdt_s", [128, 16], F32)
    ctx.dma("gpsimd", sgt[:], dr["sg_t"])
    ctx.dma("gpsimd", sut[:], dr["su_t"])
    ctx.dma("gpsimd", sdt[:], dr["sd_t"])
    es_x3 = ExitStack()
    x3 = emit_wo_ln(ctx, es, es_x3, C, cc3_out.ap(),
                    lambda k: x2sav[k * 128:(k + 1) * 128, :],
                    ib["wo1_o"].ap(), True, sel, "1")
    for e in ("tensor", "vector", "scalar"):
        ctx.wait(e, "ld")
    emit_l1ffn(ctx, es, C, x3, ib["w8g_o"].ap(), ib["w8u_o"].ap(), ib["w8d_o"].ap(),
               sgt, sut, sdt, out)
    es_x3.close()
    ctx.wait("sync", "st")
    ctx.finalize()
    es.close()
    return nc


_NC_CACHE = []


def kernel(**inputs):
    g = dict(inputs)
    consts = host_consts()
    ii = np.arange(8)
    ltm = np.zeros((128, 64), np.float32)
    for j in range(8):
        ltm[:, j * 8:(j + 1) * 8] = (ii[None, :] > j)

    x = np.asarray(g["L0_attn_x"], np.float32)
    A = lambda a: np.ascontiguousarray(a)
    in_maps = []
    for c in range(8):
        beta, lane = c // 4, c % 4
        cs = slice(512 * lane, 512 * lane + 512)
        hs = slice(4 * lane, 4 * lane + 4)
        xT = x[beta].T
        m = dict(consts)
        m["ltmask"] = ltm
        m["x_fm"] = A(xT)
        m["x_res"] = A(xT[:, lane * 1024:lane * 1024 + 1024])
        m["wq0"] = A(np.asarray(g["L0_attn_Wq"])[cs, :].T)
        m["wk0"] = A(np.asarray(g["L0_attn_Wk"])[cs, :].T)
        m["wv0"] = A(np.asarray(g["L0_attn_Wv"])[cs, :].T)
        m["wb0"] = A(np.asarray(g["L0_attn_Wbeta"])[hs, :].T)
        m["wg1t"] = A(np.asarray(g["L0_attn_Wg1"]).T)
        m["wg2sl"] = A(np.asarray(g["L0_attn_Wg2"])[cs, :].T)
        m["gnf"] = A(np.asarray(g["L0_attn_g_norm"])[cs].reshape(4, 128).T)
        m["wo0sh"] = A(np.asarray(g["L0_attn_Wo"]).T[256 * c:256 * c + 256, :])
        m["gate_w"] = A(np.asarray(g["L0_ffn_gate_w"]))
        m["wg_sh"] = A(np.asarray(g["L0_ffn_w_gate"])[c])
        m["wu_sh"] = A(np.asarray(g["L0_ffn_w_up"])[c])
        m["wd_sh"] = A(np.asarray(g["L0_ffn_w_down"])[c])
        m["swg_sh"] = A(np.asarray(g["L0_ffn_sw_gate"])[256 * c:256 * c + 256, :])
        m["swu_sh"] = A(np.asarray(g["L0_ffn_sw_up"])[256 * c:256 * c + 256, :])
        m["swd_sh"] = A(np.asarray(g["L0_ffn_sw_down"])[352 * c:352 * c + 352, :])
        m["wq1"] = A(np.asarray(g["L1_attn_Wq"])[cs, :].T)
        m["wk1"] = A(np.asarray(g["L1_attn_Wk"])[cs, :].T)
        m["wv1"] = A(np.asarray(g["L1_attn_Wv"])[cs, :].T)
        m["wo1sh"] = A(np.asarray(g["L1_attn_Wo"]).T[256 * c:256 * c + 256, :])
        m["wg8sh"] = A(np.asarray(g["L1_ffn_w_gate"])[256 * c:256 * c + 256, :])
        m["wu8sh"] = A(np.asarray(g["L1_ffn_w_up"])[256 * c:256 * c + 256, :])
        m["wd8sh"] = A(np.asarray(g["L1_ffn_w_down"])[1024 * c:1024 * c + 1024, :])
        selv = np.zeros((128, 4), np.float32)
        selv[:, lane] = 1.0
        m["sel"] = selv
        m["sg_t"] = A(np.asarray(g["L1_ffn_s_gate"], np.float32).reshape(64, 128).T)
        m["su_t"] = A(np.asarray(g["L1_ffn_s_up"], np.float32).reshape(64, 128).T)
        m["sd_t"] = A(np.asarray(g["L1_ffn_s_down"], np.float32).reshape(16, 128).T)
        in_maps.append(m)

    if not _NC_CACHE:
        _NC_CACHE.append(build_full())
    nc = _NC_CACHE[0]
    res = run_bass_kernel_spmd(nc, in_maps, core_ids=list(range(8)))
    out = np.concatenate([np.asarray(res.results[c]["out"]) for c in range(8)], axis=0)
    return out.reshape(2, SEQ, D)


# version 30
# speedup vs baseline: 1.5024x; 1.5024x over previous
import math
from contextlib import ExitStack

import numpy as np
import concourse.bass as bass
import concourse.mybir as mybir
from concourse.bass_utils import run_bass_kernel_spmd

F32 = mybir.dt.float32
BF16 = mybir.dt.bfloat16
I8 = mybir.dt.int8
I32 = mybir.dt.int32
AF = mybir.ActivationFunctionType
ALU = mybir.AluOpType
AX = mybir.AxisListType

D = 2048
NCH = D // 128
SEQ = 4096
TOK = 1024
HD = 128
H_PER = 4
EPS_DN = 1e-6
EPS_LN = 1e-5
INV_D = 1.0 / D


class Ctx:

    ENGINES = ("tensor", "vector", "scalar", "gpsimd", "sync")

    def __init__(self, nc, es):
        self.nc = nc
        self.es = es
        self.streams = {e: [] for e in self.ENGINES}
        self._h = {}
        self._n = {}
        for nm in ("ld", "mm", "vec", "scl", "gp", "sy", "st", "cc"):
            self._h[nm] = es.enter_context(nc.semaphore(nm))
            self._n[nm] = 0

    def dsem(self, nm):
        if nm not in self._h:
            self._h[nm] = self.es.enter_context(self.nc.semaphore(nm))
            self._n[nm] = 0
        return nm

    def val(self, nm):
        return self._n[nm]

    CHAIN = {"vector": "vec", "scalar": "scl", "gpsimd": "gp"}

    def op(self, eng, fn, inc=None, inck=1, chain=True):
        own = self.CHAIN.get(eng) if chain else None
        if own is not None and (inc is None or inc == own):
            h = self._h[own]
            prev = self._n[own]
            self.streams[eng].append(
                lambda e, fn=fn, h=h, p=prev: (e.wait_ge(h, p) if p else None,
                                               fn(e).then_inc(h, 1))[1])
            self._n[own] += 1
            return self._n[own]
        if inc is None:
            self.streams[eng].append(fn)
            return None
        h = self._h[inc]
        if own is not None:
            ho = self._h[own]
            prev = self._n[own]
            self.streams[eng].append(
                lambda e, fn=fn, h=h, k=inck, ho=ho, p=prev: (
                    e.wait_ge(ho, p) if p else None, fn(e).then_inc(h, k))[1])
        else:
            self.streams[eng].append(lambda e, fn=fn, h=h, k=inck: fn(e).then_inc(h, k))
        self._n[inc] += inck
        return self._n[inc]

    def wait(self, eng, sem, val=None):
        v = self._n[sem] if val is None else val
        if v <= 0:
            return
        h = self._h[sem]
        self.streams[eng].append(lambda e, h=h, v=v: e.wait_ge(h, v))

    def barrier(self, exclude=()):
        vals = {nm: v for nm, v in self._n.items() if v > 0 and nm not in exclude}
        for eng in self.ENGINES:
            for sem, v in vals.items():
                self.wait(eng, sem, v)

    def mm(self, o, l, r, start, stop, inc=None):
        return self.op("tensor",
                       lambda t, o=o, l=l, r=r, a=start, b=stop: t.matmul(o, l, r, start=a, stop=b),
                       inc=inc)

    def trans(self, o, i, ident, inc=None):
        return self.op("tensor", lambda t, o=o, i=i, d=ident: t.transpose(o, i, d), inc=inc)

    def tt(self, eng, o, a, b, alu, inc=None):
        return self.op(eng, lambda v, o=o, a=a, b=b, u=alu: v.tensor_tensor(o, a, b, u), inc=inc)

    def ts(self, eng, o, a, s1, s2, op0, op1, inc=None):
        return self.op(eng, lambda v, o=o, a=a, s1=s1, s2=s2, p0=op0, p1=op1:
                       v.tensor_scalar(o, a, s1, s2, p0, p1), inc=inc)

    def stt(self, eng, o, a, s, b, op0, op1, inc=None):
        return self.op(eng, lambda v, o=o, a=a, s=s, b=b, p0=op0, p1=op1:
                       v.scalar_tensor_tensor(o, a, s, b, p0, p1), inc=inc)

    def act(self, o, a, fname, bias=None, scale=None, accum=None, inc=None):
        def f(s, o=o, a=a, fn=fname, bi=bias, sc=scale, ac=accum):
            kw = {}
            if bi is not None:
                kw["bias"] = bi
            if sc is not None:
                kw["scale"] = sc
            if ac is not None:
                kw["accum_out"] = ac
            return s.activation(o, a, fn, **kw)
        return self.op("scalar", f, inc=inc)

    def copy(self, eng, o, a, inc=None):
        if eng == "scalar":
            return self.op(eng, lambda s, o=o, a=a: s.copy(o, a), inc=inc)
        return self.op(eng, lambda v, o=o, a=a: v.tensor_copy(o, a), inc=inc)

    def dma(self, eng, o, a, inc="ld", transpose=False):
        if inc not in self._h:
            self.dsem(inc)
        if transpose:
            return self.op(eng, lambda e, o=o, a=a: e.dma_start(out=o, in_=a, transpose=True),
                           inc=inc, inck=16, chain=False)
        return self.op(eng, lambda e, o=o, a=a: e.dma_start(out=o, in_=a), inc=inc, inck=16,
                       chain=False)

    def finalize(self):
        blk = self.es.enter_context(self.nc.Block())
        streams = self.streams

        @blk.tensor
        def _(t):
            for f in streams["tensor"]:
                f(t)

        @blk.vector
        def _(v):
            for f in streams["vector"]:
                f(v)

        @blk.scalar
        def _(s):
            for f in streams["scalar"]:
                f(s)

        @blk.gpsimd
        def _(g):
            for f in streams["gpsimd"]:
                f(g)

        @blk.sync
        def _(sy):
            for f in streams["sync"]:
                f(sy)


def sbt(ctx, es, name, shape, dtype):
    return es.enter_context(ctx.nc.sbuf_tensor(name, shape, dtype))


def pst(ctx, es, name, shape, dtype=F32):
    elems = 512 if dtype == F32 else 1024
    shape = [shape[0], max(shape[1], elems)]
    return es.enter_context(ctx.nc.psum_tensor(name, shape, dtype))


CONST_SPECS = {
    "ident_bf": ([128, 128], BF16),
    "ident_f32": ([128, 128], F32),
    "ones_row_bf": ([1, 128], BF16),
    "ones_row_f32": ([1, 128], F32),
    "ones_col_bf": ([128, 1], BF16),
    "mask_bd4n": ([128, 512], BF16),
    "mask_of4": ([128, 512], BF16),
    "mask_iU4": ([128, 512], BF16),
    "ones_t": ([128, 128], BF16),
    "ident4": ([128, 512], BF16),
    "diagmask": ([128, 4 * 512], BF16),
}


def host_consts():
    import ml_dtypes
    ii = np.arange(128)
    c = {}
    c["ident_bf"] = np.eye(128, dtype=np.float32)
    c["ident_f32"] = np.eye(128, dtype=np.float32)
    c["ones_row_bf"] = np.ones((1, 128), np.float32)
    c["ones_row_f32"] = np.ones((1, 128), np.float32)
    c["ones_col_bf"] = np.ones((128, 1), np.float32)
    c["mask_sL"] = (ii[:, None] > ii[None, :]).astype(np.float32)
    c["mask_iU"] = (ii[:, None] <= ii[None, :]).astype(np.float32)
    c["mask_bd"] = (((ii[:, None] // 32) == (ii[None, :] // 32))
                    & (ii[:, None] > ii[None, :])).astype(np.float32)
    c["mask_of"] = ((ii[:, None] // 32) > (ii[None, :] // 32)).astype(np.float32)
    c["mask_bd4n"] = np.tile(-c["mask_bd"], (1, 4))
    c["mask_of4"] = np.tile(c["mask_of"], (1, 4))
    c["mask_iU4"] = np.tile(c["mask_iU"], (1, 4))
    c["ones_t"] = np.ones((128, 128), np.float32)
    c["ident4"] = np.tile(np.eye(128, dtype=np.float32), (1, 4))
    dm = np.zeros((128, 4 * 512), np.float32)
    for j in range(4):
        dm[:, j * 512:(j + 1) * 512] = ((j * 128 + ii[:, None]) <= np.arange(512)[None, :])
    c["diagmask"] = dm
    out = {}
    for k, (shape, dt) in CONST_SPECS.items():
        a = np.ascontiguousarray(c[k].reshape(shape))
        out[k] = a.astype(ml_dtypes.bfloat16) if dt == BF16 else a.astype(np.float32)
    return out


def emit_load_consts(ctx, es, dr):
    C = {}
    for k, (shape, dt) in CONST_SPECS.items():
        C[k] = sbt(ctx, es, "c_" + k, shape, dt)
        ctx.dma("gpsimd", C[k][:], dr[k], inc="ldc")
    return C


def emit_ln_fm(ctx, es, C, r_fm, T, tag, sink):
    with ExitStack() as ph:
        sq = sbt(ctx, ph, f"lnsq{tag}", [128, 2 * 512], BF16)
        rows = sbt(ctx, ph, f"lnrw{tag}", [1, 6 * 512], F32)
        p_sum = pst(ctx, ph, f"lnps{tag}", [1, 512])
        p_sq = pst(ctx, ph, f"lnpq{tag}", [1, 512])
        p_a = pst(ctx, ph, f"lnpa{tag}", [128, 512])
        p_b = pst(ctx, ph, f"lnpb{tag}", [128, 512])

        def R(i):
            return rows[:, i * 512:(i + 1) * 512]

        applied = 0
        for g in range(T // 512):
            rsl = [r_fm[:, k * T + g * 512:k * T + g * 512 + 512] for k in range(NCH)]
            ctx.wait("tensor", "vec", applied)
            for k in range(NCH):
                ctx.mm(p_sum[:], C["ones_col_bf"][:], rsl[k], k == 0, k == NCH - 1, inc="mm")
            mmv = {}
            for k in range(NCH):
                if k >= 2:
                    ctx.wait("scalar", "mm", mmv[k - 2])
                sqs = sq[:, (k % 2) * 512:(k % 2) * 512 + 512]
                sv = ctx.act(sqs, rsl[k], AF.Square, inc="scl")
                ctx.wait("tensor", "scl", sv)
                mmv[k] = ctx.mm(p_sq[:], C["ones_col_bf"][:], sqs, k == 0, k == NCH - 1, inc="mm")
            ctx.wait("vector", "mm")
            ctx.ts("vector", R(0), p_sum[:], INV_D, 1.0, ALU.mult, ALU.mult)
            ctx.ts("vector", R(1), p_sq[:], INV_D, 1.0, ALU.mult, ALU.mult)
            ctx.tt("vector", R(4), R(0), R(0), ALU.mult)
            ctx.tt("vector", R(1), R(1), R(4), ALU.subtract)
            vv = ctx.ts("vector", R(1), R(1), 1.0, EPS_LN, ALU.mult, ALU.add, inc="vec")
            ctx.wait("scalar", "vec", vv)
            sv = ctx.act(R(4), R(1), AF.Sqrt, inc="scl")
            ctx.wait("vector", "scl", sv)
            ctx.op("vector", lambda v, o=R(2), a=R(4): v.reciprocal(o, a))
            ctx.tt("vector", R(3), R(0), R(2), ALU.mult)
            vv = ctx.ts("vector", R(3), R(3), -1.0, 1.0, ALU.mult, ALU.mult, inc="vec")
            ctx.wait("tensor", "vec", vv)
            ctx.mm(p_a[:], C["ones_row_f32"][:], R(2), True, True)
            bcv = ctx.mm(p_b[:], C["ones_row_f32"][:], R(3), True, True, inc="mm")
            ctx.wait("vector", "mm", bcv)
            for k in range(NCH):
                sink(g, k, rsl[k], p_a[:], p_b[:])
            applied = ctx.val("vec")


def ln_sink_simple(ctx, out_fm, T):
    def sink(g, k, rsl, a, b):
        osl = out_fm[:, k * T + g * 512:k * T + g * 512 + 512]
        ctx.tt("vector", osl, rsl, a, ALU.mult)
        ctx.tt("vector", osl, osl, b, ALU.add, inc="vec")
    return sink


def emit_l1ffn(ctx, es, C, x_fm, wg8, wu8, wd8, sg_t, su_t, sd_t, out_dram):
    with ExitStack() as ph:
        r_fm = sbt(ctx, ph, "f_r", [128, NCH * TOK], BF16)
        inner = ExitStack()
        fbuf = sbt(ctx, inner, "f_F", [128, 32 * 512], BF16)
        wgt = sbt(ctx, inner, "f_wg", [128, 2 * 16 * 512], BF16)
        wut = sbt(ctx, inner, "f_wu", [128, 2 * 16 * 512], BF16)
        wdt = sbt(ctx, inner, "f_wd", [128, 2 * 32 * 256], BF16)
        gtmp = sbt(ctx, inner, "f_gt", [128, 2 * 512], BF16)
        p_g = [pst(ctx, inner, f"f_pg{i}", [128, 512]) for i in range(2)]
        p_u = [pst(ctx, inner, f"f_pu{i}", [128, 512]) for i in range(2)]
        p_d = [pst(ctx, inner, f"f_pd{i}", [128, 512]) for i in range(2)]

        def colslice_dma(dst, src, c0, ncol, row0, nrow, sem, slot=0):
            sz = (nrow // 128) * ncol
            o = dst[:, slot * sz:(slot + 1) * sz].rearrange("p (c n) -> p c n", n=ncol)
            i = src[row0:row0 + nrow, c0:c0 + ncol].rearrange("(c p) n -> p c n", p=128)
            return ctx.dma("gpsimd", o, i, inc=sem)

        for th in range(2):
            tok0 = th * 512
            xsl = [x_fm[:, k * TOK + tok0:k * TOK + tok0 + 512] for k in range(NCH)]
            for hh in range(2):
                ctx.wait("vector", "mm")
                prev_mm = ctx.val("mm")
                fdv = {0: 0, 1: 0}
                sdv = {0: 0, 1: 0}
                wmm = {0: 0, 1: 0}
                wld = {}
                for hs in range(8):
                    ws = hs % 2
                    if hs == 0:
                        wld[0] = (colslice_dma(wgt, wg8, hh * 4096, 512, 0, D, "s_wg0", 0),
                                  colslice_dma(wut, wu8, hh * 4096, 512, 0, D, "s_wu0", 0))
                    if hs + 1 < 8:
                        wn = (hs + 1) % 2
                        ctx.wait("gpsimd", "mm", wmm[wn])
                        wld[hs + 1] = (
                            colslice_dma(wgt, wg8, hh * 4096 + (hs + 1) * 512, 512, 0, D,
                                         f"s_wg{wn}", wn),
                            colslice_dma(wut, wu8, hh * 4096 + (hs + 1) * 512, 512, 0, D,
                                         f"s_wu{wn}", wn))
                    ctx.wait("tensor", f"s_wg{ws}", wld[hs][0])
                    ctx.wait("tensor", f"s_wu{ws}", wld[hs][1])
                    for hc in range(4):
                        sl_ = hc % 2
                        hci = hh * 32 + hs * 4 + hc
                        fi = hs * 4 + hc
                        fsl = fbuf[:, fi * 512:fi * 512 + 512]
                        gsl = gtmp[:, sl_ * 512:sl_ * 512 + 512]
                        ctx.wait("tensor", "vec", fdv[sl_])
                        ctx.wait("tensor", "scl", sdv[sl_])
                        wbase = ws * 8192
                        mg = mu = None
                        for k in range(NCH):
                            mg = ctx.mm(p_g[sl_][:],
                                        wgt[:, wbase + (k * 4 + hc) * 128:wbase + (k * 4 + hc) * 128 + 128],
                                        xsl[k], k == 0, k == NCH - 1,
                                        inc="mm" if k == NCH - 1 else None)
                        for k in range(NCH):
                            mu = ctx.mm(p_u[sl_][:],
                                        wut[:, wbase + (k * 4 + hc) * 128:wbase + (k * 4 + hc) * 128 + 128],
                                        xsl[k], k == 0, k == NCH - 1,
                                        inc="mm" if k == NCH - 1 else None)
                        prev_mm = mu
                        wmm[ws] = mu
                        ctx.wait("vector", "mm", mg)
                        gv = ctx.ts("vector", gsl, p_g[sl_][:], sg_t[:, hci:hci + 1], 1.0,
                                    ALU.mult, ALU.mult, inc="vec")
                        ctx.wait("scalar", "vec", gv)
                        sv = ctx.act(gsl, gsl, AF.Silu, inc="scl")
                        sdv[sl_] = sv
                        ctx.wait("vector", "mm", mu)
                        ctx.ts("vector", fsl, p_u[sl_][:], su_t[:, hci:hci + 1], 1.0,
                               ALU.mult, ALU.mult)
                        ctx.wait("vector", "scl", sv)
                        fdv[sl_] = ctx.tt("vector", fsl, fsl, gsl, ALU.mult, inc="vec")
                ctx.wait("vector", "scl")
                fdone = ctx.val("vec")
                prev_dmm = ctx.val("mm")
                ddv = {0: 0, 1: 0}
                dmm = {0: 0, 1: 0}
                dld = {}
                for ds in range(8):
                    if ds == 0:
                        dld[0] = colslice_dma(wdt, wd8, 0, 256, hh * 4096, 4096, "s_wd0", 0)
                    if ds + 1 < 8:
                        dn_ = (ds + 1) % 2
                        ctx.wait("gpsimd", "mm", dmm[dn_])
                        dld[ds + 1] = colslice_dma(wdt, wd8, (ds + 1) * 256, 256,
                                                   hh * 4096, 4096, f"s_wd{dn_}", dn_)
                    ctx.wait("tensor", f"s_wd{ds % 2}", dld[ds])
                    ctx.wait("tensor", "vec", fdone)
                    for dc in range(2):
                        sl_ = dc % 2
                        dci = ds * 2 + dc
                        pd = p_d[sl_]
                        gsl = gtmp[:, sl_ * 512:sl_ * 512 + 512]
                        ctx.wait("tensor", "vec", ddv[sl_])
                        dbase = (ds % 2) * 8192
                        mv = None
                        for hc in range(32):
                            mv = ctx.mm(pd[:],
                                        wdt[:, dbase + (hc * 2 + dc) * 128:dbase + (hc * 2 + dc) * 128 + 128],
                                        fbuf[:, hc * 512:hc * 512 + 512], hc == 0, hc == 31,
                                        inc="mm" if hc == 31 else None)
                        prev_dmm = mv
                        dmm[ds % 2] = mv
                        ctx.wait("vector", "mm", mv)
                        rsl = r_fm[:, dci * TOK + tok0:dci * TOK + tok0 + 512]
                        ctx.ts("vector", gsl, pd[:], sd_t[:, dci:dci + 1], 1.0,
                               ALU.mult, ALU.mult)
                        if hh == 0:
                            ddv[sl_] = ctx.tt("vector", rsl, gsl, xsl[dci], ALU.add, inc="vec")
                        else:
                            ddv[sl_] = ctx.tt("vector", rsl, rsl, gsl, ALU.add, inc="vec")
        ctx.barrier()
        inner.close()
        y32 = sbt(ctx, ph, "f_y32", [128, NCH * 512], F32)
        ytm = sbt(ctx, ph, "f_ytm", [128, 2048], F32)
        p_t = [pst(ctx, ph, f"f_pt{i}", [128, 512]) for i in range(2)]

        def sink(g, k, rsl, a, b):
            osl = y32[:, k * 512:k * 512 + 512]
            ctx.tt("vector", osl, rsl, a, ALU.mult)
            ctx.tt("vector", osl, osl, b, ALU.add, inc="vec")
            if k == NCH - 1:
                _store_group(ctx, C, y32, ytm, p_t, out_dram, g)

        emit_ln_fm(ctx, ph, C, r_fm, TOK, "f", sink)
        ctx.wait("sync", "st")


def _store_group(ctx, C, y32, ytm, p_t, out_dram, g):
    yv = ctx.val("vec")
    ctx.wait("tensor", "vec", yv)
    evv = {}
    for tc in range(4):
        stv = ctx.val("st")
        for k in range(NCH):
            i = tc * NCH + k
            pt = p_t[i % 2]
            if i >= 2:
                ctx.wait("tensor", "vec", evv[i - 2])
            mv = ctx.trans(pt[:, 0:128], y32[:, k * 512 + tc * 128:k * 512 + tc * 128 + 128],
                           C["ident_f32"][:], inc="mm")
            ctx.wait("vector", "mm", mv)
            if k == 0 and stv > 0:
                ctx.wait("vector", "st", stv)
            evv[i] = ctx.copy("vector", ytm[:, k * 128:k * 128 + 128], pt[:, 0:128], inc="vec")
        ev = ctx.val("vec")
        ctx.wait("sync", "vec", ev)
        row0 = g * 512 + tc * 128
        ctx.dma("sync", out_dram[row0:row0 + 128, :], ytm[:], inc="st")


def kernel(**inputs):
    raise NotImplementedError("full build assembled incrementally; see scratch drivers")


def emit_moe(ctx, es, C, x_fm, gate_w_s, ltmask, wg_dr, wu_dr, wd_dr,
             swg_dr, swu_dr, swd_dr, dbg_comb=None, r_fm=None):
    if r_fm is None:
        r_fm = sbt(ctx, es, "m_r", [128, NCH * TOK], BF16)
    with ExitStack() as ph:
        comb_t = sbt(ctx, ph, "m_combt", [8, 8 * 128], F32)
        comb_r = sbt(ctx, ph, "m_combr", [1, 8 * 8 * 128], BF16)
        inner = ExitStack()
        with ExitStack() as rp:
            p_r = pst(ctx, rp, "m_pr", [128, 512])
            p_tr = pst(ctx, rp, "m_ptr", [128, 512])
            sc = sbt(ctx, rp, "m_sc", [128, 8], F32)
            rk = sbt(ctx, rp, "m_rk", [128, 8], F32)
            t8 = sbt(ctx, rp, "m_t8", [128, 8], F32)
            mrow = sbt(ctx, rp, "m_mrow", [128, 2], F32)
            for tc in range(8):
                xch = [x_fm[:, k * TOK + tc * 128:k * TOK + tc * 128 + 128] for k in range(NCH)]
                ctx.wait("tensor", "vec")
                ctx.wait("tensor", "scl")
                mv = None
                for k in range(NCH):
                    mv = ctx.mm(p_r[:, 0:8], xch[k], gate_w_s[:, k * 8:k * 8 + 8],
                                k == 0, k == NCH - 1, inc="mm" if k == NCH - 1 else None)
                ctx.wait("vector", "mm", mv)
                ctx.op("vector", lambda v, o=mrow[:, 0:1], a=p_r[:, 0:8]:
                       v.reduce_max(o, a, axis=AX.X))
                ctx.ts("vector", mrow[:, 1:2], mrow[:, 0:1], -1.0, 1.0, ALU.mult, ALU.mult,
                       inc="vec")
                ctx.wait("scalar", "vec")
                sv = ctx.act(sc[:], p_r[:, 0:8], AF.Exp, bias=mrow[:, 1:2], inc="scl")
                ctx.wait("vector", "scl", sv)
                ctx.op("vector", lambda v, o=mrow[:, 0:1], a=sc[:]:
                       v.reduce_sum(o, a, axis=AX.X))
                ctx.op("vector", lambda v, o=mrow[:, 1:2], a=mrow[:, 0:1]: v.reciprocal(o, a))
                ctx.ts("vector", sc[:], sc[:], mrow[:, 1:2], 1.0, ALU.mult, ALU.mult)
                ctx.ts("vector", rk[:], sc[:], sc[:, 0:1], 0.0, ALU.is_lt, ALU.add)
                for j in range(1, 8):
                    ctx.ts("vector", t8[:], sc[:], sc[:, j:j + 1], 1.0, ALU.is_lt, ALU.mult)
                    ctx.tt("vector", rk[:], rk[:], t8[:], ALU.add)
                for j in range(7):
                    ctx.ts("vector", t8[:], sc[:], sc[:, j:j + 1], 1.0, ALU.is_equal, ALU.mult)
                    ctx.tt("vector", t8[:], t8[:], ltmask[:, j * 8:j * 8 + 8], ALU.mult)
                    ctx.tt("vector", rk[:], rk[:], t8[:], ALU.add)
                ctx.ts("vector", rk[:], rk[:], 2.0, 1.0, ALU.is_lt, ALU.mult)
                cv = ctx.tt("vector", sc[:], sc[:], rk[:], ALU.mult, inc="vec")
                ctx.wait("tensor", "vec", cv)
                tv = ctx.trans(p_tr[0:8, 0:128], sc[:], C["ident_f32"][:], inc="mm")
                ctx.wait("vector", "mm", tv)
                ctx.copy("vector", comb_t[:, tc * 128:tc * 128 + 128], p_tr[0:8, 0:128],
                         inc="vec")
            ctx.wait("gpsimd", "vec")
            for e_ in range(8):
                ctx.dma("gpsimd", comb_r[:, e_ * 1024:(e_ + 1) * 1024],
                        comb_t[e_:e_ + 1, :], inc="m_cr")
            ctx.barrier()
            ctx.wait("tensor", "m_cr")
            if dbg_comb is not None:
                ctx.dma("gpsimd", dbg_comb, comb_r[:], inc="m_dbg")
                ctx.wait("gpsimd", "m_dbg")
        wgt = sbt(ctx, inner, "m_wg", [128, 2 * 16 * 512], BF16)
        wut = sbt(ctx, inner, "m_wu", [128, 2 * 16 * 512], BF16)
        wdt = sbt(ctx, inner, "m_wd", [128, 2 * 11 * 512], BF16)
        gbuf = sbt(ctx, inner, "m_G", [128, 11 * TOK], BF16)
        gtmp = sbt(ctx, inner, "m_gt", [128, 2 * 512], BF16)
        p_g = [pst(ctx, inner, f"m_pg{i}", [128, 512]) for i in range(2)]
        p_u = [pst(ctx, inner, f"m_pu{i}", [128, 512]) for i in range(2)]
        p_cb = [pst(ctx, inner, f"m_pcb{i}", [128, 512]) for i in range(2)]
        p_d = [pst(ctx, inner, f"m_pd{i}", [128, 512]) for i in range(2)]

        def colslice(dst, slot, slotsz, src, r0, nr, c0, ncol, sem):
            o = dst[:, slot * slotsz:slot * slotsz + (nr // 128) * ncol]
            o = o.rearrange("p (c n) -> p c n", n=ncol)
            i = src[r0:r0 + nr, c0:c0 + ncol].rearrange("(c p) n -> p c n", p=128)
            return ctx.dma("gpsimd", o, i, inc=sem)

        moe_wld = {}
        moe_wmm = {0: 0, 1: 0}
        for e in range(10):
            routed = e < 8
            Hh = 1408
            nhc = Hh // 128
            supers = [(0, 512), (512, 512), (1024, 384)]
            if routed:
                g_src = wg_dr[e * 2048:(e + 1) * 2048, :]
                u_src = wu_dr[e * 2048:(e + 1) * 2048, :]
                d_src = wd_dr[e * 1408:(e + 1) * 1408, :]
            else:
                sh = (e - 8) * 1408
                g_src = swg_dr[:, sh:sh + 1408]
                u_src = swu_dr[:, sh:sh + 1408]
                d_src = swd_dr[sh:sh + 1408, :]
            if routed:
                ctx.wait("tensor", "vec")
                for half in range(2):
                    ctx.mm(p_cb[half][:], C["ones_row_bf"][:],
                           comb_r[:, e * 1024 + half * 512:e * 1024 + half * 512 + 512],
                           True, True, inc="mm")
            gdv = {0: ctx.val("vec"), 1: ctx.val("vec")}
            sdv = {0: ctx.val("scl"), 1: ctx.val("scl")}
            for si, (h0, w) in enumerate(supers):
                gi0 = e * 3 + si
                slot = gi0 % 2
                if gi0 == 0:
                    moe_wld[0] = (
                        colslice(wgt, 0, 16 * 512, g_src, 0, 2048, h0, w, "m_swg0"),
                        colslice(wut, 0, 16 * 512, u_src, 0, 2048, h0, w, "m_swu0"))
                nsi = gi0 + 1
                if nsi < 30:
                    ne, nsl = nsi // 3, nsi % 3
                    nh0, nw = supers[nsl]
                    if ne < 8:
                        ng = wg_dr[ne * 2048:(ne + 1) * 2048, :]
                        nu = wu_dr[ne * 2048:(ne + 1) * 2048, :]
                    else:
                        nsh = (ne - 8) * 1408
                        ng = swg_dr[:, nsh:nsh + 1408]
                        nu = swu_dr[:, nsh:nsh + 1408]
                    ctx.wait("gpsimd", "mm", moe_wmm[nsi % 2])
                    moe_wld[nsi] = (
                        colslice(wgt, nsi % 2, 16 * 512, ng, 0, 2048, nh0, nw,
                                 f"m_swg{nsi % 2}"),
                        colslice(wut, nsi % 2, 16 * 512, nu, 0, 2048, nh0, nw,
                                 f"m_swu{nsi % 2}"))
                ctx.wait("tensor", f"m_swg{slot}", moe_wld[gi0][0])
                ctx.wait("tensor", f"m_swu{slot}", moe_wld[gi0][1])
                for hc in range(w // 128):
                    gi = h0 // 128 + hc
                    for half in range(2):
                        sl_ = (hc * 2 + half) % 2
                        xsl = [x_fm[:, k * TOK + half * 512:k * TOK + half * 512 + 512]
                               for k in range(NCH)]
                        gsl = gbuf[:, gi * TOK + half * 512:gi * TOK + half * 512 + 512]
                        gts = gtmp[:, sl_ * 512:sl_ * 512 + 512]
                        ctx.wait("tensor", "vec", gdv[sl_])
                        ctx.wait("tensor", "scl", sdv[sl_])
                        mg = mu = None
                        for k in range(NCH):
                            off = slot * 16 * 512 + (k * (w // 128) + hc) * 128
                            mg = ctx.mm(p_g[sl_][:], wgt[:, off:off + 128], xsl[k],
                                        k == 0, k == NCH - 1,
                                        inc="mm" if k == NCH - 1 else None)
                        for k in range(NCH):
                            off = slot * 16 * 512 + (k * (w // 128) + hc) * 128
                            mu = ctx.mm(p_u[sl_][:], wut[:, off:off + 128], xsl[k],
                                        k == 0, k == NCH - 1,
                                        inc="mm" if k == NCH - 1 else None)
                        moe_wmm[slot] = mu
                        ctx.wait("scalar", "mm", mg)
                        sv = ctx.act(gts, p_g[sl_][:], AF.Silu, inc="scl")
                        sdv[sl_] = sv
                        ctx.wait("vector", "mm", mu)
                        ctx.wait("vector", "scl", sv)
                        if routed:
                            ctx.tt("vector", gsl, gts, p_u[sl_][:], ALU.mult)
                            gdv[sl_] = ctx.tt("vector", gsl, gsl, p_cb[half][:], ALU.mult,
                                              inc="vec")
                        else:
                            gdv[sl_] = ctx.tt("vector", gsl, gts, p_u[sl_][:], ALU.mult,
                                              inc="vec")
            gdone = ctx.val("vec")
            dn_slot = {0: ctx.val("mm"), 1: ctx.val("mm")}
            pdv = {0: ctx.val("vec"), 1: ctx.val("vec")}
            dslotsz = 11 * 512
            for ds in range(4):
                slot = ds % 2
                ctx.wait("gpsimd", "mm", dn_slot[slot])
                d_ld = colslice(wdt, slot, dslotsz, d_src, 0, Hh, ds * 512, 512, f"m_swd{slot}")
                ctx.wait("tensor", f"m_swd{slot}", d_ld)
                ctx.wait("tensor", "vec", gdone)
                for dc in range(4):
                    dci = ds * 4 + dc
                    for half in range(2):
                        sl_ = (dc * 2 + half) % 2
                        pd = p_d[sl_]
                        ctx.wait("tensor", "vec", pdv[sl_])
                        mv = None
                        for hc in range(nhc):
                            off = slot * dslotsz + hc * 512 + dc * 128
                            mv = ctx.mm(pd[:], wdt[:, off:off + 128],
                                        gbuf[:, hc * TOK + half * 512:hc * TOK + half * 512 + 512],
                                        hc == 0, hc == nhc - 1,
                                        inc="mm" if hc == nhc - 1 else None)
                        dn_slot[slot] = mv
                        ctx.wait("vector", "mm", mv)
                        rsl = r_fm[:, dci * TOK + half * 512:dci * TOK + half * 512 + 512]
                        if e == 0:
                            pdv[sl_] = ctx.tt("vector", rsl, pd[:],
                                              x_fm[:, dci * TOK + half * 512:dci * TOK + half * 512 + 512],
                                              ALU.add, inc="vec")
                        else:
                            pdv[sl_] = ctx.tt("vector", rsl, rsl, pd[:], ALU.add, inc="vec")
        ctx.barrier()
        inner.close()
        emit_ln_fm(ctx, ph, C, r_fm, TOK, "m", ln_sink_simple(ctx, r_fm, TOK))
        ctx.barrier()
    return r_fm


DN_GROUPS = 8
DN_CHUNKS = 4
DN_HORNER = 3
MHA_STAGE = 4


def emit_mha(ctx, es, C, xsrc, wq_dr, wk_dr, wv_dr, cc3_dr, pre_x=None, half_done=None):
    RT2 = 1.0 / math.sqrt(128.0)
    with ExitStack() as ph:
        wq_s = sbt(ctx, ph, "a_wq", [128, 16 * 512], BF16)
        wk_s = sbt(ctx, ph, "a_wk", [128, 16 * 512], BF16)
        wv_s = sbt(ctx, ph, "a_wv", [128, 16 * 512], BF16)
        q_fm = sbt(ctx, ph, "a_qfm", [128, 4 * 512], BF16)
        k_fm = sbt(ctx, ph, "a_kfm", [128, 4 * SEQ], BF16)
        v_tm = sbt(ctx, ph, "a_vtm", [128, 4 * SEQ], BF16)
        xg = sbt(ctx, ph, "a_xg", [128, 2 * 16 * 512], BF16)
        cosx = sbt(ctx, ph, "a_cos", [128, 32 * 128], BF16)
        sinx = sbt(ctx, ph, "a_sin", [128, 32 * 128], BF16)
        qrp = sbt(ctx, ph, "a_qrp", [128, 512], BF16)
        krp = sbt(ctx, ph, "a_krp", [128, 512], BF16)
        tmpab = sbt(ctx, ph, "a_tab", [128, 256], BF16)
        sqb = sbt(ctx, ph, "a_sqb", [128, 512], BF16)
        ptb = sbt(ctx, ph, "a_ptb", [128, 4 * 512], BF16)
        exps = sbt(ctx, ph, "a_exps", [128, 4 * 512], F32)
        obuf = sbt(ctx, ph, "a_obuf", [128, 512], BF16)
        mb_sb = sbt(ctx, ph, "a_mbsb", [128, 512], BF16)
        rows = sbt(ctx, ph, "a_rows", [1, 3 * 512], F32)
        qnr = sbt(ctx, ph, "a_qnr", [1, 4 * 512], BF16)
        runmax = sbt(ctx, ph, "a_rmx", [1, 8], F32)
        lrow = sbt(ctx, ph, "a_lrow", [1, 512], BF16)
        p_q = pst(ctx, ph, "a_pq", [128, 512])
        p_k = pst(ctx, ph, "a_pk", [128, 512])
        p_v = pst(ctx, ph, "a_pv", [128, 512])
        p_st = pst(ctx, ph, "a_pst", [128, 512])
        p_mb = pst(ctx, ph, "a_pmb", [128, 512])
        p_o = pst(ctx, ph, "a_po", [128, 512])
        p_l = pst(ctx, ph, "a_pl", [1, 512])

        def R(i):
            return rows[:, i * 512:(i + 1) * 512]

        ctx.dma("gpsimd", wq_s[:].rearrange("p (c n) -> p c n", n=512),
                wq_dr.rearrange("(c p) n -> p c n", p=128), inc="a_w")
        ctx.dma("gpsimd", wk_s[:].rearrange("p (c n) -> p c n", n=512),
                wk_dr.rearrange("(c p) n -> p c n", p=128), inc="a_w")
        wv_v = ctx.dma("gpsimd", wv_s[:].rearrange("p (c n) -> p c n", n=512),
                       wv_dr.rearrange("(c p) n -> p c n", p=128), inc="a_w")
        with ExitStack() as tp:
            itr = sbt(ctx, tp, "a_itr", [1, 128], F32)
            thr = sbt(ctx, tp, "a_thr", [1, 64], F32)
            ang = sbt(ctx, tp, "a_ang", [128, 64], F32)
            red = sbt(ctx, tp, "a_red", [128, 64], F32)
            ki = sbt(ctx, tp, "a_ki", [128, 64], I32)

            gv = ctx.op("gpsimd", lambda g, o=thr[:]: g.iota(
                o, pattern=[[1, 64]], base=0, channel_multiplier=0,
                allow_small_or_imprecise_dtypes=True), inc="gp")
            ctx.wait("scalar", "gp", gv)
            sv = ctx.act(thr[:], thr[:], AF.Exp, scale=float(-math.log(10000.0) / 64.0),
                         inc="scl")
            ctx.wait("tensor", "gp", gv)
            ctx.wait("tensor", "scl", sv)
            ang_mv = 0
            for tc in range(32):
                if ang_mv:
                    ctx.wait("gpsimd", "mm", ang_mv)
                gv = ctx.op("gpsimd", lambda g, o=itr[:], b=tc * 128: g.iota(
                    o, pattern=[[1, 128]], base=b, channel_multiplier=0,
                    allow_small_or_imprecise_dtypes=True), inc="gp")
                ctx.wait("tensor", "gp", gv)
                mv = ctx.mm(p_st[:, 0:64], itr[:], thr[:],
                            True, True, inc="mm")
                ang_mv = mv
                ctx.wait("vector", "mm", mv)
                for half, shift in ((0, 0.0), (1, 0.5 * math.pi)):
                    ctx.ts("vector", ang[:], p_st[:, 0:64], 1.0, shift, ALU.mult, ALU.add)
                    ctx.ts("vector", red[:], ang[:], float(1.0 / (2 * math.pi)), 1.0,
                           ALU.mult, ALU.mult)
                    ctx.copy("vector", ki[:], red[:])
                    ctx.copy("vector", red[:], ki[:])
                    ctx.stt("vector", ang[:], red[:], float(-2 * math.pi), ang[:],
                            ALU.mult, ALU.add)
                    ctx.ts("vector", red[:], ang[:], float(math.pi), 1.0, ALU.is_gt, ALU.mult)
                    vv = ctx.stt("vector", ang[:], red[:], float(-2 * math.pi), ang[:],
                                 ALU.mult, ALU.add, inc="vec")
                    ctx.wait("scalar", "vec", vv)
                    dst = sinx if half == 0 else cosx
                    s0 = ctx.act(dst[:, tc * 128:tc * 128 + 127:2], ang[:], AF.Sin,
                                 inc="scl")
                    s1 = ctx.act(dst[:, tc * 128 + 1:tc * 128 + 128:2], ang[:], AF.Sin,
                                 inc="scl")
                    ctx.wait("vector", "scl", s1)
                ctx.wait("tensor", "vec")
        ctx.wait("tensor", "a_w", wv_v)
        ctx.wait("vector", "a_w", wv_v)

        ctx.op("vector", lambda v, o=runmax[:]: v.memset(o, 0.0), inc="vec")

        xg_mm = {0: 0, 1: 0}
        for g in range(8 if MHA_STAGE >= 2 else 0):
            slot = g % 2
            if g == 0 and pre_x is not None:
                pre_x()
            ctx.wait("gpsimd", "mm", xg_mm[slot])
            xv = ctx.dma("gpsimd",
                         xg[:, slot * 8192:slot * 8192 + 8192].rearrange(
                             "p (c n) -> p c n", n=512),
                         xsrc(g).rearrange("(c p) n -> p c n", p=128), inc=f"a_x{slot}")
            ctx.wait("tensor", f"a_x{slot}", xv)
            for tc4 in range(4):
                tc = g * 4 + tc4
                xsl = [xg[:, slot * 8192 + k * 512 + tc4 * 128:
                          slot * 8192 + k * 512 + tc4 * 128 + 128] for k in range(NCH)]
                ctx.wait("tensor", "vec")
                mq = mk = mvv = None
                for k in range(NCH):
                    mq = ctx.mm(p_q[:], xsl[k], wq_s[:, k * 512:k * 512 + 512],
                                k == 0, k == NCH - 1, inc="mm" if k == NCH - 1 else None)
                for k in range(NCH):
                    mk = ctx.mm(p_k[:], xsl[k], wk_s[:, k * 512:k * 512 + 512],
                                k == 0, k == NCH - 1, inc="mm" if k == NCH - 1 else None)
                for k in range(NCH):
                    mvv = ctx.mm(p_v[:], xsl[k], wv_s[:, k * 512:k * 512 + 512],
                                 k == 0, k == NCH - 1, inc="mm" if k == NCH - 1 else None)
                xg_mm[slot] = mvv
                ctx.wait("vector", "mm", mvv)
                ctx.copy("vector", v_tm[:, tc * 512:tc * 512 + 512], p_v[:], inc="vec")
                if MHA_STAGE < 2.3:
                    continue
                ctx.wait("vector", "mm")
                cose = cosx[:, tc * 128:tc * 128 + 128]
                sine = sinx[:, tc * 128:tc * 128 + 128]
                for src_ps, dstbuf in ((p_q, qrp), (p_k, krp)):
                    for h in range(4):
                        psl = src_ps[:, h * 128:h * 128 + 128]
                        dsl = dstbuf[:, h * 128:h * 128 + 128]
                        ctx.tt("vector", tmpab[:, 0:128], psl, cose, ALU.mult)
                        ctx.tt("vector", tmpab[:, 128:256], psl, sine, ALU.mult)
                        ctx.tt("vector", dstbuf[:, h * 128:h * 128 + 127:2],
                               tmpab[:, 0:127:2], tmpab[:, 129:256:2], ALU.subtract)
                        ctx.tt("vector", dstbuf[:, h * 128 + 1:h * 128 + 128:2],
                               tmpab[:, 1:128:2], tmpab[:, 128:255:2], ALU.add,
                               inc="vec")
                if MHA_STAGE < 2.6:
                    continue
                ctx.wait("tensor", "vec")
                ptr_v = [p_q[:].bitcast(BF16), p_k[:].bitcast(BF16)]
                evv = {}
                for i, (srcbuf, dstfm) in enumerate(((qrp, q_fm), (krp, k_fm))):
                    for h in range(4):
                        j = i * 4 + h
                        if j >= 2:
                            ctx.wait("tensor", "vec", evv[j - 2])
                        tv = ctx.trans(ptr_v[j % 2][:, 0:128],
                                       srcbuf[:, h * 128:h * 128 + 128],
                                       C["ident_bf"][:], inc="mm")
                        ctx.wait("vector", "mm", tv)
                        dcol = (h * 512 + tc4 * 128) if i == 0 else (h * SEQ + tc * 128)
                        evv[j] = ctx.copy(
                            "vector",
                            dstfm[:, dcol:dcol + 128],
                            ptr_v[j % 2][:, 0:128], inc="vec")
            if MHA_STAGE < 3:
                continue
            for h in range(4):
                qw = h * 512
                kw = h * SEQ + g * 512
                kv = ctx.tt("vector", sqb[:], k_fm[:, kw:kw + 512], k_fm[:, kw:kw + 512],
                            ALU.mult, inc="vec")
                ctx.wait("tensor", "vec", kv)
                mv = ctx.mm(p_l[:], C["ones_col_bf"][:], sqb[:], True, True, inc="mm")
                ctx.wait("vector", "mm", mv)
                ctx.op("vector", lambda v, o=R(0)[:, 0:1], a=p_l[:, 0:512]:
                       v.reduce_max(o, a, axis=AX.X))
                ctx.tt("vector", runmax[:, h:h + 1], runmax[:, h:h + 1], R(0)[:, 0:1],
                       ALU.max, inc="vec")
                qv = ctx.tt("vector", sqb[:], q_fm[:, qw:qw + 512], q_fm[:, qw:qw + 512],
                            ALU.mult, inc="vec")
                ctx.wait("tensor", "vec", qv)
                mv = ctx.mm(p_l[:], C["ones_col_bf"][:], sqb[:], True, True, inc="mm")
                ctx.wait("scalar", "mm", mv)
                sv = ctx.act(R(1), p_l[:, 0:512], AF.Ln, inc="scl")
                sv = ctx.act(qnr[:, h * 512:h * 512 + 512], R(1), AF.Exp, scale=0.5,
                             inc="scl")
                ctx.wait("vector", "scl", sv)
            if MHA_STAGE < 4:
                continue
            sv0 = ctx.val("vec")
            ctx.wait("scalar", "vec", sv0)
            sv = ctx.act(runmax[:, 4:8], runmax[:, 0:4], AF.Ln, inc="scl")
            sv = ctx.act(runmax[:, 4:8], runmax[:, 4:8], AF.Exp, scale=0.5, inc="scl")
            ctx.wait("vector", "scl", sv)
            for h in range(4):
                qw = h * 512
                mrv = ctx.ts("vector", R(2), qnr[:, h * 512:h * 512 + 512],
                             runmax[:, 4 + h:5 + h], RT2, ALU.mult, ALU.mult, inc="vec")
                ctx.wait("tensor", "vec", mrv)
                mbv = ctx.mm(p_mb[:], C["ones_row_f32"][:], R(2), True, True, inc="mm")
                ctx.wait("vector", "mm", mbv)
                ctx.copy("vector", mb_sb[:], p_mb[:], inc="vec")
                nk = 4 * (g + 1)
                lv = None
                stslots = [p_st, p_q, p_k, p_v]
                NS = 4
                stv_s = {i: 0 for i in range(NS)}
                rdy = {}
                ptv = {i: 0 for i in range(NS)}
                svv = {i: 0 for i in range(NS)}
                pvv = {i: 0 for i in range(NS)}
                for kc in range(nk + 1):
                    if kc < nk:
                        ps = kc % NS
                        if ptv[ps]:
                            ctx.wait("tensor", "vec", ptv[ps])
                        stv_s[ps] = ctx.mm(
                            stslots[ps][:],
                            k_fm[:, h * SEQ + kc * 128:h * SEQ + kc * 128 + 128],
                            q_fm[:, qw:qw + 512], True, True, inc="mm")
                    if kc > 0:
                        pc = (kc - 1) % NS
                        ctx.wait("tensor", "vec" if (kc - 1 - 4 * g) >= 0 else "scl",
                                 rdy[kc - 1])
                        ctx.mm(p_o[:], v_tm[:, ((kc - 1) * 4 + h) * 128:
                                              ((kc - 1) * 4 + h) * 128 + 128],
                               ptb[:, pc * 512:pc * 512 + 512], kc - 1 == 0,
                               kc - 1 == nk - 1, inc="mm")
                        lv = ctx.mm(p_l[:], C["ones_col_bf"][:],
                                    ptb[:, pc * 512:pc * 512 + 512], kc - 1 == 0,
                                    kc - 1 == nk - 1, inc="mm")
                        pvv[pc] = lv
                    if kc < nk:
                        ps = kc % NS
                        ctx.wait("vector", "mm", stv_s[ps])
                        if svv[ps]:
                            ctx.wait("vector", "scl", svv[ps])
                        ev = ctx.stt("vector", exps[:, ps * 512:ps * 512 + 512],
                                     stslots[ps][:], RT2, mb_sb[:], ALU.mult,
                                     ALU.subtract, inc="vec")
                        ptv[ps] = ev
                        ctx.wait("scalar", "vec", ev)
                        if pvv[ps]:
                            ctx.wait("scalar", "mm", pvv[ps])
                        sv = ctx.act(ptb[:, ps * 512:ps * 512 + 512],
                                     exps[:, ps * 512:ps * 512 + 512], AF.Exp, inc="scl")
                        svv[ps] = sv
                        rdy[kc] = sv
                        dj = kc - 4 * g
                        if dj >= 0:
                            ctx.wait("vector", "scl", sv)
                            rdy[kc] = ctx.tt("vector", ptb[:, ps * 512:ps * 512 + 512],
                                             ptb[:, ps * 512:ps * 512 + 512],
                                             C["diagmask"][:, dj * 512:dj * 512 + 512],
                                             ALU.mult, inc="vec")
                ctx.wait("vector", "mm", lv)
                ov = ctx.copy("vector", obuf[:], p_o[:], inc="vec")
                lvv = ctx.copy("vector", lrow[:], p_l[:, 0:512], inc="vec")
                ctx.wait("sync", "vec", lvv)
                halfd = cc3_dr[0] if g < 4 else cc3_dr[1]
                r0 = ((g // 2) % 2) * 516
                c0 = (g % 2) * 512
                ctx.dma("sync", halfd[r0 + h * 128:r0 + h * 128 + 128, c0:c0 + 512],
                        obuf[:], inc="a_st")
                sov = ctx.dma("sync", halfd[r0 + 512 + h:r0 + 513 + h, c0:c0 + 512],
                              lrow[:], inc="a_st")
                ctx.wait("vector", "a_st", sov)
            if half_done is not None and g in (3, 7):
                half_done(g // 4)
        ctx.barrier()


def emit_deltanet(ctx, es, C, xsrc, wq_dr, wk_dr, wv_dr, wb_dr, wg1_dr, wg2_dr,
                  gnf_dr, cc1_dr, half_done=None):
    with ExitStack() as ph:
        wq_s = sbt(ctx, ph, "d_wq", [128, 16 * 512], BF16)
        wk_s = sbt(ctx, ph, "d_wk", [128, 16 * 512], BF16)
        wv_s = sbt(ctx, ph, "d_wv", [128, 16 * 512], BF16)
        wb_s = sbt(ctx, ph, "d_wb", [128, 16 * 4], BF16)
        wg1_s = sbt(ctx, ph, "d_wg1", [128, 16 * 128], BF16)
        wg2_s = sbt(ctx, ph, "d_wg2", [128, 512], BF16)
        gnf4 = sbt(ctx, ph, "d_gnf", [128, 512], F32)
        xg = sbt(ctx, ph, "d_xg", [128, 2 * 16 * 512], BF16)
        qf = sbt(ctx, ph, "d_qf", [128, 2 * 2048], BF16)
        kf = sbt(ctx, ph, "d_kf", [128, 2 * 2048], BF16)
        kt = sbt(ctx, ph, "d_kt", [128, 2 * 2048], BF16)
        vt = sbt(ctx, ph, "d_vt", [128, 2 * 2048], BF16)
        gate = sbt(ctx, ph, "d_gate", [128, 2 * 2048], BF16)
        xg1 = sbt(ctx, ph, "d_xg1", [128, 2 * 512], BF16)
        bt = sbt(ctx, ph, "d_bt", [128, 2 * 16], F32)
        ksum = sbt(ctx, ph, "d_ksum", [128, 2 * 16], F32)
        qg_all = sbt(ctx, ph, "d_qga", [128, 2 * 2048], BF16)
        ktm_g = sbt(ctx, ph, "d_ktmg", [128, 2048], F32)
        ogb = sbt(ctx, ph, "d_ogb", [128, 2048], BF16)
        sgt = sbt(ctx, ph, "d_sgt", [128, 512], F32)
        rowb = sbt(ctx, ph, "d_rowb", [1, 2048], F32)
        nb4 = sbt(ctx, ph, "d_nb", [128, 512], BF16)
        ntb4 = sbt(ctx, ph, "d_ntb", [128, 2 * 512], BF16)
        mqb4 = sbt(ctx, ph, "d_mqb", [128, 2 * 512], BF16)
        spw4 = sbt(ctx, ph, "d_spw", [128, 512], BF16)
        spwt4 = sbt(ctx, ph, "d_spwt", [128, 512], BF16)
        ptb4 = sbt(ctx, ph, "d_pt", [128, 2 * 512], BF16)
        rb4 = sbt(ctx, ph, "d_rb", [128, 512], BF16)
        fb4 = sbt(ctx, ph, "d_fb", [128, 512], BF16)
        zb4 = sbt(ctx, ph, "d_zb", [128, 512], BF16)
        ub4 = sbt(ctx, ph, "d_ub", [128, 512], BF16)
        btp4 = sbt(ctx, ph, "d_btp", [128, 2 * 512], BF16)
        s32 = sbt(ctx, ph, "d_s32", [128, 2 * 4 * 128], F32)
        sbf = sbt(ctx, ph, "d_sbf", [128, 2 * 4 * 128], BF16)
        pA = pst(ctx, ph, "d_pA", [128, 512])
        pB = pst(ctx, ph, "d_pB", [128, 512])
        pC = pst(ctx, ph, "d_pC", [128, 512])
        pD = pst(ctx, ph, "d_pD", [128, 512])
        pE = pst(ctx, ph, "d_pE", [128, 512])
        pF = pst(ctx, ph, "d_pF", [128, 512])
        pG = pst(ctx, ph, "d_pG", [128, 512])
        pH = pst(ctx, ph, "d_pH", [128, 512])

        for dst, src, n in ((wq_s, wq_dr, 512), (wk_s, wk_dr, 512), (wv_s, wv_dr, 512),
                            (wb_s, wb_dr, 4), (wg1_s, wg1_dr, 128)):
            ctx.dma("gpsimd", dst[:].rearrange("p (c n) -> p c n", n=n),
                    src.rearrange("(c p) n -> p c n", p=128), inc="d_w")
        ctx.dma("gpsimd", wg2_s[:], wg2_dr, inc="d_w")
        wv_v = ctx.dma("gpsimd", gnf4[:], gnf_dr, inc="d_w")
        for e in ("tensor", "vector", "scalar"):
            ctx.wait(e, "d_w", wv_v)

        sgtb_all = sgt[:].bitcast(BF16)
        sgt4 = sgtb_all[:, 0:512]
        sgt4b = sgtb_all[:, 512:1024]
        pcfree = {0: 0, 1: 0}
        xg_mm = {0: 0, 1: 0}
        for g in range(DN_GROUPS):
            slot = g % 2
            so = slot * 2048
            ctx.wait("gpsimd", "mm", xg_mm[slot])
            xv = ctx.dma("gpsimd",
                         xg[:, slot * 8192:slot * 8192 + 8192].rearrange(
                             "p (c n) -> p c n", n=512),
                         xsrc(g).rearrange("(c p) n -> p c n", p=128), inc=f"d_x{slot}")
            ctx.wait("tensor", f"d_x{slot}", xv)
            for tc4 in range(4):
                xsl = [xg[:, slot * 8192 + k * 512 + tc4 * 128:
                          slot * 8192 + k * 512 + tc4 * 128 + 128] for k in range(NCH)]
                ctx.wait("tensor", "vec")
                ctx.wait("tensor", "scl")
                mvq = mv1 = mvv_ = mvb = mv2 = None
                for k in range(NCH):
                    mvq = ctx.mm(pA[:], xsl[k], wq_s[:, k * 512:k * 512 + 512],
                                 k == 0, k == NCH - 1, inc="mm" if k == NCH - 1 else None)
                for k in range(NCH):
                    mv1 = ctx.mm(pB[:], xsl[k], wk_s[:, k * 512:k * 512 + 512],
                                 k == 0, k == NCH - 1, inc="mm" if k == NCH - 1 else None)
                for k in range(NCH):
                    mvv_ = ctx.mm(pC[:], xsl[k], wv_s[:, k * 512:k * 512 + 512],
                                  k == 0, k == NCH - 1, inc="mm" if k == NCH - 1 else None)
                for k in range(NCH):
                    mvb = ctx.mm(pD[:, 0:4], xsl[k], wb_s[:, k * 4:k * 4 + 4],
                                 k == 0, k == NCH - 1, inc="mm" if k == NCH - 1 else None)
                for k in range(NCH):
                    mv2 = ctx.mm(pE[:, 0:128], wg1_s[:, k * 128:k * 128 + 128], xsl[k],
                                 k == 0, k == NCH - 1, inc="mm" if k == NCH - 1 else None)
                ctx.wait("scalar", "mm", mvq)
                ctx.wait("scalar", "vec")
                sv = ctx.act(sgt[:], pA[:], AF.Sigmoid, inc="scl")
                ctx.wait("vector", "scl", sv)
                qtv = ctx.tt("vector", qg_all[:, so + tc4 * 512:so + tc4 * 512 + 512],
                             pA[:], sgt[:], ALU.mult, inc="vec")
                ctx.wait("scalar", "mm", mv1)
                ctx.wait("scalar", "vec", qtv)
                sv = ctx.act(sgt[:], pB[:], AF.Sigmoid, inc="scl")
                ctx.wait("vector", "scl", sv)
                ctx.tt("vector", ktm_g[:, tc4 * 512:tc4 * 512 + 512], pB[:], sgt[:],
                       ALU.mult, inc="vec")
                ctx.wait("scalar", "mm", mvb)
                bv = ctx.act(bt[:, slot * 16 + tc4 * 4:slot * 16 + tc4 * 4 + 4],
                             pD[:, 0:4], AF.Sigmoid, inc="scl")
                ctx.wait("vector", "scl", bv)
                ctx.ts("vector", bt[:, slot * 16 + tc4 * 4:slot * 16 + tc4 * 4 + 4],
                       bt[:, slot * 16 + tc4 * 4:slot * 16 + tc4 * 4 + 4], 2.0, 1.0,
                       ALU.mult, ALU.mult, inc="vec")
                ctx.wait("vector", "mm", mvv_)
                ctx.copy("vector", vt[:, so + tc4 * 512:so + tc4 * 512 + 512], pC[:],
                         inc="vec")
                ctx.wait("vector", "mm", mv2)
                xv1 = ctx.copy("vector", xg1[:, slot * 512 + tc4 * 128:
                                             slot * 512 + tc4 * 128 + 128],
                               pE[:, 0:128], inc="vec")
                ctx.wait("tensor", "vec", xv1)
                gmv = None
                for h in range(4):
                    gmv = ctx.mm(pF[:, h * 128:h * 128 + 128],
                                 wg2_s[:, h * 128:h * 128 + 128],
                                 xg1[:, slot * 512 + tc4 * 128:slot * 512 + tc4 * 128 + 128],
                                 True, True, inc="mm" if h == 3 else None)
                ctx.wait("scalar", "mm", gmv)
                ctx.act(gate[:, so + tc4 * 512:so + tc4 * 512 + 512], pF[:],
                        AF.Sigmoid, inc="scl")
                for h in range(4):
                    ctx.tt("vector", sgt[:, 0:128],
                           ktm_g[:, tc4 * 512 + h * 128:tc4 * 512 + h * 128 + 128],
                           ktm_g[:, tc4 * 512 + h * 128:tc4 * 512 + h * 128 + 128],
                           ALU.mult)
                    ctx.op("vector", lambda v,
                           o=ksum[:, slot * 16 + tc4 * 4 + h:slot * 16 + tc4 * 4 + h + 1],
                           a=sgt[:, 0:128]: v.reduce_sum(o, a, axis=AX.X), inc="vec")
            vv = ctx.ts("vector", ksum[:, slot * 16:slot * 16 + 16],
                        ksum[:, slot * 16:slot * 16 + 16], 1.0, EPS_DN, ALU.mult,
                        ALU.add, inc="vec")
            ctx.wait("scalar", "vec", vv)
            sv = ctx.act(ksum[:, slot * 16:slot * 16 + 16],
                         ksum[:, slot * 16:slot * 16 + 16], AF.Sqrt, inc="scl")
            ctx.wait("vector", "scl", sv)
            ctx.op("vector", lambda v, o=ksum[:, slot * 16:slot * 16 + 16],
                   a=ksum[:, slot * 16:slot * 16 + 16]: v.reciprocal(o, a), inc="vec")
            pv = [pA[:].bitcast(BF16), pB[:].bitcast(BF16)]
            evv = {}
            ctx.wait("tensor", "vec")
            for tc4 in range(4):
                for h in range(4):
                    i = tc4 * 4 + h
                    ctx.ts("vector", kt[:, so + i * 128:so + i * 128 + 128],
                           ktm_g[:, tc4 * 512 + h * 128:tc4 * 512 + h * 128 + 128],
                           ksum[:, slot * 16 + i:slot * 16 + i + 1], 1.0,
                           ALU.mult, ALU.mult, inc="vec")
            ktv = ctx.val("vec")
            ctx.wait("tensor", "vec", ktv)
            for j in range(32):
                srcb = qg_all if j < 16 else kt
                dstb = qf if j < 16 else kf
                i = j % 16
                if j >= 2:
                    ctx.wait("tensor", "vec", evv[j - 2])
                tv = ctx.trans(pv[j % 2][:, 0:128], srcb[:, so + i * 128:so + i * 128 + 128],
                               C["ident_bf"][:], inc="mm")
                ctx.wait("vector", "mm", tv)
                evv[j] = ctx.copy("vector", dstb[:, so + i * 128:so + i * 128 + 128],
                                  pv[j % 2][:, 0:128], inc="vec")
            ctx.wait("vector", "scl")

            def pc_gen(tc4):
                sl_ = tc4 % 2
                kfs = [kf[:, so + (tc4 * 4 + h) * 128:so + (tc4 * 4 + h) * 128 + 128]
                       for h in range(4)]
                qfs = [qf[:, so + (tc4 * 4 + h) * 128:so + (tc4 * 4 + h) * 128 + 128]
                       for h in range(4)]
                bcol = [bt[:, slot * 16 + tc4 * 4 + h:slot * 16 + tc4 * 4 + h + 1]
                        for h in range(4)]
                ptb = ptb4[:, sl_ * 512:sl_ * 512 + 512]
                ntb = ntb4[:, sl_ * 512:sl_ * 512 + 512]
                mqb = mqb4[:, sl_ * 512:sl_ * 512 + 512]
                btp = btp4[:, sl_ * 512:sl_ * 512 + 512]

                def B4(p, h):
                    return p[:, h * 128:h * 128 + 128]

                ctx.wait("tensor", "vec", pcfree[sl_])
                ctx.wait("tensor", "scl")
                m1 = None
                for h in range(4):
                    ctx.mm(B4(pF, h), kfs[h], kfs[h], True, True)
                for h in range(4):
                    m1 = ctx.mm(B4(pG, h), kfs[h], qfs[h], True, True,
                                inc="mm" if h == 3 else None)
                for h in range(4):
                    ctx.ts("vector", btp[:, h * 128:h * 128 + 128], C["ones_t"][:],
                           bcol[h], 1.0, ALU.mult, ALU.mult)
                yield
                ctx.wait("vector", "mm", m1)
                ctx.tt("vector", sgt4[:], pF[:], btp[:], ALU.mult)
                ctx.tt("vector", nb4[:], sgt4[:], C["mask_bd4n"][:], ALU.mult)
                ctx.tt("vector", fb4[:], sgt4[:], C["mask_of4"][:], ALU.mult)
                v1 = ctx.tt("vector", mqb, pG[:], C["mask_iU4"][:], ALU.mult, inc="vec")
                yield
                ctx.wait("tensor", "vec", v1)
                tv = None
                for h in range(2):
                    tv = ctx.trans([pF, pG][h][:].bitcast(BF16)[:, 0:128],
                                   nb4[:, h * 128:h * 128 + 128], C["ident_bf"][:],
                                   inc="mm" if h == 1 else None)
                yield
                ctx.wait("vector", "mm", tv)
                for h in range(2):
                    ctx.copy("vector", spwt4[:, h * 128:h * 128 + 128],
                             [pF, pG][h][:].bitcast(BF16)[:, 0:128],
                             inc="vec" if h == 1 else None)
                ctx.wait("tensor", "vec")
                tv = None
                for h in range(2, 4):
                    tv = ctx.trans([pF, pG][h - 2][:].bitcast(BF16)[:, 0:128],
                                   nb4[:, h * 128:h * 128 + 128], C["ident_bf"][:],
                                   inc="mm" if h == 3 else None)
                yield
                ctx.wait("vector", "mm", tv)
                for h in range(2, 4):
                    ctx.copy("vector", spwt4[:, h * 128:h * 128 + 128],
                             [pF, pG][h - 2][:].bitcast(BF16)[:, 0:128])
                ctx.copy("vector", spw4[:], nb4[:])
                v1 = ctx.tt("vector", ptb, spwt4[:], C["ident4"][:], ALU.add, inc="vec")
                yield
                for lvl in range(1, 5):
                    ctx.wait("tensor", "vec", v1)
                    mv = None
                    for h in range(4):
                        ctx.mm(B4(pF, h), spwt4[:, h * 128:h * 128 + 128],
                               spw4[:, h * 128:h * 128 + 128], True, True)
                    if lvl < 4:
                        for h in range(4):
                            mv = ctx.mm(B4(pG, h), spw4[:, h * 128:h * 128 + 128],
                                        spwt4[:, h * 128:h * 128 + 128], True, True,
                                        inc="mm" if h == 3 else None)
                    else:
                        mv = ctx.mm(B4(pF, 3), spwt4[:, 384:512], spw4[:, 384:512],
                                    True, True, inc="mm")
                    yield
                    ctx.wait("vector", "mm", mv)
                    ctx.copy("vector", spw4[:], pF[:])
                    if lvl < 4:
                        v1 = ctx.copy("vector", spwt4[:], pG[:], inc="vec")
                    else:
                        v1 = ctx.copy("vector", spw4[:, 0:1], spw4[:, 0:1], inc="vec")
                    yield
                    ctx.wait("tensor", "vec", v1)
                    mv = None
                    for h in range(4):
                        mv = ctx.mm(B4(pH, h), spw4[:, h * 128:h * 128 + 128],
                                    ptb[:, h * 128:h * 128 + 128], True, True,
                                    inc="mm" if h == 3 else None)
                    yield
                    ctx.wait("vector", "mm", mv)
                    v1 = ctx.tt("vector", ptb, pH[:], ptb, ALU.add, inc="vec")
                yield
                ctx.wait("tensor", "vec", v1)
                mv = None
                for h in range(4):
                    mv = ctx.mm(B4(pH, h), fb4[:, h * 128:h * 128 + 128],
                                ptb[:, h * 128:h * 128 + 128], True, True,
                                inc="mm" if h == 3 else None)
                yield
                ctx.wait("vector", "mm", mv)
                pcfree[sl_] = ctx.copy("vector", ntb, pH[:], inc="vec")
                yield

            def serial_gen(tc4):
                sl_ = tc4 % 2
                first = (g == 0 and tc4 == 0)
                par = (g * 4 + tc4) % 2
                kfs = [kf[:, so + (tc4 * 4 + h) * 128:so + (tc4 * 4 + h) * 128 + 128]
                       for h in range(4)]
                qfs = [qf[:, so + (tc4 * 4 + h) * 128:so + (tc4 * 4 + h) * 128 + 128]
                       for h in range(4)]
                kts = [kt[:, so + (tc4 * 4 + h) * 128:so + (tc4 * 4 + h) * 128 + 128]
                       for h in range(4)]
                soldbf = [sbf[:, par * 512 + h * 128:par * 512 + h * 128 + 128]
                          for h in range(4)]
                ptb = ptb4[:, sl_ * 512:sl_ * 512 + 512]
                ntb = ntb4[:, sl_ * 512:sl_ * 512 + 512]
                mqb = mqb4[:, sl_ * 512:sl_ * 512 + 512]
                btp = btp4[:, sl_ * 512:sl_ * 512 + 512]
                vt_w = vt[:, so + tc4 * 512:so + tc4 * 512 + 512]

                def B4(p, h):
                    return p[:, h * 128:h * 128 + 128]

                ctx.wait("tensor", "vec")
                if not first:
                    mv = None
                    for h in range(4):
                        mv = ctx.mm(B4(pA, h), kfs[h], soldbf[h], True, True,
                                    inc="mm" if h == 3 else None)
                    yield
                    ctx.wait("vector", "mm", mv)
                    ctx.stt("vector", sgt4b[:], pA[:], -1.0, vt_w, ALU.mult, ALU.add)
                    rv = ctx.tt("vector", rb4[:], sgt4b[:], btp, ALU.mult, inc="vec")
                else:
                    rv = ctx.tt("vector", rb4[:], vt_w, btp, ALU.mult, inc="vec")
                yield
                ctx.wait("tensor", "vec", rv)
                mv = None
                for h in range(4):
                    mv = ctx.mm(B4(pB, h), ptb[:, h * 128:h * 128 + 128],
                                rb4[:, h * 128:h * 128 + 128], True, True,
                                inc="mm" if h == 3 else None)
                yield
                ctx.wait("vector", "mm", mv)
                zv = ctx.copy("vector", zb4[:], pB[:], inc="vec")
                yield
                usrc4 = zb4
                for it in range(3):
                    udst4 = ub4 if it == 2 else (rb4 if it % 2 == 0 else ub4)
                    ctx.wait("tensor", "vec", zv)
                    mv = None
                    for h in range(4):
                        mv = ctx.mm(B4(pC, h), ntb[:, h * 128:h * 128 + 128],
                                    usrc4[:, h * 128:h * 128 + 128], True, True,
                                    inc="mm" if h == 3 else None)
                    yield
                    ctx.wait("vector", "mm", mv)
                    zv = ctx.stt("vector", udst4[:], pC[:], -1.0, zb4[:], ALU.mult,
                                 ALU.add, inc="vec")
                    usrc4 = udst4
                yield
                ctx.wait("tensor", "vec", zv)
                mv = None
                for h in range(4):
                    mv = ctx.mm(B4(pD, h), kts[h], ub4[:, h * 128:h * 128 + 128],
                                True, True, inc="mm" if h == 3 else None)
                yield
                ctx.wait("vector", "mm", mv)
                s32o = s32[:, par * 512:par * 512 + 512]
                s32n = s32[:, (1 - par) * 512:(1 - par) * 512 + 512]
                sbfn = sbf[:, (1 - par) * 512:(1 - par) * 512 + 512]
                if not first:
                    ctx.tt("vector", s32n, pD[:], s32o, ALU.add)
                else:
                    ctx.copy("vector", s32n, pD[:])
                sv_ = ctx.copy("vector", sbfn, s32n, inc="vec")
                yield
                ctx.wait("tensor", "vec", sv_)
                mv = None
                for h in range(4):
                    if not first:
                        ctx.mm(B4(pE, h), soldbf[h], qfs[h], True, False)
                        mv = ctx.mm(B4(pE, h), ub4[:, h * 128:h * 128 + 128],
                                    mqb[:, h * 128:h * 128 + 128], False, True,
                                    inc="mm" if h == 3 else None)
                    else:
                        mv = ctx.mm(B4(pE, h), ub4[:, h * 128:h * 128 + 128],
                                    mqb[:, h * 128:h * 128 + 128], True, True,
                                    inc="mm" if h == 3 else None)
                yield
                ctx.wait("vector", "mm", mv)
                ogw = ogb[:, tc4 * 512:tc4 * 512 + 512]
                ctx.tt("vector", ogw, pE[:], gate[:, so + tc4 * 512:so + tc4 * 512 + 512],
                       ALU.mult)
                gv = ctx.tt("vector", sgt4b[:], ogw, ogw, ALU.mult, inc="vec")
                yield
                ctx.wait("tensor", "vec", gv)
                mv = ctx.mm(pA[0:1, 0:512], C["ones_col_bf"][:], sgt4b[:], True, True,
                            inc="mm")
                yield
                ctx.wait("vector", "mm", mv)
                ctx.op("vector", lambda v, o=rowb[:, tc4 * 512:tc4 * 512 + 512],
                       a=pA[0:1, 0:512]: v.tensor_copy(o, a), inc="vec")
                yield

            for t in range(DN_CHUNKS + 1):
                gens = []
                if t < DN_CHUNKS:
                    gens.append(pc_gen(t))
                if t > 0:
                    gens.append(serial_gen(t - 1))
                while gens:
                    nxt = []
                    for gg in gens:
                        try:
                            next(gg)
                            nxt.append(gg)
                        except StopIteration:
                            pass
                    gens = nxt
            if DN_CHUNKS < 4:
                continue
            vv = ctx.val("vec")
            ctx.wait("scalar", "vec", vv)
            sv = ctx.ts("vector", rowb[:], rowb[:], 1.0 / 128.0, EPS_DN, ALU.mult,
                        ALU.add, inc="vec")
            ctx.wait("scalar", "vec", sv)
            sv = ctx.act(rowb[:], rowb[:], AF.Sqrt, inc="scl")
            ctx.wait("vector", "scl", sv)
            rv = ctx.op("vector", lambda v, o=rowb[:], a=rowb[:]: v.reciprocal(o, a),
                        inc="vec")
            ctx.wait("tensor", "vec", rv)
            for tc4 in range(4):
                bv = ctx.mm(pH[:], C["ones_row_f32"][:],
                            rowb[:, tc4 * 512:tc4 * 512 + 512], True, True, inc="mm")
                ctx.wait("vector", "mm", bv)
                ogw = ogb[:, tc4 * 512:tc4 * 512 + 512]
                ctx.tt("vector", ogw, ogw, pH[:], ALU.mult)
                ov = ctx.tt("vector", ogw, ogw, gnf4[:], ALU.mult, inc="vec")
                ctx.wait("tensor", "vec", ov)
                pcfree[0] = pcfree[1] = ov
            ovv = ctx.val("vec")
            ctx.wait("sync", "vec", ovv)
            half = cc1_dr[0] if g < 4 else cc1_dr[1]
            for h in range(4):
                r0 = ((g // 2) % 2) * 512 + h * 128
                c0 = (g % 2) * 512
                sv_ = ctx.dma("sync",
                              half[r0:r0 + 128, c0:c0 + 512]
                              .rearrange("q (a n) -> q a n", a=4),
                              ogb[:].rearrange("p (a c n) -> p a c n", a=4, c=4)[:, :, h, :],
                              inc="d_og")
            ctx.wait("vector", "d_og")
            if half_done is not None and g in (3, 7):
                half_done(g // 4)
            xg_mm[slot] = ctx.val("mm")
        ctx.barrier()


def emit_wo_ln(ctx, es, out_es, C, og_dr, resid_ap, wo_dr, l_norm, sel, tag):
    BLK = 516 if l_norm else 512
    r_fm = sbt(ctx, out_es, f"w_r{tag}", [128, NCH * TOK], BF16)
    with ExitStack() as ph:
        og_s = sbt(ctx, ph, f"w_og{tag}", [128, NCH * TOK], BF16)
        nslot = 1 if l_norm else 2
        og4 = sbt(ctx, ph, f"w_og4{tag}", [128, nslot * 4 * TOK], BF16)
        res_s = sbt(ctx, ph, f"w_res{tag}", [128, NCH * TOK], BF16)
        wot = sbt(ctx, ph, f"w_wo{tag}", [128, 2 * 16 * 512], BF16)
        p_w = [pst(ctx, ph, f"w_pw{tag}{i}", [128, 512]) for i in range(2)]
        p_rl = pst(ctx, ph, f"w_prl{tag}", [128, 512])
        sumv = {0: 0, 1: 0}
        for k in range(NCH):
            r_, lc = k // 4, k % 4
            sl_ = k % nslot
            o4 = og4[:, sl_ * 4 * TOK:(sl_ + 1) * 4 * TOK]
            ctx.wait("gpsimd", "vec", sumv[sl_])
            for w in range(4):
                ldv = ctx.dma("gpsimd", o4[:, w * TOK:(w + 1) * TOK],
                              og_dr(r_, w)[lc * 128:lc * 128 + 128, :],
                              inc=f"w_g{tag}{sl_}")
            ctx.wait("vector", f"w_g{tag}{sl_}", ldv)
            osl = og_s[:, k * TOK:(k + 1) * TOK]
            ctx.ts("vector", osl, o4[:, 0:TOK], sel[:, 0:1], 1.0, ALU.mult, ALU.mult)
            for w in range(1, 3):
                ctx.stt("vector", osl, o4[:, w * TOK:(w + 1) * TOK], sel[:, w:w + 1],
                        osl, ALU.mult, ALU.add)
            sumv[sl_] = ctx.stt("vector", osl, o4[:, 3 * TOK:4 * TOK], sel[:, 3:4],
                                osl, ALU.mult, ALU.add, inc="vec")
        for k in range(NCH):
            ov = ctx.dma("gpsimd", res_s[:, k * TOK:(k + 1) * TOK], resid_ap(k),
                         inc=f"w_ld{tag}")
        ogv = ctx.val("vec")
        if l_norm:
            lr16 = sbt(ctx, ph, f"w_lr16{tag}", [16, 4 * TOK], BF16)
            lrs = sbt(ctx, ph, f"w_lrs{tag}", [16, TOK], F32)
            lr0 = sbt(ctx, ph, f"w_lr0{tag}", [1, 16 * TOK], BF16)
            for w in range(4):
                for r_ in range(4):
                    ov = ctx.dma("gpsimd", lr16[r_ * 4:r_ * 4 + 4, w * TOK:(w + 1) * TOK],
                                 og_dr(r_, w)[512:516, :], inc=f"w_ll{tag}")
            ctx.wait("vector", f"w_ll{tag}", ov)
            ctx.ts("vector", lrs[:], lr16[:, 0:TOK], sel[0:16, 0:1], 1.0,
                   ALU.mult, ALU.mult)
            for w in range(1, 4):
                ctx.stt("vector", lrs[:], lr16[:, w * TOK:(w + 1) * TOK],
                        sel[0:16, w:w + 1], lrs[:], ALU.mult, ALU.add)
            rv = ctx.op("vector", lambda v, o=lrs[:], a=lrs[:]:
                        v.reciprocal(o, a), inc="vec")
            ctx.wait("gpsimd", "vec", rv)
            for k in range(16):
                ov = ctx.dma("gpsimd", lr0[:, k * TOK:(k + 1) * TOK],
                             lrs[k:k + 1, :], inc=f"w_lr{tag}")
            ctx.wait("tensor", f"w_lr{tag}", ov)
            ctx.wait("tensor", f"w_ld{tag}")
            ctx.wait("tensor", "vec", ogv)
            for k in range(NCH):
                for th in range(2):
                    bv = ctx.mm(p_rl[:], C["ones_row_bf"][:],
                                lr0[:, k * TOK + th * 512:k * TOK + th * 512 + 512],
                                True, True, inc="mm")
                    ctx.wait("vector", "mm", bv)
                    osl = og_s[:, k * TOK + th * 512:k * TOK + th * 512 + 512]
                    nv = ctx.tt("vector", osl, osl, p_rl[:], ALU.mult, inc="vec")
                    ctx.wait("tensor", "vec", nv)
        else:
            ctx.wait("tensor", f"w_ld{tag}")
            ctx.wait("tensor", "vec", ogv)
        wo_mm = {0: 0, 1: 0}
        wev = {0: ctx.val("vec"), 1: ctx.val("vec")}
        for ds in range(4):
            slot = ds % 2
            ctx.wait("gpsimd", "mm", wo_mm[slot])
            o = wot[:, slot * 8192:slot * 8192 + 8192].rearrange("p (c n) -> p c n", n=512)
            i = wo_dr[:, ds * 512:ds * 512 + 512].rearrange("(c p) n -> p c n", p=128)
            wv = ctx.dma("gpsimd", o, i, inc=f"w_w{tag}{slot}")
            ctx.wait("tensor", f"w_w{tag}{slot}", wv)
            for dc in range(4):
                dci = ds * 4 + dc
                for th in range(2):
                    pw = p_w[th]
                    ctx.wait("tensor", "vec", wev[th])
                    mv = None
                    for k in range(NCH):
                        off = slot * 8192 + (k * 4 + dc) * 128
                        mv = ctx.mm(pw[:], wot[:, off:off + 128],
                                    og_s[:, k * TOK + th * 512:k * TOK + th * 512 + 512],
                                    k == 0, k == NCH - 1,
                                    inc="mm" if k == NCH - 1 else None)
                    wo_mm[slot] = mv
                    ctx.wait("vector", "mm", mv)
                    wev[th] = ctx.tt("vector",
                                     r_fm[:, dci * TOK + th * 512:dci * TOK + th * 512 + 512],
                                     pw[:], res_s[:, dci * TOK + th * 512:dci * TOK + th * 512 + 512],
                                     ALU.add, inc="vec")
        ctx.barrier()
        emit_ln_fm(ctx, ph, C, r_fm, TOK, f"w{tag}", ln_sink_simple(ctx, r_fm, TOK))
        ctx.barrier()
    return r_fm



def build_full():
    nc = bass.Bass("TRN2", target_bir_lowering=False)
    dr = {}

    def ein(name, shape, dt):
        dr[name] = nc.dram_tensor(name, shape, dt, kind="ExternalInput").ap()
        return dr[name]

    for k, (shape, dt) in CONST_SPECS.items():
        ein(k, shape, dt)
    ein("x_fm", [D, SEQ], F32)
    ein("x_res", [D, TOK], F32)
    for nm in ("wq0", "wk0", "wv0"):
        ein(nm, [D, 512], F32)
    ein("wb0", [D, 4], F32)
    ein("wg1t", [D, 128], F32)
    ein("wg2sl", [128, 512], F32)
    ein("gnf", [128, 512], F32)
    ein("wo0f", [D, D], F32)
    ein("gate_w", [D, 8], F32)
    ein("ltmask", [128, 64], F32)
    ein("wg_f", [8 * D, 1408], F32)
    ein("wu_f", [8 * D, 1408], F32)
    ein("wd_f", [8 * 1408, D], F32)
    ein("swg_f", [D, 2816], F32)
    ein("swu_f", [D, 2816], F32)
    ein("swd_f", [2816, D], F32)
    for nm in ("wq1", "wk1", "wv1"):
        ein(nm, [D, 512], F32)
    ein("wo1f", [D, D], F32)
    ein("wg8f", [D, 8192], I8)
    ein("wu8f", [D, 8192], I8)
    ein("wd8f", [8192, D], I8)
    ein("sel", [128, 4], F32)
    ein("sg_t", [128, 64], F32)
    ein("su_t", [128, 64], F32)
    ein("sd_t", [128, 16], F32)
    out = nc.dram_tensor("out", [TOK, D], F32, kind="ExternalOutput").ap()

    cc1_inA = nc.dram_tensor("cc1_inA", [1, 1024 * TOK], BF16)
    cc1_inB = nc.dram_tensor("cc1_inB", [1, 1024 * TOK], BF16)
    cc1_outA = nc.dram_tensor("cc1_outA", [4, 1024 * TOK], BF16)
    cc1_outB = nc.dram_tensor("cc1_outB", [4, 1024 * TOK], BF16)
    cc2_in = nc.dram_tensor("cc2_in", [1, D * TOK], BF16)
    cc2_out = nc.dram_tensor("cc2_out", [4, D * TOK], BF16)
    cc3_inA = nc.dram_tensor("cc3_inA", [1, 2 * 516 * TOK], BF16)
    cc3_inB = nc.dram_tensor("cc3_inB", [1, 2 * 516 * TOK], BF16)
    cc3_outA = nc.dram_tensor("cc3_outA", [4, 2 * 516 * TOK], BF16)
    cc3_outB = nc.dram_tensor("cc3_outB", [4, 2 * 516 * TOK], BF16)
    x2sav = nc.dram_tensor("x2sav", [D, TOK], BF16)
    cc1_inv = (cc1_inA.ap().rearrange("o (b t) -> (o b) t", t=TOK),
               cc1_inB.ap().rearrange("o (b t) -> (o b) t", t=TOK))
    cc3_inv = (cc3_inA.ap().rearrange("o (b t) -> (o b) t", t=TOK),
               cc3_inB.ap().rearrange("o (b t) -> (o b) t", t=TOK))
    cc2_inv = cc2_in.ap().rearrange("o (r t) -> (o r) t", t=TOK)
    cc2_outv = cc2_out.ap().rearrange("r (b t) -> (r b) t", t=TOK)

    def mk_outv(tA, tB):
        vA = tA.ap().rearrange("r (w b t) -> r w b t", w=2, t=TOK)
        vB = tB.ap().rearrange("r (w b t) -> r w b t", w=2, t=TOK)

        def f(r, w):
            return (vA if w < 2 else vB)[r, w % 2]
        return f

    cc1_outv = mk_outv(cc1_outA, cc1_outB)
    cc3_outv = mk_outv(cc3_outA, cc3_outB)

    G8 = [list(range(8))]
    G4 = [[0, 1, 2, 3], [4, 5, 6, 7]]

    es = ExitStack()
    ctx = Ctx(nc, es)
    C = emit_load_consts(ctx, es, dr)
    gws = sbt(ctx, es, "gws", [128, NCH * 8], BF16)
    ltm = sbt(ctx, es, "ltm", [128, 64], F32)
    sel = sbt(ctx, es, "sel_s", [128, 4], F32)
    ctx.dma("gpsimd", sel[:], dr["sel"], inc="ldc")
    ctx.dma("gpsimd", gws[:].rearrange("p (c t) -> p c t", t=8),
            dr["gate_w"].rearrange("(c p) t -> p c t", p=128), inc="ldc")
    ctx.dma("gpsimd", ltm[:], dr["ltmask"], inc="ldc")

    for e_ in ("tensor", "vector", "scalar", "sync", "gpsimd"):
        ctx.wait(e_, "ldc")

    def og_half(idx):
        ctx.wait("gpsimd", "d_og")
        t_in = [cc1_inA, cc1_inB][idx]
        t_out = [cc1_outA, cc1_outB][idx]
        ctx.op("gpsimd", lambda g, i=t_in, o=t_out: g.collective_compute(
            "AllGather", ALU.bypass, replica_groups=G4,
            ins=[i.ap().opt()], outs=[o.ap().opt()]), inc="cc", chain=False)

    emit_deltanet(ctx, es, C, lambda g: dr["x_fm"][:, g * 512:(g + 1) * 512],
                  dr["wq0"], dr["wk0"], dr["wv0"], dr["wb0"], dr["wg1t"],
                  dr["wg2sl"], dr["gnf"], cc1_inv, half_done=og_half)
    ctx.wait("gpsimd", "cc")

    es_x2 = ExitStack()
    x2buf = sbt(ctx, es_x2, "x2buf", [128, NCH * TOK], BF16)
    es_x1 = ExitStack()
    x1 = emit_wo_ln(ctx, es, es_x1, C, cc1_outv,
                    lambda k: dr["x_res"][k * 128:(k + 1) * 128, :],
                    dr["wo0f"], False, sel, "0")

    x2 = emit_moe(ctx, es_x2, C, x1, gws, ltm, dr["wg_f"], dr["wu_f"],
                  dr["wd_f"], dr["swg_f"], dr["swu_f"], dr["swd_f"],
                  r_fm=x2buf)
    es_x1.close()
    x2v = ctx.dma("gpsimd", cc2_inv.rearrange("(c p) t -> p c t", p=128),
                  x2[:].rearrange("p (c t) -> p c t", t=TOK), inc="x2s")
    sv2 = ctx.dma("gpsimd", x2sav[:, :].rearrange("(c p) t -> p c t", p=128),
                  x2[:].rearrange("p (c t) -> p c t", t=TOK), inc="x2p")
    ctx.wait("gpsimd", "x2s", x2v)
    ctx.op("gpsimd", lambda g: g.collective_compute(
        "AllGather", ALU.bypass, replica_groups=G4,
        ins=[cc2_in.ap().opt()], outs=[cc2_out.ap().opt()]), inc="cc", chain=False)
    cc_n = ctx.val("cc")
    x2p_n = sv2
    ctx.wait("vector", "x2s", x2v)
    ctx.wait("vector", "x2p", sv2)
    ctx.barrier(exclude=("cc",))
    es_x2.close()

    def o1_half(idx):
        ctx.wait("gpsimd", "a_st")
        t_in = [cc3_inA, cc3_inB][idx]
        t_out = [cc3_outA, cc3_outB][idx]
        ctx.op("gpsimd", lambda g, i=t_in, o=t_out: g.collective_compute(
            "AllGather", ALU.bypass, replica_groups=G4,
            ins=[i.ap().opt()], outs=[o.ap().opt()]), inc="cc", chain=False)

    emit_mha(ctx, es, C,
             lambda g: cc2_outv[(g // 2) * D:(g // 2) * D + D,
                                  (g % 2) * 512:(g % 2) * 512 + 512],
             dr["wq1"], dr["wk1"], dr["wv1"], cc3_inv,
             pre_x=lambda: ctx.wait("gpsimd", "cc", cc_n), half_done=o1_half)
    ctx.wait("gpsimd", "cc")

    sgt = sbt(ctx, es, "sgt_s", [128, 64], F32)
    sut = sbt(ctx, es, "sut_s", [128, 64], F32)
    sdt = sbt(ctx, es, "sdt_s", [128, 16], F32)
    ctx.dma("gpsimd", sgt[:], dr["sg_t"])
    ctx.dma("gpsimd", sut[:], dr["su_t"])
    ctx.dma("gpsimd", sdt[:], dr["sd_t"])
    es_x3 = ExitStack()
    x3 = emit_wo_ln(ctx, es, es_x3, C, cc3_outv,
                    lambda k: x2sav.ap()[k * 128:(k + 1) * 128, :],
                    dr["wo1f"], True, sel, "1")
    for e in ("tensor", "vector", "scalar"):
        ctx.wait(e, "ld")
    emit_l1ffn(ctx, es, C, x3, dr["wg8f"], dr["wu8f"], dr["wd8f"],
               sgt, sut, sdt, out)
    es_x3.close()
    ctx.wait("sync", "st")
    ctx.finalize()
    es.close()
    return nc


_NC_CACHE = []


def kernel(**inputs):
    g = dict(inputs)
    consts = host_consts()
    ii = np.arange(8)
    ltm = np.zeros((128, 64), np.float32)
    for j in range(8):
        ltm[:, j * 8:(j + 1) * 8] = (ii[None, :] > j)

    x = np.asarray(g["L0_attn_x"], np.float32)
    A = lambda a: np.ascontiguousarray(a)
    wo0f = A(np.asarray(g["L0_attn_Wo"]).T)
    wo1f = A(np.asarray(g["L1_attn_Wo"]).T)
    wg_f = A(np.asarray(g["L0_ffn_w_gate"]).reshape(8 * 2048, 1408))
    wu_f = A(np.asarray(g["L0_ffn_w_up"]).reshape(8 * 2048, 1408))
    wd_f = A(np.asarray(g["L0_ffn_w_down"]).reshape(8 * 1408, 2048))
    swg_f = A(np.asarray(g["L0_ffn_sw_gate"]))
    swu_f = A(np.asarray(g["L0_ffn_sw_up"]))
    swd_f = A(np.asarray(g["L0_ffn_sw_down"]))
    wg8f = A(np.asarray(g["L1_ffn_w_gate"]))
    wu8f = A(np.asarray(g["L1_ffn_w_up"]))
    wd8f = A(np.asarray(g["L1_ffn_w_down"]))
    gate_w = A(np.asarray(g["L0_ffn_gate_w"]))
    in_maps = []
    for c in range(8):
        beta, lane = c // 4, c % 4
        cs = slice(512 * lane, 512 * lane + 512)
        hs = slice(4 * lane, 4 * lane + 4)
        xT = x[beta].T
        m = dict(consts)
        m["ltmask"] = ltm
        m["x_fm"] = A(xT)
        m["x_res"] = A(xT[:, lane * 1024:lane * 1024 + 1024])
        m["wq0"] = A(np.asarray(g["L0_attn_Wq"])[cs, :].T)
        m["wk0"] = A(np.asarray(g["L0_attn_Wk"])[cs, :].T)
        m["wv0"] = A(np.asarray(g["L0_attn_Wv"])[cs, :].T)
        m["wb0"] = A(np.asarray(g["L0_attn_Wbeta"])[hs, :].T)
        m["wg1t"] = A(np.asarray(g["L0_attn_Wg1"]).T)
        m["wg2sl"] = A(np.asarray(g["L0_attn_Wg2"])[cs, :].T)
        gn_ = np.asarray(g["L0_attn_g_norm"])[cs].reshape(4, 128)
        m["gnf"] = A(np.concatenate([np.tile(gn_[h][:, None], (1, 128))
                                     for h in range(4)], axis=1))
        m["wo0f"] = wo0f
        m["gate_w"] = gate_w
        m["wg_f"] = wg_f
        m["wu_f"] = wu_f
        m["wd_f"] = wd_f
        m["swg_f"] = swg_f
        m["swu_f"] = swu_f
        m["swd_f"] = swd_f
        m["wq1"] = A(np.asarray(g["L1_attn_Wq"])[cs, :].T)
        m["wk1"] = A(np.asarray(g["L1_attn_Wk"])[cs, :].T)
        m["wv1"] = A(np.asarray(g["L1_attn_Wv"])[cs, :].T)
        m["wo1f"] = wo1f
        m["wg8f"] = wg8f
        m["wu8f"] = wu8f
        m["wd8f"] = wd8f
        selv = np.zeros((128, 4), np.float32)
        selv[:, lane] = 1.0
        m["sel"] = selv
        m["sg_t"] = A(np.asarray(g["L1_ffn_s_gate"], np.float32).reshape(64, 128).T)
        m["su_t"] = A(np.asarray(g["L1_ffn_s_up"], np.float32).reshape(64, 128).T)
        m["sd_t"] = A(np.asarray(g["L1_ffn_s_down"], np.float32).reshape(16, 128).T)
        in_maps.append(m)

    if not _NC_CACHE:
        _NC_CACHE.append(build_full())
    nc = _NC_CACHE[0]
    res = run_bass_kernel_spmd(nc, in_maps, core_ids=list(range(8)))
    out = np.concatenate([np.asarray(res.results[c]["out"]) for c in range(8)], axis=0)
    return out.reshape(2, SEQ, D)


# version 34
# speedup vs baseline: 1.5478x; 1.0302x over previous
import math
from contextlib import ExitStack

import numpy as np
import concourse.bass as bass
import concourse.mybir as mybir
from concourse.bass_utils import run_bass_kernel_spmd

F32 = mybir.dt.float32
BF16 = mybir.dt.bfloat16
I8 = mybir.dt.int8
I32 = mybir.dt.int32
AF = mybir.ActivationFunctionType
ALU = mybir.AluOpType
AX = mybir.AxisListType

D = 2048
NCH = D // 128
SEQ = 4096
TOK = 1024
HD = 128
H_PER = 4
EPS_DN = 1e-6
EPS_LN = 1e-5
INV_D = 1.0 / D


class Ctx:

    ENGINES = ("tensor", "vector", "scalar", "gpsimd", "sync")

    def __init__(self, nc, es):
        self.nc = nc
        self.es = es
        self.streams = {e: [] for e in self.ENGINES}
        self._h = {}
        self._n = {}
        for nm in ("ld", "mm", "vec", "scl", "gp", "sy", "st", "cc"):
            self._h[nm] = es.enter_context(nc.semaphore(nm))
            self._n[nm] = 0

    def dsem(self, nm):
        if nm not in self._h:
            self._h[nm] = self.es.enter_context(self.nc.semaphore(nm))
            self._n[nm] = 0
        return nm

    def val(self, nm):
        return self._n[nm]

    CHAIN = {"vector": "vec", "scalar": "scl", "gpsimd": "gp"}

    def op(self, eng, fn, inc=None, inck=1, chain=True):
        own = self.CHAIN.get(eng) if chain else None
        if own is not None and (inc is None or inc == own):
            h = self._h[own]
            prev = self._n[own]
            self.streams[eng].append(
                lambda e, fn=fn, h=h, p=prev: (e.wait_ge(h, p) if p else None,
                                               fn(e).then_inc(h, 1))[1])
            self._n[own] += 1
            return self._n[own]
        if inc is None:
            self.streams[eng].append(fn)
            return None
        h = self._h[inc]
        if own is not None:
            ho = self._h[own]
            prev = self._n[own]
            self.streams[eng].append(
                lambda e, fn=fn, h=h, k=inck, ho=ho, p=prev: (
                    e.wait_ge(ho, p) if p else None, fn(e).then_inc(h, k))[1])
        else:
            self.streams[eng].append(lambda e, fn=fn, h=h, k=inck: fn(e).then_inc(h, k))
        self._n[inc] += inck
        return self._n[inc]

    def wait(self, eng, sem, val=None):
        v = self._n[sem] if val is None else val
        if v <= 0:
            return
        h = self._h[sem]
        self.streams[eng].append(lambda e, h=h, v=v: e.wait_ge(h, v))

    def barrier(self, exclude=()):
        vals = {nm: v for nm, v in self._n.items() if v > 0 and nm not in exclude}
        for eng in self.ENGINES:
            for sem, v in vals.items():
                self.wait(eng, sem, v)

    def mm(self, o, l, r, start, stop, inc=None):
        return self.op("tensor",
                       lambda t, o=o, l=l, r=r, a=start, b=stop: t.matmul(o, l, r, start=a, stop=b),
                       inc=inc)

    def trans(self, o, i, ident, inc=None):
        return self.op("tensor", lambda t, o=o, i=i, d=ident: t.transpose(o, i, d), inc=inc)

    def tt(self, eng, o, a, b, alu, inc=None):
        return self.op(eng, lambda v, o=o, a=a, b=b, u=alu: v.tensor_tensor(o, a, b, u), inc=inc)

    def ts(self, eng, o, a, s1, s2, op0, op1, inc=None):
        return self.op(eng, lambda v, o=o, a=a, s1=s1, s2=s2, p0=op0, p1=op1:
                       v.tensor_scalar(o, a, s1, s2, p0, p1), inc=inc)

    def stt(self, eng, o, a, s, b, op0, op1, inc=None):
        return self.op(eng, lambda v, o=o, a=a, s=s, b=b, p0=op0, p1=op1:
                       v.scalar_tensor_tensor(o, a, s, b, p0, p1), inc=inc)

    def act(self, o, a, fname, bias=None, scale=None, accum=None, inc=None):
        def f(s, o=o, a=a, fn=fname, bi=bias, sc=scale, ac=accum):
            kw = {}
            if bi is not None:
                kw["bias"] = bi
            if sc is not None:
                kw["scale"] = sc
            if ac is not None:
                kw["accum_out"] = ac
            return s.activation(o, a, fn, **kw)
        return self.op("scalar", f, inc=inc)

    def copy(self, eng, o, a, inc=None):
        if eng == "scalar":
            return self.op(eng, lambda s, o=o, a=a: s.copy(o, a), inc=inc)
        return self.op(eng, lambda v, o=o, a=a: v.tensor_copy(o, a), inc=inc)

    def dma(self, eng, o, a, inc="ld", transpose=False):
        if inc not in self._h:
            self.dsem(inc)
        if transpose:
            return self.op(eng, lambda e, o=o, a=a: e.dma_start(out=o, in_=a, transpose=True),
                           inc=inc, inck=16, chain=False)
        return self.op(eng, lambda e, o=o, a=a: e.dma_start(out=o, in_=a), inc=inc, inck=16,
                       chain=False)

    def finalize(self):
        blk = self.es.enter_context(self.nc.Block())
        streams = self.streams

        @blk.tensor
        def _(t):
            for f in streams["tensor"]:
                f(t)

        @blk.vector
        def _(v):
            for f in streams["vector"]:
                f(v)

        @blk.scalar
        def _(s):
            for f in streams["scalar"]:
                f(s)

        @blk.gpsimd
        def _(g):
            for f in streams["gpsimd"]:
                f(g)

        @blk.sync
        def _(sy):
            for f in streams["sync"]:
                f(sy)


def sbt(ctx, es, name, shape, dtype):
    return es.enter_context(ctx.nc.sbuf_tensor(name, shape, dtype))


def pst(ctx, es, name, shape, dtype=F32):
    elems = 512 if dtype == F32 else 1024
    shape = [shape[0], max(shape[1], elems)]
    return es.enter_context(ctx.nc.psum_tensor(name, shape, dtype))


CONST_SPECS = {
    "ident_bf": ([128, 128], BF16),
    "ident_f32": ([128, 128], F32),
    "ones_row_bf": ([1, 128], BF16),
    "ones_row_f32": ([1, 128], F32),
    "ones_col_bf": ([128, 1], BF16),
    "mask_bd4n": ([128, 512], BF16),
    "mask_of4": ([128, 512], BF16),
    "mask_iU4": ([128, 512], BF16),
    "ones_t": ([128, 128], BF16),
    "ident4": ([128, 512], BF16),
    "diagmask": ([128, 4 * 512], BF16),
}


def host_consts():
    import ml_dtypes
    ii = np.arange(128)
    c = {}
    c["ident_bf"] = np.eye(128, dtype=np.float32)
    c["ident_f32"] = np.eye(128, dtype=np.float32)
    c["ones_row_bf"] = np.ones((1, 128), np.float32)
    c["ones_row_f32"] = np.ones((1, 128), np.float32)
    c["ones_col_bf"] = np.ones((128, 1), np.float32)
    c["mask_sL"] = (ii[:, None] > ii[None, :]).astype(np.float32)
    c["mask_iU"] = (ii[:, None] <= ii[None, :]).astype(np.float32)
    c["mask_bd"] = (((ii[:, None] // 32) == (ii[None, :] // 32))
                    & (ii[:, None] > ii[None, :])).astype(np.float32)
    c["mask_of"] = ((ii[:, None] // 32) > (ii[None, :] // 32)).astype(np.float32)
    c["mask_bd4n"] = np.tile(-c["mask_bd"], (1, 4))
    c["mask_of4"] = np.tile(c["mask_of"], (1, 4))
    c["mask_iU4"] = np.tile(c["mask_iU"], (1, 4))
    c["ones_t"] = np.ones((128, 128), np.float32)
    c["ident4"] = np.tile(np.eye(128, dtype=np.float32), (1, 4))
    dm = np.zeros((128, 4 * 512), np.float32)
    for j in range(4):
        dm[:, j * 512:(j + 1) * 512] = ((j * 128 + ii[:, None]) <= np.arange(512)[None, :])
    c["diagmask"] = dm
    out = {}
    for k, (shape, dt) in CONST_SPECS.items():
        a = np.ascontiguousarray(c[k].reshape(shape))
        out[k] = a.astype(ml_dtypes.bfloat16) if dt == BF16 else a.astype(np.float32)
    return out


def emit_load_consts(ctx, es, dr):
    C = {}
    for k, (shape, dt) in CONST_SPECS.items():
        C[k] = sbt(ctx, es, "c_" + k, shape, dt)
        ctx.dma("gpsimd", C[k][:], dr[k], inc="ldc")
    return C


def emit_ln_fm(ctx, es, C, r_fm, T, tag, sink):
    with ExitStack() as ph:
        sq = sbt(ctx, ph, f"lnsq{tag}", [128, 2 * 512], BF16)
        rows = sbt(ctx, ph, f"lnrw{tag}", [1, 6 * 512], F32)
        p_sum = pst(ctx, ph, f"lnps{tag}", [1, 512])
        p_sq = pst(ctx, ph, f"lnpq{tag}", [1, 512])
        p_a = pst(ctx, ph, f"lnpa{tag}", [128, 512])
        p_b = pst(ctx, ph, f"lnpb{tag}", [128, 512])

        def R(i):
            return rows[:, i * 512:(i + 1) * 512]

        applied = 0
        for g in range(T // 512):
            rsl = [r_fm[:, k * T + g * 512:k * T + g * 512 + 512] for k in range(NCH)]
            ctx.wait("tensor", "vec", applied)
            for k in range(NCH):
                ctx.mm(p_sum[:], C["ones_col_bf"][:], rsl[k], k == 0, k == NCH - 1, inc="mm")
            mmv = {}
            for k in range(NCH):
                if k >= 2:
                    ctx.wait("scalar", "mm", mmv[k - 2])
                sqs = sq[:, (k % 2) * 512:(k % 2) * 512 + 512]
                sv = ctx.act(sqs, rsl[k], AF.Square, inc="scl")
                ctx.wait("tensor", "scl", sv)
                mmv[k] = ctx.mm(p_sq[:], C["ones_col_bf"][:], sqs, k == 0, k == NCH - 1, inc="mm")
            ctx.wait("vector", "mm")
            ctx.ts("vector", R(0), p_sum[:], INV_D, 1.0, ALU.mult, ALU.mult)
            ctx.ts("vector", R(1), p_sq[:], INV_D, 1.0, ALU.mult, ALU.mult)
            ctx.tt("vector", R(4), R(0), R(0), ALU.mult)
            ctx.tt("vector", R(1), R(1), R(4), ALU.subtract)
            vv = ctx.ts("vector", R(1), R(1), 1.0, EPS_LN, ALU.mult, ALU.add, inc="vec")
            ctx.wait("scalar", "vec", vv)
            sv = ctx.act(R(4), R(1), AF.Sqrt, inc="scl")
            ctx.wait("vector", "scl", sv)
            ctx.op("vector", lambda v, o=R(2), a=R(4): v.reciprocal(o, a))
            ctx.tt("vector", R(3), R(0), R(2), ALU.mult)
            vv = ctx.ts("vector", R(3), R(3), -1.0, 1.0, ALU.mult, ALU.mult, inc="vec")
            ctx.wait("tensor", "vec", vv)
            ctx.mm(p_a[:], C["ones_row_f32"][:], R(2), True, True)
            bcv = ctx.mm(p_b[:], C["ones_row_f32"][:], R(3), True, True, inc="mm")
            ctx.wait("vector", "mm", bcv)
            for k in range(NCH):
                sink(g, k, rsl[k], p_a[:], p_b[:])
            applied = ctx.val("vec")


def ln_sink_simple(ctx, out_fm, T):
    def sink(g, k, rsl, a, b):
        osl = out_fm[:, k * T + g * 512:k * T + g * 512 + 512]
        ctx.tt("vector", osl, rsl, a, ALU.mult)
        ctx.tt("vector", osl, osl, b, ALU.add, inc="vec")
    return sink


def emit_l1ffn(ctx, es, C, x_fm, wg8, wu8, wd8, sg_t, su_t, sd_t, out_dram):
    with ExitStack() as ph:
        r_fm = sbt(ctx, ph, "f_r", [128, NCH * TOK], BF16)
        inner = ExitStack()
        fbuf = sbt(ctx, inner, "f_F", [128, 32 * 512], BF16)
        wgt = sbt(ctx, inner, "f_wg", [128, 2 * 16 * 512], BF16)
        wut = sbt(ctx, inner, "f_wu", [128, 2 * 16 * 512], BF16)
        wdt = sbt(ctx, inner, "f_wd", [128, 2 * 32 * 256], BF16)
        gtmp = sbt(ctx, inner, "f_gt", [128, 2 * 512], BF16)
        p_g = [pst(ctx, inner, f"f_pg{i}", [128, 512]) for i in range(2)]
        p_u = [pst(ctx, inner, f"f_pu{i}", [128, 512]) for i in range(2)]
        p_d = [pst(ctx, inner, f"f_pd{i}", [128, 512]) for i in range(2)]

        def colslice_dma(dst, src, c0, ncol, row0, nrow, sem, slot=0):
            sz = (nrow // 128) * ncol
            o = dst[:, slot * sz:(slot + 1) * sz].rearrange("p (c n) -> p c n", n=ncol)
            i = src[row0:row0 + nrow, c0:c0 + ncol].rearrange("(c p) n -> p c n", p=128)
            return ctx.dma("gpsimd", o, i, inc=sem)

        for th in range(2):
            tok0 = th * 512
            xsl = [x_fm[:, k * TOK + tok0:k * TOK + tok0 + 512] for k in range(NCH)]
            for hh in range(2):
                ctx.wait("vector", "mm")
                prev_mm = ctx.val("mm")
                fdv = {0: 0, 1: 0}
                sdv = {0: 0, 1: 0}
                wmm = {0: 0, 1: 0}
                wld = {}
                for hs in range(8):
                    ws = hs % 2
                    if hs == 0:
                        wld[0] = (colslice_dma(wgt, wg8, hh * 4096, 512, 0, D, "s_wg0", 0),
                                  colslice_dma(wut, wu8, hh * 4096, 512, 0, D, "s_wu0", 0))
                    if hs + 1 < 8:
                        wn = (hs + 1) % 2
                        ctx.wait("gpsimd", "mm", wmm[wn])
                        wld[hs + 1] = (
                            colslice_dma(wgt, wg8, hh * 4096 + (hs + 1) * 512, 512, 0, D,
                                         f"s_wg{wn}", wn),
                            colslice_dma(wut, wu8, hh * 4096 + (hs + 1) * 512, 512, 0, D,
                                         f"s_wu{wn}", wn))
                    ctx.wait("tensor", f"s_wg{ws}", wld[hs][0])
                    ctx.wait("tensor", f"s_wu{ws}", wld[hs][1])
                    for hc in range(4):
                        sl_ = hc % 2
                        hci = hh * 32 + hs * 4 + hc
                        fi = hs * 4 + hc
                        fsl = fbuf[:, fi * 512:fi * 512 + 512]
                        gsl = gtmp[:, sl_ * 512:sl_ * 512 + 512]
                        ctx.wait("tensor", "vec", fdv[sl_])
                        ctx.wait("tensor", "scl", sdv[sl_])
                        wbase = ws * 8192
                        mg = mu = None
                        for k in range(NCH):
                            mg = ctx.mm(p_g[sl_][:],
                                        wgt[:, wbase + (k * 4 + hc) * 128:wbase + (k * 4 + hc) * 128 + 128],
                                        xsl[k], k == 0, k == NCH - 1,
                                        inc="mm" if k == NCH - 1 else None)
                        for k in range(NCH):
                            mu = ctx.mm(p_u[sl_][:],
                                        wut[:, wbase + (k * 4 + hc) * 128:wbase + (k * 4 + hc) * 128 + 128],
                                        xsl[k], k == 0, k == NCH - 1,
                                        inc="mm" if k == NCH - 1 else None)
                        prev_mm = mu
                        wmm[ws] = mu
                        ctx.wait("vector", "mm", mg)
                        gv = ctx.ts("vector", gsl, p_g[sl_][:], sg_t[:, hci:hci + 1], 1.0,
                                    ALU.mult, ALU.mult, inc="vec")
                        ctx.wait("scalar", "vec", gv)
                        sv = ctx.act(gsl, gsl, AF.Silu, inc="scl")
                        sdv[sl_] = sv
                        ctx.wait("vector", "mm", mu)
                        ctx.ts("vector", fsl, p_u[sl_][:], su_t[:, hci:hci + 1], 1.0,
                               ALU.mult, ALU.mult)
                        ctx.wait("vector", "scl", sv)
                        fdv[sl_] = ctx.tt("vector", fsl, fsl, gsl, ALU.mult, inc="vec")
                ctx.wait("vector", "scl")
                fdone = ctx.val("vec")
                prev_dmm = ctx.val("mm")
                ddv = {0: 0, 1: 0}
                dmm = {0: 0, 1: 0}
                dld = {}
                for ds in range(8):
                    if ds == 0:
                        dld[0] = colslice_dma(wdt, wd8, 0, 256, hh * 4096, 4096, "s_wd0", 0)
                    if ds + 1 < 8:
                        dn_ = (ds + 1) % 2
                        ctx.wait("gpsimd", "mm", dmm[dn_])
                        dld[ds + 1] = colslice_dma(wdt, wd8, (ds + 1) * 256, 256,
                                                   hh * 4096, 4096, f"s_wd{dn_}", dn_)
                    ctx.wait("tensor", f"s_wd{ds % 2}", dld[ds])
                    ctx.wait("tensor", "vec", fdone)
                    for dc in range(2):
                        sl_ = dc % 2
                        dci = ds * 2 + dc
                        pd = p_d[sl_]
                        gsl = gtmp[:, sl_ * 512:sl_ * 512 + 512]
                        ctx.wait("tensor", "vec", ddv[sl_])
                        dbase = (ds % 2) * 8192
                        mv = None
                        for hc in range(32):
                            mv = ctx.mm(pd[:],
                                        wdt[:, dbase + (hc * 2 + dc) * 128:dbase + (hc * 2 + dc) * 128 + 128],
                                        fbuf[:, hc * 512:hc * 512 + 512], hc == 0, hc == 31,
                                        inc="mm" if hc == 31 else None)
                        prev_dmm = mv
                        dmm[ds % 2] = mv
                        ctx.wait("vector", "mm", mv)
                        rsl = r_fm[:, dci * TOK + tok0:dci * TOK + tok0 + 512]
                        ctx.ts("vector", gsl, pd[:], sd_t[:, dci:dci + 1], 1.0,
                               ALU.mult, ALU.mult)
                        if hh == 0:
                            ddv[sl_] = ctx.tt("vector", rsl, gsl, xsl[dci], ALU.add, inc="vec")
                        else:
                            ddv[sl_] = ctx.tt("vector", rsl, rsl, gsl, ALU.add, inc="vec")
        ctx.barrier()
        inner.close()
        y32 = sbt(ctx, ph, "f_y32", [128, NCH * 512], F32)
        ytm = sbt(ctx, ph, "f_ytm", [128, 2048], F32)
        p_t = [pst(ctx, ph, f"f_pt{i}", [128, 512]) for i in range(2)]

        def sink(g, k, rsl, a, b):
            osl = y32[:, k * 512:k * 512 + 512]
            ctx.tt("vector", osl, rsl, a, ALU.mult)
            ctx.tt("vector", osl, osl, b, ALU.add, inc="vec")
            if k == NCH - 1:
                _store_group(ctx, C, y32, ytm, p_t, out_dram, g)

        emit_ln_fm(ctx, ph, C, r_fm, TOK, "f", sink)
        ctx.wait("sync", "st")


def _store_group(ctx, C, y32, ytm, p_t, out_dram, g):
    yv = ctx.val("vec")
    ctx.wait("tensor", "vec", yv)
    evv = {}
    for tc in range(4):
        stv = ctx.val("st")
        for k in range(NCH):
            i = tc * NCH + k
            pt = p_t[i % 2]
            if i >= 2:
                ctx.wait("tensor", "vec", evv[i - 2])
            mv = ctx.trans(pt[:, 0:128], y32[:, k * 512 + tc * 128:k * 512 + tc * 128 + 128],
                           C["ident_f32"][:], inc="mm")
            ctx.wait("vector", "mm", mv)
            if k == 0 and stv > 0:
                ctx.wait("vector", "st", stv)
            evv[i] = ctx.copy("vector", ytm[:, k * 128:k * 128 + 128], pt[:, 0:128], inc="vec")
        ev = ctx.val("vec")
        ctx.wait("sync", "vec", ev)
        row0 = g * 512 + tc * 128
        ctx.dma("sync", out_dram[row0:row0 + 128, :], ytm[:], inc="st")


def kernel(**inputs):
    raise NotImplementedError("full build assembled incrementally; see scratch drivers")


def emit_moe(ctx, es, C, x_fm, gate_w_s, ltmask, wg_dr, wu_dr, wd_dr,
             swg_dr, swu_dr, swd_dr, dbg_comb=None, r_fm=None):
    if r_fm is None:
        r_fm = sbt(ctx, es, "m_r", [128, NCH * TOK], BF16)
    with ExitStack() as ph:
        comb_t = sbt(ctx, ph, "m_combt", [8, 8 * 128], F32)
        comb_r = sbt(ctx, ph, "m_combr", [1, 8 * 8 * 128], BF16)
        inner = ExitStack()
        with ExitStack() as rp:
            p_r = pst(ctx, rp, "m_pr", [128, 512])
            p_tr = pst(ctx, rp, "m_ptr", [128, 512])
            sc = sbt(ctx, rp, "m_sc", [128, 8], F32)
            rk = sbt(ctx, rp, "m_rk", [128, 8], F32)
            t8 = sbt(ctx, rp, "m_t8", [128, 8], F32)
            mrow = sbt(ctx, rp, "m_mrow", [128, 2], F32)
            for tc in range(8):
                xch = [x_fm[:, k * TOK + tc * 128:k * TOK + tc * 128 + 128] for k in range(NCH)]
                ctx.wait("tensor", "vec")
                ctx.wait("tensor", "scl")
                mv = None
                for k in range(NCH):
                    mv = ctx.mm(p_r[:, 0:8], xch[k], gate_w_s[:, k * 8:k * 8 + 8],
                                k == 0, k == NCH - 1, inc="mm" if k == NCH - 1 else None)
                ctx.wait("vector", "mm", mv)
                ctx.op("vector", lambda v, o=mrow[:, 0:1], a=p_r[:, 0:8]:
                       v.reduce_max(o, a, axis=AX.X))
                ctx.ts("vector", mrow[:, 1:2], mrow[:, 0:1], -1.0, 1.0, ALU.mult, ALU.mult,
                       inc="vec")
                ctx.wait("scalar", "vec")
                sv = ctx.act(sc[:], p_r[:, 0:8], AF.Exp, bias=mrow[:, 1:2], inc="scl")
                ctx.wait("vector", "scl", sv)
                ctx.op("vector", lambda v, o=mrow[:, 0:1], a=sc[:]:
                       v.reduce_sum(o, a, axis=AX.X))
                ctx.op("vector", lambda v, o=mrow[:, 1:2], a=mrow[:, 0:1]: v.reciprocal(o, a))
                ctx.ts("vector", sc[:], sc[:], mrow[:, 1:2], 1.0, ALU.mult, ALU.mult)
                ctx.ts("vector", rk[:], sc[:], sc[:, 0:1], 0.0, ALU.is_lt, ALU.add)
                for j in range(1, 8):
                    ctx.ts("vector", t8[:], sc[:], sc[:, j:j + 1], 1.0, ALU.is_lt, ALU.mult)
                    ctx.tt("vector", rk[:], rk[:], t8[:], ALU.add)
                for j in range(7):
                    ctx.ts("vector", t8[:], sc[:], sc[:, j:j + 1], 1.0, ALU.is_equal, ALU.mult)
                    ctx.tt("vector", t8[:], t8[:], ltmask[:, j * 8:j * 8 + 8], ALU.mult)
                    ctx.tt("vector", rk[:], rk[:], t8[:], ALU.add)
                ctx.ts("vector", rk[:], rk[:], 2.0, 1.0, ALU.is_lt, ALU.mult)
                cv = ctx.tt("vector", sc[:], sc[:], rk[:], ALU.mult, inc="vec")
                ctx.wait("tensor", "vec", cv)
                tv = ctx.trans(p_tr[0:8, 0:128], sc[:], C["ident_f32"][:], inc="mm")
                ctx.wait("vector", "mm", tv)
                ctx.copy("vector", comb_t[:, tc * 128:tc * 128 + 128], p_tr[0:8, 0:128],
                         inc="vec")
            ctx.wait("gpsimd", "vec")
            for e_ in range(8):
                ctx.dma("gpsimd", comb_r[:, e_ * 1024:(e_ + 1) * 1024],
                        comb_t[e_:e_ + 1, :], inc="m_cr")
            ctx.barrier()
            ctx.wait("tensor", "m_cr")
            if dbg_comb is not None:
                ctx.dma("gpsimd", dbg_comb, comb_r[:], inc="m_dbg")
                ctx.wait("gpsimd", "m_dbg")
        wgt = sbt(ctx, inner, "m_wg", [128, 2 * 16 * 512], BF16)
        wut = sbt(ctx, inner, "m_wu", [128, 2 * 16 * 512], BF16)
        wdt = sbt(ctx, inner, "m_wd", [128, 2 * 11 * 512], BF16)
        gbuf = sbt(ctx, inner, "m_G", [128, 11 * TOK], BF16)
        gtmp = sbt(ctx, inner, "m_gt", [128, 2 * 512], BF16)
        p_g = [pst(ctx, inner, f"m_pg{i}", [128, 512]) for i in range(2)]
        p_u = [pst(ctx, inner, f"m_pu{i}", [128, 512]) for i in range(2)]
        p_cb = [pst(ctx, inner, f"m_pcb{i}", [128, 512]) for i in range(2)]
        p_d = [pst(ctx, inner, f"m_pd{i}", [128, 512]) for i in range(2)]

        def colslice(dst, slot, slotsz, src, r0, nr, c0, ncol, sem):
            o = dst[:, slot * slotsz:slot * slotsz + (nr // 128) * ncol]
            o = o.rearrange("p (c n) -> p c n", n=ncol)
            i = src[r0:r0 + nr, c0:c0 + ncol].rearrange("(c p) n -> p c n", p=128)
            return ctx.dma("gpsimd", o, i, inc=sem)

        moe_wld = {}
        moe_wmm = {0: 0, 1: 0}
        for e in range(10):
            routed = e < 8
            Hh = 1408
            nhc = Hh // 128
            supers = [(0, 512), (512, 512), (1024, 384)]
            if routed:
                g_src = wg_dr[e * 2048:(e + 1) * 2048, :]
                u_src = wu_dr[e * 2048:(e + 1) * 2048, :]
                d_src = wd_dr[e * 1408:(e + 1) * 1408, :]
            else:
                sh = (e - 8) * 1408
                g_src = swg_dr[:, sh:sh + 1408]
                u_src = swu_dr[:, sh:sh + 1408]
                d_src = swd_dr[sh:sh + 1408, :]
            if routed:
                ctx.wait("tensor", "vec")
                for half in range(2):
                    ctx.mm(p_cb[half][:], C["ones_row_bf"][:],
                           comb_r[:, e * 1024 + half * 512:e * 1024 + half * 512 + 512],
                           True, True, inc="mm")
            gdv = {0: ctx.val("vec"), 1: ctx.val("vec")}
            sdv = {0: ctx.val("scl"), 1: ctx.val("scl")}
            for si, (h0, w) in enumerate(supers):
                gi0 = e * 3 + si
                slot = gi0 % 2
                if gi0 == 0:
                    moe_wld[0] = (
                        colslice(wgt, 0, 16 * 512, g_src, 0, 2048, h0, w, "m_swg0"),
                        colslice(wut, 0, 16 * 512, u_src, 0, 2048, h0, w, "m_swu0"))
                nsi = gi0 + 1
                if nsi < 30:
                    ne, nsl = nsi // 3, nsi % 3
                    nh0, nw = supers[nsl]
                    if ne < 8:
                        ng = wg_dr[ne * 2048:(ne + 1) * 2048, :]
                        nu = wu_dr[ne * 2048:(ne + 1) * 2048, :]
                    else:
                        nsh = (ne - 8) * 1408
                        ng = swg_dr[:, nsh:nsh + 1408]
                        nu = swu_dr[:, nsh:nsh + 1408]
                    ctx.wait("gpsimd", "mm", moe_wmm[nsi % 2])
                    moe_wld[nsi] = (
                        colslice(wgt, nsi % 2, 16 * 512, ng, 0, 2048, nh0, nw,
                                 f"m_swg{nsi % 2}"),
                        colslice(wut, nsi % 2, 16 * 512, nu, 0, 2048, nh0, nw,
                                 f"m_swu{nsi % 2}"))
                ctx.wait("tensor", f"m_swg{slot}", moe_wld[gi0][0])
                ctx.wait("tensor", f"m_swu{slot}", moe_wld[gi0][1])
                for hc in range(w // 128):
                    gi = h0 // 128 + hc
                    for half in range(2):
                        sl_ = (hc * 2 + half) % 2
                        xsl = [x_fm[:, k * TOK + half * 512:k * TOK + half * 512 + 512]
                               for k in range(NCH)]
                        gsl = gbuf[:, gi * TOK + half * 512:gi * TOK + half * 512 + 512]
                        gts = gtmp[:, sl_ * 512:sl_ * 512 + 512]
                        ctx.wait("tensor", "vec", gdv[sl_])
                        ctx.wait("tensor", "scl", sdv[sl_])
                        mg = mu = None
                        for k in range(NCH):
                            off = slot * 16 * 512 + (k * (w // 128) + hc) * 128
                            mg = ctx.mm(p_g[sl_][:], wgt[:, off:off + 128], xsl[k],
                                        k == 0, k == NCH - 1,
                                        inc="mm" if k == NCH - 1 else None)
                        for k in range(NCH):
                            off = slot * 16 * 512 + (k * (w // 128) + hc) * 128
                            mu = ctx.mm(p_u[sl_][:], wut[:, off:off + 128], xsl[k],
                                        k == 0, k == NCH - 1,
                                        inc="mm" if k == NCH - 1 else None)
                        moe_wmm[slot] = mu
                        ctx.wait("scalar", "mm", mg)
                        sv = ctx.act(gts, p_g[sl_][:], AF.Silu, inc="scl")
                        sdv[sl_] = sv
                        ctx.wait("vector", "mm", mu)
                        ctx.wait("vector", "scl", sv)
                        if routed:
                            ctx.tt("vector", gsl, gts, p_u[sl_][:], ALU.mult)
                            gdv[sl_] = ctx.tt("vector", gsl, gsl, p_cb[half][:], ALU.mult,
                                              inc="vec")
                        else:
                            gdv[sl_] = ctx.tt("vector", gsl, gts, p_u[sl_][:], ALU.mult,
                                              inc="vec")
            gdone = ctx.val("vec")
            dn_slot = {0: ctx.val("mm"), 1: ctx.val("mm")}
            pdv = {0: ctx.val("vec"), 1: ctx.val("vec")}
            dslotsz = 11 * 512
            for ds in range(4):
                slot = ds % 2
                ctx.wait("gpsimd", "mm", dn_slot[slot])
                d_ld = colslice(wdt, slot, dslotsz, d_src, 0, Hh, ds * 512, 512, f"m_swd{slot}")
                ctx.wait("tensor", f"m_swd{slot}", d_ld)
                ctx.wait("tensor", "vec", gdone)
                for dc in range(4):
                    dci = ds * 4 + dc
                    for half in range(2):
                        sl_ = (dc * 2 + half) % 2
                        pd = p_d[sl_]
                        ctx.wait("tensor", "vec", pdv[sl_])
                        mv = None
                        for hc in range(nhc):
                            off = slot * dslotsz + hc * 512 + dc * 128
                            mv = ctx.mm(pd[:], wdt[:, off:off + 128],
                                        gbuf[:, hc * TOK + half * 512:hc * TOK + half * 512 + 512],
                                        hc == 0, hc == nhc - 1,
                                        inc="mm" if hc == nhc - 1 else None)
                        dn_slot[slot] = mv
                        ctx.wait("vector", "mm", mv)
                        rsl = r_fm[:, dci * TOK + half * 512:dci * TOK + half * 512 + 512]
                        if e == 0:
                            pdv[sl_] = ctx.tt("vector", rsl, pd[:],
                                              x_fm[:, dci * TOK + half * 512:dci * TOK + half * 512 + 512],
                                              ALU.add, inc="vec")
                        else:
                            pdv[sl_] = ctx.tt("vector", rsl, rsl, pd[:], ALU.add, inc="vec")
        ctx.barrier()
        inner.close()
        emit_ln_fm(ctx, ph, C, r_fm, TOK, "m", ln_sink_simple(ctx, r_fm, TOK))
        ctx.barrier()
    return r_fm


DN_GROUPS = 8
DN_CHUNKS = 4
DN_HORNER = 3
MHA_STAGE = 4


def emit_mha(ctx, es, C, xsrc, wq_dr, wk_dr, wv_dr, cc3_dr, pre_x=None, half_done=None):
    RT2 = 1.0 / math.sqrt(128.0)
    with ExitStack() as ph:
        wq_s = sbt(ctx, ph, "a_wq", [128, 16 * 512], BF16)
        wk_s = sbt(ctx, ph, "a_wk", [128, 16 * 512], BF16)
        wv_s = sbt(ctx, ph, "a_wv", [128, 16 * 512], BF16)
        q_fm = sbt(ctx, ph, "a_qfm", [128, 4 * 512], BF16)
        k_fm = sbt(ctx, ph, "a_kfm", [128, 4 * SEQ], BF16)
        v_tm = sbt(ctx, ph, "a_vtm", [128, 4 * SEQ], BF16)
        xg = sbt(ctx, ph, "a_xg", [128, 2 * 16 * 512], BF16)
        cosx = sbt(ctx, ph, "a_cos", [128, 32 * 128], BF16)
        sinx = sbt(ctx, ph, "a_sin", [128, 32 * 128], BF16)
        qrp = sbt(ctx, ph, "a_qrp", [128, 512], BF16)
        krp = sbt(ctx, ph, "a_krp", [128, 512], BF16)
        tmpab = sbt(ctx, ph, "a_tab", [128, 256], BF16)
        sqb = sbt(ctx, ph, "a_sqb", [128, 512], BF16)
        ptb = sbt(ctx, ph, "a_ptb", [128, 4 * 512], BF16)
        exps = sbt(ctx, ph, "a_exps", [128, 4 * 512], F32)
        obuf = sbt(ctx, ph, "a_obuf", [128, 512], BF16)
        mb_sb = sbt(ctx, ph, "a_mbsb", [128, 512], BF16)
        rows = sbt(ctx, ph, "a_rows", [1, 3 * 512], F32)
        qnr = sbt(ctx, ph, "a_qnr", [1, 4 * 512], BF16)
        runmax = sbt(ctx, ph, "a_rmx", [1, 8], F32)
        lrow = sbt(ctx, ph, "a_lrow", [1, 512], BF16)
        p_q = pst(ctx, ph, "a_pq", [128, 512])
        p_k = pst(ctx, ph, "a_pk", [128, 512])
        p_v = pst(ctx, ph, "a_pv", [128, 512])
        p_st = pst(ctx, ph, "a_pst", [128, 512])
        p_mb = pst(ctx, ph, "a_pmb", [128, 512])
        p_o = pst(ctx, ph, "a_po", [128, 512])
        p_l = pst(ctx, ph, "a_pl", [1, 512])

        def R(i):
            return rows[:, i * 512:(i + 1) * 512]

        ctx.dma("gpsimd", wq_s[:].rearrange("p (c n) -> p c n", n=512),
                wq_dr.rearrange("(c p) n -> p c n", p=128), inc="a_w")
        ctx.dma("gpsimd", wk_s[:].rearrange("p (c n) -> p c n", n=512),
                wk_dr.rearrange("(c p) n -> p c n", p=128), inc="a_w")
        wv_v = ctx.dma("gpsimd", wv_s[:].rearrange("p (c n) -> p c n", n=512),
                       wv_dr.rearrange("(c p) n -> p c n", p=128), inc="a_w")
        with ExitStack() as tp:
            itr = sbt(ctx, tp, "a_itr", [1, 128], F32)
            thr = sbt(ctx, tp, "a_thr", [1, 64], F32)
            ang = sbt(ctx, tp, "a_ang", [128, 64], F32)
            red = sbt(ctx, tp, "a_red", [128, 64], F32)
            ki = sbt(ctx, tp, "a_ki", [128, 64], I32)

            gv = ctx.op("gpsimd", lambda g, o=thr[:]: g.iota(
                o, pattern=[[1, 64]], base=0, channel_multiplier=0,
                allow_small_or_imprecise_dtypes=True), inc="gp")
            ctx.wait("scalar", "gp", gv)
            sv = ctx.act(thr[:], thr[:], AF.Exp, scale=float(-math.log(10000.0) / 64.0),
                         inc="scl")
            ctx.wait("tensor", "gp", gv)
            ctx.wait("tensor", "scl", sv)
            ang_mv = 0
            for tc in range(32):
                if ang_mv:
                    ctx.wait("gpsimd", "mm", ang_mv)
                gv = ctx.op("gpsimd", lambda g, o=itr[:], b=tc * 128: g.iota(
                    o, pattern=[[1, 128]], base=b, channel_multiplier=0,
                    allow_small_or_imprecise_dtypes=True), inc="gp")
                ctx.wait("tensor", "gp", gv)
                mv = ctx.mm(p_st[:, 0:64], itr[:], thr[:],
                            True, True, inc="mm")
                ang_mv = mv
                ctx.wait("vector", "mm", mv)
                for half, shift in ((0, 0.0), (1, 0.5 * math.pi)):
                    ctx.ts("vector", ang[:], p_st[:, 0:64], 1.0, shift, ALU.mult, ALU.add)
                    ctx.ts("vector", red[:], ang[:], float(1.0 / (2 * math.pi)), 1.0,
                           ALU.mult, ALU.mult)
                    ctx.copy("vector", ki[:], red[:])
                    ctx.copy("vector", red[:], ki[:])
                    ctx.stt("vector", ang[:], red[:], float(-2 * math.pi), ang[:],
                            ALU.mult, ALU.add)
                    ctx.ts("vector", red[:], ang[:], float(math.pi), 1.0, ALU.is_gt, ALU.mult)
                    vv = ctx.stt("vector", ang[:], red[:], float(-2 * math.pi), ang[:],
                                 ALU.mult, ALU.add, inc="vec")
                    ctx.wait("scalar", "vec", vv)
                    dst = sinx if half == 0 else cosx
                    s0 = ctx.act(dst[:, tc * 128:tc * 128 + 127:2], ang[:], AF.Sin,
                                 inc="scl")
                    s1 = ctx.act(dst[:, tc * 128 + 1:tc * 128 + 128:2], ang[:], AF.Sin,
                                 inc="scl")
                    ctx.wait("vector", "scl", s1)
                ctx.wait("tensor", "vec")
        ctx.wait("tensor", "a_w", wv_v)
        ctx.wait("vector", "a_w", wv_v)

        ctx.op("vector", lambda v, o=runmax[:]: v.memset(o, 0.0), inc="vec")

        xg_mm = {0: 0, 1: 0}
        for g in range(8 if MHA_STAGE >= 2 else 0):
            slot = g % 2
            if g == 0 and pre_x is not None:
                pre_x()
            ctx.wait("gpsimd", "mm", xg_mm[slot])
            xv = ctx.dma("gpsimd",
                         xg[:, slot * 8192:slot * 8192 + 8192].rearrange(
                             "p (c n) -> p c n", n=512),
                         xsrc(g).rearrange("(c p) n -> p c n", p=128), inc=f"a_x{slot}")
            ctx.wait("tensor", f"a_x{slot}", xv)
            for tc4 in range(4):
                tc = g * 4 + tc4
                xsl = [xg[:, slot * 8192 + k * 512 + tc4 * 128:
                          slot * 8192 + k * 512 + tc4 * 128 + 128] for k in range(NCH)]
                ctx.wait("tensor", "vec")
                mq = mk = mvv = None
                for k in range(NCH):
                    mq = ctx.mm(p_q[:], xsl[k], wq_s[:, k * 512:k * 512 + 512],
                                k == 0, k == NCH - 1, inc="mm" if k == NCH - 1 else None)
                for k in range(NCH):
                    mk = ctx.mm(p_k[:], xsl[k], wk_s[:, k * 512:k * 512 + 512],
                                k == 0, k == NCH - 1, inc="mm" if k == NCH - 1 else None)
                for k in range(NCH):
                    mvv = ctx.mm(p_v[:], xsl[k], wv_s[:, k * 512:k * 512 + 512],
                                 k == 0, k == NCH - 1, inc="mm" if k == NCH - 1 else None)
                xg_mm[slot] = mvv
                ctx.wait("vector", "mm", mvv)
                ctx.copy("vector", v_tm[:, tc * 512:tc * 512 + 512], p_v[:], inc="vec")
                if MHA_STAGE < 2.3:
                    continue
                ctx.wait("vector", "mm")
                cose = cosx[:, tc * 128:tc * 128 + 128]
                sine = sinx[:, tc * 128:tc * 128 + 128]
                for src_ps, dstbuf in ((p_q, qrp), (p_k, krp)):
                    for h in range(4):
                        psl = src_ps[:, h * 128:h * 128 + 128]
                        dsl = dstbuf[:, h * 128:h * 128 + 128]
                        ctx.tt("vector", tmpab[:, 0:128], psl, cose, ALU.mult)
                        ctx.tt("vector", tmpab[:, 128:256], psl, sine, ALU.mult)
                        ctx.tt("vector", dstbuf[:, h * 128:h * 128 + 127:2],
                               tmpab[:, 0:127:2], tmpab[:, 129:256:2], ALU.subtract)
                        ctx.tt("vector", dstbuf[:, h * 128 + 1:h * 128 + 128:2],
                               tmpab[:, 1:128:2], tmpab[:, 128:255:2], ALU.add,
                               inc="vec")
                if MHA_STAGE < 2.6:
                    continue
                ctx.wait("tensor", "vec")
                ptr_v = [p_q[:].bitcast(BF16), p_k[:].bitcast(BF16)]
                evv = {}
                for i, (srcbuf, dstfm) in enumerate(((qrp, q_fm), (krp, k_fm))):
                    for h in range(4):
                        j = i * 4 + h
                        if j >= 2:
                            ctx.wait("tensor", "vec", evv[j - 2])
                        tv = ctx.trans(ptr_v[j % 2][:, 0:128],
                                       srcbuf[:, h * 128:h * 128 + 128],
                                       C["ident_bf"][:], inc="mm")
                        ctx.wait("vector", "mm", tv)
                        dcol = (h * 512 + tc4 * 128) if i == 0 else (h * SEQ + tc * 128)
                        evv[j] = ctx.copy(
                            "vector",
                            dstfm[:, dcol:dcol + 128],
                            ptr_v[j % 2][:, 0:128], inc="vec")
            if MHA_STAGE < 3:
                continue
            for h in range(4):
                qw = h * 512
                kw = h * SEQ + g * 512
                kv = ctx.tt("vector", sqb[:], k_fm[:, kw:kw + 512], k_fm[:, kw:kw + 512],
                            ALU.mult, inc="vec")
                ctx.wait("tensor", "vec", kv)
                mv = ctx.mm(p_l[:], C["ones_col_bf"][:], sqb[:], True, True, inc="mm")
                ctx.wait("vector", "mm", mv)
                ctx.op("vector", lambda v, o=R(0)[:, 0:1], a=p_l[:, 0:512]:
                       v.reduce_max(o, a, axis=AX.X))
                ctx.tt("vector", runmax[:, h:h + 1], runmax[:, h:h + 1], R(0)[:, 0:1],
                       ALU.max, inc="vec")
                qv = ctx.tt("vector", sqb[:], q_fm[:, qw:qw + 512], q_fm[:, qw:qw + 512],
                            ALU.mult, inc="vec")
                ctx.wait("tensor", "vec", qv)
                mv = ctx.mm(p_l[:], C["ones_col_bf"][:], sqb[:], True, True, inc="mm")
                ctx.wait("scalar", "mm", mv)
                sv = ctx.act(R(1), p_l[:, 0:512], AF.Ln, inc="scl")
                sv = ctx.act(qnr[:, h * 512:h * 512 + 512], R(1), AF.Exp, scale=0.5,
                             inc="scl")
                ctx.wait("vector", "scl", sv)
            if MHA_STAGE < 4:
                continue
            sv0 = ctx.val("vec")
            ctx.wait("scalar", "vec", sv0)
            sv = ctx.act(runmax[:, 4:8], runmax[:, 0:4], AF.Ln, inc="scl")
            sv = ctx.act(runmax[:, 4:8], runmax[:, 4:8], AF.Exp, scale=0.5, inc="scl")
            ctx.wait("vector", "scl", sv)
            for h in range(4):
                qw = h * 512
                mrv = ctx.ts("vector", R(2), qnr[:, h * 512:h * 512 + 512],
                             runmax[:, 4 + h:5 + h], RT2, ALU.mult, ALU.mult, inc="vec")
                ctx.wait("tensor", "vec", mrv)
                mbv = ctx.mm(p_mb[:], C["ones_row_f32"][:], R(2), True, True, inc="mm")
                ctx.wait("vector", "mm", mbv)
                ctx.copy("vector", mb_sb[:], p_mb[:], inc="vec")
                nk = 4 * (g + 1)
                lv = None
                stslots = [p_st, p_q, p_k, p_v]
                NS = 4
                stv_s = {i: 0 for i in range(NS)}
                rdy = {}
                ptv = {i: 0 for i in range(NS)}
                svv = {i: 0 for i in range(NS)}
                pvv = {i: 0 for i in range(NS)}
                LAG = 2
                for kc in range(nk + LAG):
                    if kc < nk:
                        ps = kc % NS
                        if ptv[ps]:
                            ctx.wait("tensor", "vec", ptv[ps])
                        stv_s[ps] = ctx.mm(
                            stslots[ps][:],
                            k_fm[:, h * SEQ + kc * 128:h * SEQ + kc * 128 + 128],
                            q_fm[:, qw:qw + 512], True, True, inc="mm")
                    if kc >= LAG:
                        kp = kc - LAG
                        pc = kp % NS
                        ctx.wait("tensor", "vec" if (kp - 4 * g) >= 0 else "scl",
                                 rdy[kp])
                        ctx.mm(p_o[:], v_tm[:, (kp * 4 + h) * 128:
                                              (kp * 4 + h) * 128 + 128],
                               ptb[:, pc * 512:pc * 512 + 512], kp == 0,
                               kp == nk - 1, inc="mm")
                        lv = ctx.mm(p_l[:], C["ones_col_bf"][:],
                                    ptb[:, pc * 512:pc * 512 + 512], kp == 0,
                                    kp == nk - 1, inc="mm")
                        pvv[pc] = lv
                    if kc < nk:
                        ps = kc % NS
                        ctx.wait("vector", "mm", stv_s[ps])
                        if svv[ps]:
                            ctx.wait("vector", "scl", svv[ps])
                        ev = ctx.stt("vector", exps[:, ps * 512:ps * 512 + 512],
                                     stslots[ps][:], RT2, mb_sb[:], ALU.mult,
                                     ALU.subtract, inc="vec")
                        ptv[ps] = ev
                        ctx.wait("scalar", "vec", ev)
                        if pvv[ps]:
                            ctx.wait("scalar", "mm", pvv[ps])
                        sv = ctx.act(ptb[:, ps * 512:ps * 512 + 512],
                                     exps[:, ps * 512:ps * 512 + 512], AF.Exp, inc="scl")
                        svv[ps] = sv
                        rdy[kc] = sv
                        dj = kc - 4 * g
                        if dj >= 0:
                            ctx.wait("vector", "scl", sv)
                            rdy[kc] = ctx.tt("vector", ptb[:, ps * 512:ps * 512 + 512],
                                             ptb[:, ps * 512:ps * 512 + 512],
                                             C["diagmask"][:, dj * 512:dj * 512 + 512],
                                             ALU.mult, inc="vec")
                ctx.wait("vector", "mm", lv)
                ov = ctx.copy("vector", obuf[:], p_o[:], inc="vec")
                lvv = ctx.copy("vector", lrow[:], p_l[:, 0:512], inc="vec")
                ctx.wait("sync", "vec", lvv)
                halfd = cc3_dr[0] if g < 4 else cc3_dr[1]
                r0 = ((g // 2) % 2) * 516
                c0 = (g % 2) * 512
                ctx.dma("sync", halfd[r0 + h * 128:r0 + h * 128 + 128, c0:c0 + 512],
                        obuf[:], inc="a_st")
                sov = ctx.dma("sync", halfd[r0 + 512 + h:r0 + 513 + h, c0:c0 + 512],
                              lrow[:], inc="a_st")
                ctx.wait("vector", "a_st", sov)
            if half_done is not None and g in (3, 7):
                half_done(g // 4)
        ctx.barrier()


def emit_deltanet(ctx, es, C, xsrc, wq_dr, wk_dr, wv_dr, wb_dr, wg1_dr, wg2_dr,
                  gnf_dr, cc1_dr, half_done=None):
    with ExitStack() as ph:
        wq_s = sbt(ctx, ph, "d_wq", [128, 16 * 512], BF16)
        wk_s = sbt(ctx, ph, "d_wk", [128, 16 * 512], BF16)
        wv_s = sbt(ctx, ph, "d_wv", [128, 16 * 512], BF16)
        wb_s = sbt(ctx, ph, "d_wb", [128, 16 * 4], BF16)
        wg1_s = sbt(ctx, ph, "d_wg1", [128, 16 * 128], BF16)
        wg2_s = sbt(ctx, ph, "d_wg2", [128, 512], BF16)
        gnf4 = sbt(ctx, ph, "d_gnf", [128, 512], F32)
        xg = sbt(ctx, ph, "d_xg", [128, 2 * 16 * 512], BF16)
        qf = sbt(ctx, ph, "d_qf", [128, 2 * 2048], BF16)
        kf = sbt(ctx, ph, "d_kf", [128, 2 * 2048], BF16)
        kt = sbt(ctx, ph, "d_kt", [128, 2 * 2048], BF16)
        vt = sbt(ctx, ph, "d_vt", [128, 2 * 2048], BF16)
        gate = sbt(ctx, ph, "d_gate", [128, 2 * 2048], BF16)
        xg1 = sbt(ctx, ph, "d_xg1", [128, 2 * 512], BF16)
        bt = sbt(ctx, ph, "d_bt", [128, 2 * 16], F32)
        ksum = sbt(ctx, ph, "d_ksum", [128, 2 * 16], F32)
        qg_all = sbt(ctx, ph, "d_qga", [128, 2 * 2048], BF16)
        ktm_g = sbt(ctx, ph, "d_ktmg", [128, 2048], F32)
        ogb = sbt(ctx, ph, "d_ogb", [128, 2048], BF16)
        sgt = sbt(ctx, ph, "d_sgt", [128, 512], F32)
        rowb = sbt(ctx, ph, "d_rowb", [1, 2048], F32)
        nb4 = sbt(ctx, ph, "d_nb", [128, 512], BF16)
        ntb4 = sbt(ctx, ph, "d_ntb", [128, 2 * 512], BF16)
        mqb4 = sbt(ctx, ph, "d_mqb", [128, 2 * 512], BF16)
        spw4 = sbt(ctx, ph, "d_spw", [128, 512], BF16)
        spwt4 = sbt(ctx, ph, "d_spwt", [128, 512], BF16)
        ptb4 = sbt(ctx, ph, "d_pt", [128, 2 * 512], BF16)
        rb4 = sbt(ctx, ph, "d_rb", [128, 512], BF16)
        fb4 = sbt(ctx, ph, "d_fb", [128, 512], BF16)
        zb4 = sbt(ctx, ph, "d_zb", [128, 512], BF16)
        ub4 = sbt(ctx, ph, "d_ub", [128, 512], BF16)
        btp4 = sbt(ctx, ph, "d_btp", [128, 2 * 512], BF16)
        s32 = sbt(ctx, ph, "d_s32", [128, 2 * 4 * 128], F32)
        sbf = sbt(ctx, ph, "d_sbf", [128, 2 * 4 * 128], BF16)
        pA = pst(ctx, ph, "d_pA", [128, 512])
        pB = pst(ctx, ph, "d_pB", [128, 512])
        pC = pst(ctx, ph, "d_pC", [128, 512])
        pD = pst(ctx, ph, "d_pD", [128, 512])
        pE = pst(ctx, ph, "d_pE", [128, 512])
        pF = pst(ctx, ph, "d_pF", [128, 512])
        pG = pst(ctx, ph, "d_pG", [128, 512])
        pH = pst(ctx, ph, "d_pH", [128, 512])

        for dst, src, n in ((wq_s, wq_dr, 512), (wk_s, wk_dr, 512), (wv_s, wv_dr, 512),
                            (wb_s, wb_dr, 4), (wg1_s, wg1_dr, 128)):
            ctx.dma("gpsimd", dst[:].rearrange("p (c n) -> p c n", n=n),
                    src.rearrange("(c p) n -> p c n", p=128), inc="d_w")
        ctx.dma("gpsimd", wg2_s[:], wg2_dr, inc="d_w")
        wv_v = ctx.dma("gpsimd", gnf4[:], gnf_dr, inc="d_w")
        for e in ("tensor", "vector", "scalar"):
            ctx.wait(e, "d_w", wv_v)

        sgtb_all = sgt[:].bitcast(BF16)
        sgt4 = sgtb_all[:, 0:512]
        sgt4b = sgtb_all[:, 512:1024]
        pcfree = {0: 0, 1: 0}
        xg_mm = {0: 0, 1: 0}
        for g in range(DN_GROUPS):
            slot = g % 2
            so = slot * 2048
            ctx.wait("gpsimd", "mm", xg_mm[slot])
            xv = ctx.dma("gpsimd",
                         xg[:, slot * 8192:slot * 8192 + 8192].rearrange(
                             "p (c n) -> p c n", n=512),
                         xsrc(g).rearrange("(c p) n -> p c n", p=128), inc=f"d_x{slot}")
            ctx.wait("tensor", f"d_x{slot}", xv)
            for tc4 in range(4):
                xsl = [xg[:, slot * 8192 + k * 512 + tc4 * 128:
                          slot * 8192 + k * 512 + tc4 * 128 + 128] for k in range(NCH)]
                ctx.wait("tensor", "vec")
                ctx.wait("tensor", "scl")
                mvq = mv1 = mvv_ = mvb = mv2 = None
                for k in range(NCH):
                    mvq = ctx.mm(pA[:], xsl[k], wq_s[:, k * 512:k * 512 + 512],
                                 k == 0, k == NCH - 1, inc="mm" if k == NCH - 1 else None)
                for k in range(NCH):
                    mv1 = ctx.mm(pB[:], xsl[k], wk_s[:, k * 512:k * 512 + 512],
                                 k == 0, k == NCH - 1, inc="mm" if k == NCH - 1 else None)
                for k in range(NCH):
                    mvv_ = ctx.mm(pC[:], xsl[k], wv_s[:, k * 512:k * 512 + 512],
                                  k == 0, k == NCH - 1, inc="mm" if k == NCH - 1 else None)
                for k in range(NCH):
                    mvb = ctx.mm(pD[:, 0:4], xsl[k], wb_s[:, k * 4:k * 4 + 4],
                                 k == 0, k == NCH - 1, inc="mm" if k == NCH - 1 else None)
                for k in range(NCH):
                    mv2 = ctx.mm(pE[:, 0:128], wg1_s[:, k * 128:k * 128 + 128], xsl[k],
                                 k == 0, k == NCH - 1, inc="mm" if k == NCH - 1 else None)
                ctx.wait("scalar", "mm", mvq)
                ctx.wait("scalar", "vec")
                sv = ctx.act(sgt[:], pA[:], AF.Sigmoid, inc="scl")
                ctx.wait("vector", "scl", sv)
                qtv = ctx.tt("vector", qg_all[:, so + tc4 * 512:so + tc4 * 512 + 512],
                             pA[:], sgt[:], ALU.mult, inc="vec")
                ctx.wait("scalar", "mm", mv1)
                ctx.wait("scalar", "vec", qtv)
                sv = ctx.act(sgt[:], pB[:], AF.Sigmoid, inc="scl")
                ctx.wait("vector", "scl", sv)
                ctx.tt("vector", ktm_g[:, tc4 * 512:tc4 * 512 + 512], pB[:], sgt[:],
                       ALU.mult, inc="vec")
                ctx.wait("scalar", "mm", mvb)
                bv = ctx.act(bt[:, slot * 16 + tc4 * 4:slot * 16 + tc4 * 4 + 4],
                             pD[:, 0:4], AF.Sigmoid, inc="scl")
                ctx.wait("vector", "scl", bv)
                ctx.ts("vector", bt[:, slot * 16 + tc4 * 4:slot * 16 + tc4 * 4 + 4],
                       bt[:, slot * 16 + tc4 * 4:slot * 16 + tc4 * 4 + 4], 2.0, 1.0,
                       ALU.mult, ALU.mult, inc="vec")
                ctx.wait("vector", "mm", mvv_)
                ctx.copy("vector", vt[:, so + tc4 * 512:so + tc4 * 512 + 512], pC[:],
                         inc="vec")
                ctx.wait("vector", "mm", mv2)
                xv1 = ctx.copy("vector", xg1[:, slot * 512 + tc4 * 128:
                                             slot * 512 + tc4 * 128 + 128],
                               pE[:, 0:128], inc="vec")
                ctx.wait("tensor", "vec", xv1)
                gmv = None
                for h in range(4):
                    gmv = ctx.mm(pF[:, h * 128:h * 128 + 128],
                                 wg2_s[:, h * 128:h * 128 + 128],
                                 xg1[:, slot * 512 + tc4 * 128:slot * 512 + tc4 * 128 + 128],
                                 True, True, inc="mm" if h == 3 else None)
                ctx.wait("scalar", "mm", gmv)
                ctx.act(gate[:, so + tc4 * 512:so + tc4 * 512 + 512], pF[:],
                        AF.Sigmoid, inc="scl")
                for h in range(4):
                    ctx.tt("vector", sgt[:, 0:128],
                           ktm_g[:, tc4 * 512 + h * 128:tc4 * 512 + h * 128 + 128],
                           ktm_g[:, tc4 * 512 + h * 128:tc4 * 512 + h * 128 + 128],
                           ALU.mult)
                    ctx.op("vector", lambda v,
                           o=ksum[:, slot * 16 + tc4 * 4 + h:slot * 16 + tc4 * 4 + h + 1],
                           a=sgt[:, 0:128]: v.reduce_sum(o, a, axis=AX.X), inc="vec")
            vv = ctx.ts("vector", ksum[:, slot * 16:slot * 16 + 16],
                        ksum[:, slot * 16:slot * 16 + 16], 1.0, EPS_DN, ALU.mult,
                        ALU.add, inc="vec")
            ctx.wait("scalar", "vec", vv)
            sv = ctx.act(ksum[:, slot * 16:slot * 16 + 16],
                         ksum[:, slot * 16:slot * 16 + 16], AF.Sqrt, inc="scl")
            ctx.wait("vector", "scl", sv)
            ctx.op("vector", lambda v, o=ksum[:, slot * 16:slot * 16 + 16],
                   a=ksum[:, slot * 16:slot * 16 + 16]: v.reciprocal(o, a), inc="vec")
            pv = [pA[:].bitcast(BF16), pB[:].bitcast(BF16)]
            evv = {}
            ctx.wait("tensor", "vec")
            for tc4 in range(4):
                for h in range(4):
                    i = tc4 * 4 + h
                    ctx.ts("vector", kt[:, so + i * 128:so + i * 128 + 128],
                           ktm_g[:, tc4 * 512 + h * 128:tc4 * 512 + h * 128 + 128],
                           ksum[:, slot * 16 + i:slot * 16 + i + 1], 1.0,
                           ALU.mult, ALU.mult, inc="vec")
            ktv = ctx.val("vec")
            ctx.wait("tensor", "vec", ktv)
            for j in range(32):
                srcb = qg_all if j < 16 else kt
                dstb = qf if j < 16 else kf
                i = j % 16
                if j >= 2:
                    ctx.wait("tensor", "vec", evv[j - 2])
                tv = ctx.trans(pv[j % 2][:, 0:128], srcb[:, so + i * 128:so + i * 128 + 128],
                               C["ident_bf"][:], inc="mm")
                ctx.wait("vector", "mm", tv)
                evv[j] = ctx.copy("vector", dstb[:, so + i * 128:so + i * 128 + 128],
                                  pv[j % 2][:, 0:128], inc="vec")
            ctx.wait("vector", "scl")

            def pc_gen(tc4):
                sl_ = tc4 % 2
                kfs = [kf[:, so + (tc4 * 4 + h) * 128:so + (tc4 * 4 + h) * 128 + 128]
                       for h in range(4)]
                qfs = [qf[:, so + (tc4 * 4 + h) * 128:so + (tc4 * 4 + h) * 128 + 128]
                       for h in range(4)]
                bcol = [bt[:, slot * 16 + tc4 * 4 + h:slot * 16 + tc4 * 4 + h + 1]
                        for h in range(4)]
                ptb = ptb4[:, sl_ * 512:sl_ * 512 + 512]
                ntb = ntb4[:, sl_ * 512:sl_ * 512 + 512]
                mqb = mqb4[:, sl_ * 512:sl_ * 512 + 512]
                btp = btp4[:, sl_ * 512:sl_ * 512 + 512]

                def B4(p, h):
                    return p[:, h * 128:h * 128 + 128]

                ctx.wait("tensor", "vec", pcfree[sl_])
                ctx.wait("tensor", "scl")
                m1 = None
                for h in range(4):
                    ctx.mm(B4(pF, h), kfs[h], kfs[h], True, True)
                for h in range(4):
                    m1 = ctx.mm(B4(pG, h), kfs[h], qfs[h], True, True,
                                inc="mm" if h == 3 else None)
                for h in range(4):
                    ctx.ts("vector", btp[:, h * 128:h * 128 + 128], C["ones_t"][:],
                           bcol[h], 1.0, ALU.mult, ALU.mult)
                yield
                ctx.wait("vector", "mm", m1)
                ctx.tt("vector", sgt4[:], pF[:], btp[:], ALU.mult)
                ctx.tt("vector", nb4[:], sgt4[:], C["mask_bd4n"][:], ALU.mult)
                ctx.tt("vector", fb4[:], sgt4[:], C["mask_of4"][:], ALU.mult)
                v1 = ctx.tt("vector", mqb, pG[:], C["mask_iU4"][:], ALU.mult, inc="vec")
                yield
                ctx.wait("tensor", "vec", v1)
                tv = None
                for h in range(2):
                    tv = ctx.trans([pF, pG][h][:].bitcast(BF16)[:, 0:128],
                                   nb4[:, h * 128:h * 128 + 128], C["ident_bf"][:],
                                   inc="mm" if h == 1 else None)
                yield
                ctx.wait("vector", "mm", tv)
                for h in range(2):
                    ctx.copy("vector", spwt4[:, h * 128:h * 128 + 128],
                             [pF, pG][h][:].bitcast(BF16)[:, 0:128],
                             inc="vec" if h == 1 else None)
                ctx.wait("tensor", "vec")
                tv = None
                for h in range(2, 4):
                    tv = ctx.trans([pF, pG][h - 2][:].bitcast(BF16)[:, 0:128],
                                   nb4[:, h * 128:h * 128 + 128], C["ident_bf"][:],
                                   inc="mm" if h == 3 else None)
                yield
                ctx.wait("vector", "mm", tv)
                for h in range(2, 4):
                    ctx.copy("vector", spwt4[:, h * 128:h * 128 + 128],
                             [pF, pG][h - 2][:].bitcast(BF16)[:, 0:128])
                ctx.copy("vector", spw4[:], nb4[:])
                v1 = ctx.tt("vector", ptb, spwt4[:], C["ident4"][:], ALU.add, inc="vec")
                yield
                for lvl in range(1, 5):
                    ctx.wait("tensor", "vec", v1)
                    mv = None
                    for h in range(4):
                        ctx.mm(B4(pF, h), spwt4[:, h * 128:h * 128 + 128],
                               spw4[:, h * 128:h * 128 + 128], True, True)
                    if lvl < 4:
                        for h in range(4):
                            mv = ctx.mm(B4(pG, h), spw4[:, h * 128:h * 128 + 128],
                                        spwt4[:, h * 128:h * 128 + 128], True, True,
                                        inc="mm" if h == 3 else None)
                    else:
                        mv = ctx.mm(B4(pF, 3), spwt4[:, 384:512], spw4[:, 384:512],
                                    True, True, inc="mm")
                    yield
                    ctx.wait("vector", "mm", mv)
                    ctx.copy("vector", spw4[:], pF[:])
                    if lvl < 4:
                        v1 = ctx.copy("vector", spwt4[:], pG[:], inc="vec")
                    else:
                        v1 = ctx.copy("vector", spw4[:, 0:1], spw4[:, 0:1], inc="vec")
                    yield
                    ctx.wait("tensor", "vec", v1)
                    mv = None
                    for h in range(4):
                        mv = ctx.mm(B4(pH, h), spw4[:, h * 128:h * 128 + 128],
                                    ptb[:, h * 128:h * 128 + 128], True, True,
                                    inc="mm" if h == 3 else None)
                    yield
                    ctx.wait("vector", "mm", mv)
                    v1 = ctx.tt("vector", ptb, pH[:], ptb, ALU.add, inc="vec")
                yield
                ctx.wait("tensor", "vec", v1)
                mv = None
                for h in range(4):
                    mv = ctx.mm(B4(pH, h), fb4[:, h * 128:h * 128 + 128],
                                ptb[:, h * 128:h * 128 + 128], True, True,
                                inc="mm" if h == 3 else None)
                yield
                ctx.wait("vector", "mm", mv)
                pcfree[sl_] = ctx.copy("vector", ntb, pH[:], inc="vec")
                yield

            def serial_gen(tc4):
                sl_ = tc4 % 2
                first = (g == 0 and tc4 == 0)
                par = (g * 4 + tc4) % 2
                kfs = [kf[:, so + (tc4 * 4 + h) * 128:so + (tc4 * 4 + h) * 128 + 128]
                       for h in range(4)]
                qfs = [qf[:, so + (tc4 * 4 + h) * 128:so + (tc4 * 4 + h) * 128 + 128]
                       for h in range(4)]
                kts = [kt[:, so + (tc4 * 4 + h) * 128:so + (tc4 * 4 + h) * 128 + 128]
                       for h in range(4)]
                soldbf = [sbf[:, par * 512 + h * 128:par * 512 + h * 128 + 128]
                          for h in range(4)]
                ptb = ptb4[:, sl_ * 512:sl_ * 512 + 512]
                ntb = ntb4[:, sl_ * 512:sl_ * 512 + 512]
                mqb = mqb4[:, sl_ * 512:sl_ * 512 + 512]
                btp = btp4[:, sl_ * 512:sl_ * 512 + 512]
                vt_w = vt[:, so + tc4 * 512:so + tc4 * 512 + 512]

                def B4(p, h):
                    return p[:, h * 128:h * 128 + 128]

                ctx.wait("tensor", "vec")
                if not first:
                    mv = None
                    for h in range(4):
                        mv = ctx.mm(B4(pA, h), kfs[h], soldbf[h], True, True,
                                    inc="mm" if h == 3 else None)
                    yield
                    ctx.wait("vector", "mm", mv)
                    ctx.stt("vector", sgt4b[:], pA[:], -1.0, vt_w, ALU.mult, ALU.add)
                    rv = ctx.tt("vector", rb4[:], sgt4b[:], btp, ALU.mult, inc="vec")
                else:
                    rv = ctx.tt("vector", rb4[:], vt_w, btp, ALU.mult, inc="vec")
                yield
                ctx.wait("tensor", "vec", rv)
                mv = None
                for h in range(4):
                    mv = ctx.mm(B4(pB, h), ptb[:, h * 128:h * 128 + 128],
                                rb4[:, h * 128:h * 128 + 128], True, True,
                                inc="mm" if h == 3 else None)
                yield
                ctx.wait("vector", "mm", mv)
                zv = ctx.copy("vector", zb4[:], pB[:], inc="vec")
                yield
                usrc4 = zb4
                for it in range(3):
                    udst4 = ub4 if it == 2 else (rb4 if it % 2 == 0 else ub4)
                    ctx.wait("tensor", "vec", zv)
                    mv = None
                    for h in range(4):
                        mv = ctx.mm(B4(pC, h), ntb[:, h * 128:h * 128 + 128],
                                    usrc4[:, h * 128:h * 128 + 128], True, True,
                                    inc="mm" if h == 3 else None)
                    yield
                    ctx.wait("vector", "mm", mv)
                    zv = ctx.stt("vector", udst4[:], pC[:], -1.0, zb4[:], ALU.mult,
                                 ALU.add, inc="vec")
                    usrc4 = udst4
                yield
                ctx.wait("tensor", "vec", zv)
                mv = None
                for h in range(4):
                    mv = ctx.mm(B4(pD, h), kts[h], ub4[:, h * 128:h * 128 + 128],
                                True, True, inc="mm" if h == 3 else None)
                yield
                ctx.wait("vector", "mm", mv)
                s32o = s32[:, par * 512:par * 512 + 512]
                s32n = s32[:, (1 - par) * 512:(1 - par) * 512 + 512]
                sbfn = sbf[:, (1 - par) * 512:(1 - par) * 512 + 512]
                if not first:
                    ctx.tt("vector", s32n, pD[:], s32o, ALU.add)
                else:
                    ctx.copy("vector", s32n, pD[:])
                sv_ = ctx.copy("vector", sbfn, s32n, inc="vec")
                yield
                ctx.wait("tensor", "vec", sv_)
                mv = None
                for h in range(4):
                    if not first:
                        ctx.mm(B4(pE, h), soldbf[h], qfs[h], True, False)
                        mv = ctx.mm(B4(pE, h), ub4[:, h * 128:h * 128 + 128],
                                    mqb[:, h * 128:h * 128 + 128], False, True,
                                    inc="mm" if h == 3 else None)
                    else:
                        mv = ctx.mm(B4(pE, h), ub4[:, h * 128:h * 128 + 128],
                                    mqb[:, h * 128:h * 128 + 128], True, True,
                                    inc="mm" if h == 3 else None)
                yield
                ctx.wait("vector", "mm", mv)
                ogw = ogb[:, tc4 * 512:tc4 * 512 + 512]
                ctx.tt("vector", ogw, pE[:], gate[:, so + tc4 * 512:so + tc4 * 512 + 512],
                       ALU.mult)
                gv = ctx.tt("vector", sgt4b[:], ogw, ogw, ALU.mult, inc="vec")
                yield
                ctx.wait("tensor", "vec", gv)
                mv = ctx.mm(pA[0:1, 0:512], C["ones_col_bf"][:], sgt4b[:], True, True,
                            inc="mm")
                yield
                ctx.wait("vector", "mm", mv)
                ctx.op("vector", lambda v, o=rowb[:, tc4 * 512:tc4 * 512 + 512],
                       a=pA[0:1, 0:512]: v.tensor_copy(o, a), inc="vec")
                yield

            for t in range(DN_CHUNKS + 1):
                gens = []
                if t < DN_CHUNKS:
                    gens.append(pc_gen(t))
                if t > 0:
                    gens.append(serial_gen(t - 1))
                while gens:
                    nxt = []
                    for gg in gens:
                        try:
                            next(gg)
                            nxt.append(gg)
                        except StopIteration:
                            pass
                    gens = nxt
            if DN_CHUNKS < 4:
                continue
            vv = ctx.val("vec")
            ctx.wait("scalar", "vec", vv)
            sv = ctx.ts("vector", rowb[:], rowb[:], 1.0 / 128.0, EPS_DN, ALU.mult,
                        ALU.add, inc="vec")
            ctx.wait("scalar", "vec", sv)
            sv = ctx.act(rowb[:], rowb[:], AF.Sqrt, inc="scl")
            ctx.wait("vector", "scl", sv)
            rv = ctx.op("vector", lambda v, o=rowb[:], a=rowb[:]: v.reciprocal(o, a),
                        inc="vec")
            ctx.wait("tensor", "vec", rv)
            for tc4 in range(4):
                bv = ctx.mm(pH[:], C["ones_row_f32"][:],
                            rowb[:, tc4 * 512:tc4 * 512 + 512], True, True, inc="mm")
                ctx.wait("vector", "mm", bv)
                ogw = ogb[:, tc4 * 512:tc4 * 512 + 512]
                ctx.tt("vector", ogw, ogw, pH[:], ALU.mult)
                ov = ctx.tt("vector", ogw, ogw, gnf4[:], ALU.mult, inc="vec")
                ctx.wait("tensor", "vec", ov)
                pcfree[0] = pcfree[1] = ov
            ovv = ctx.val("vec")
            ctx.wait("sync", "vec", ovv)
            half = cc1_dr[0] if g < 4 else cc1_dr[1]
            for h in range(4):
                r0 = ((g // 2) % 2) * 512 + h * 128
                c0 = (g % 2) * 512
                sv_ = ctx.dma("sync",
                              half[r0:r0 + 128, c0:c0 + 512]
                              .rearrange("q (a n) -> q a n", a=4),
                              ogb[:].rearrange("p (a c n) -> p a c n", a=4, c=4)[:, :, h, :],
                              inc="d_og")
            ctx.wait("vector", "d_og")
            if half_done is not None and g in (3, 7):
                half_done(g // 4)
            xg_mm[slot] = ctx.val("mm")
        ctx.barrier()


def emit_wo_ln(ctx, es, out_es, C, og_dr, resid_ap, wo_dr, l_norm, sel, tag,
               cc_halves=None):
    BLK = 516 if l_norm else 512
    r_fm = sbt(ctx, out_es, f"w_r{tag}", [128, NCH * TOK], BF16)
    with ExitStack() as ph:
        og_s = sbt(ctx, ph, f"w_og{tag}", [128, NCH * TOK], BF16)
        nslot = 2
        og4 = sbt(ctx, ph, f"w_og4{tag}", [128, nslot * 2 * TOK], BF16)
        res_s = sbt(ctx, ph, f"w_res{tag}", [128, NCH * TOK], BF16)
        wot = sbt(ctx, ph, f"w_wo{tag}", [128, 2 * 16 * 512], BF16)
        if l_norm:
            lr16 = sbt(ctx, ph, f"w_lr16{tag}", [16, 4 * TOK], BF16)
            lrs = sbt(ctx, ph, f"w_lrs{tag}", [16, TOK], F32)
            lr0 = sbt(ctx, ph, f"w_lr0{tag}", [1, 16 * TOK], BF16)
        p_w = [pst(ctx, ph, f"w_pw{tag}{i}", [128, 512]) for i in range(2)]
        p_rl = pst(ctx, ph, f"w_prl{tag}", [128, 512])
        for k in range(NCH):
            ov = ctx.dma("gpsimd", res_s[:, k * TOK:(k + 1) * TOK], resid_ap(k),
                         inc=f"w_ld{tag}")
        sumv = {0: 0, 1: 0}
        ov_ll = 0
        for hf in range(2):
            if cc_halves is not None:
                ctx.wait("gpsimd", "cc", cc_halves[hf])
            wA, wB = 2 * hf, 2 * hf + 1
            for k in range(NCH):
                r_, lc = k // 4, k % 4
                sl_ = k % nslot
                o4 = og4[:, sl_ * 2 * TOK:(sl_ + 1) * 2 * TOK]
                ctx.wait("gpsimd", "vec", sumv[sl_])
                for wi, w in enumerate((wA, wB)):
                    ldv = ctx.dma("gpsimd", o4[:, wi * TOK:(wi + 1) * TOK],
                                  og_dr(r_, w)[lc * 128:lc * 128 + 128, :],
                                  inc=f"w_g{tag}{sl_}")
                ctx.wait("vector", f"w_g{tag}{sl_}", ldv)
                osl = og_s[:, k * TOK:(k + 1) * TOK]
                if hf == 0:
                    ctx.ts("vector", osl, o4[:, 0:TOK], sel[:, 0:1], 1.0,
                           ALU.mult, ALU.mult)
                else:
                    ctx.stt("vector", osl, o4[:, 0:TOK], sel[:, 2:3], osl,
                            ALU.mult, ALU.add)
                sumv[sl_] = ctx.stt("vector", osl, o4[:, TOK:2 * TOK],
                                    sel[:, wB:wB + 1], osl, ALU.mult, ALU.add,
                                    inc="vec")
            if l_norm:
                for w in (wA, wB):
                    for r_ in range(4):
                        ov_ll = ctx.dma("gpsimd",
                                        lr16[r_ * 4:r_ * 4 + 4, w * TOK:(w + 1) * TOK],
                                        og_dr(r_, w)[512:516, :], inc=f"w_ll{tag}")
        ogv = ctx.val("vec")
        if l_norm:
            ctx.wait("vector", f"w_ll{tag}", ov_ll)
            ctx.ts("vector", lrs[:], lr16[:, 0:TOK], sel[0:16, 0:1], 1.0,
                   ALU.mult, ALU.mult)
            for w in range(1, 4):
                ctx.stt("vector", lrs[:], lr16[:, w * TOK:(w + 1) * TOK],
                        sel[0:16, w:w + 1], lrs[:], ALU.mult, ALU.add)
            rv = ctx.op("vector", lambda v, o=lrs[:], a=lrs[:]:
                        v.reciprocal(o, a), inc="vec")
            ctx.wait("gpsimd", "vec", rv)
            for k in range(16):
                ov = ctx.dma("gpsimd", lr0[:, k * TOK:(k + 1) * TOK],
                             lrs[k:k + 1, :], inc=f"w_lr{tag}")
            ctx.wait("tensor", f"w_lr{tag}", ov)
            ctx.wait("tensor", f"w_ld{tag}")
            ctx.wait("tensor", "vec", ogv)
            for k in range(NCH):
                for th in range(2):
                    bv = ctx.mm(p_rl[:], C["ones_row_bf"][:],
                                lr0[:, k * TOK + th * 512:k * TOK + th * 512 + 512],
                                True, True, inc="mm")
                    ctx.wait("vector", "mm", bv)
                    osl = og_s[:, k * TOK + th * 512:k * TOK + th * 512 + 512]
                    nv = ctx.tt("vector", osl, osl, p_rl[:], ALU.mult, inc="vec")
                    ctx.wait("tensor", "vec", nv)
        else:
            ctx.wait("tensor", f"w_ld{tag}")
            ctx.wait("tensor", "vec", ogv)
        wo_mm = {0: 0, 1: 0}
        wev = {0: ctx.val("vec"), 1: ctx.val("vec")}
        for ds in range(4):
            slot = ds % 2
            ctx.wait("gpsimd", "mm", wo_mm[slot])
            o = wot[:, slot * 8192:slot * 8192 + 8192].rearrange("p (c n) -> p c n", n=512)
            i = wo_dr[:, ds * 512:ds * 512 + 512].rearrange("(c p) n -> p c n", p=128)
            wv = ctx.dma("gpsimd", o, i, inc=f"w_w{tag}{slot}")
            ctx.wait("tensor", f"w_w{tag}{slot}", wv)
            for dc in range(4):
                dci = ds * 4 + dc
                for th in range(2):
                    pw = p_w[th]
                    ctx.wait("tensor", "vec", wev[th])
                    mv = None
                    for k in range(NCH):
                        off = slot * 8192 + (k * 4 + dc) * 128
                        mv = ctx.mm(pw[:], wot[:, off:off + 128],
                                    og_s[:, k * TOK + th * 512:k * TOK + th * 512 + 512],
                                    k == 0, k == NCH - 1,
                                    inc="mm" if k == NCH - 1 else None)
                    wo_mm[slot] = mv
                    ctx.wait("vector", "mm", mv)
                    wev[th] = ctx.tt("vector",
                                     r_fm[:, dci * TOK + th * 512:dci * TOK + th * 512 + 512],
                                     pw[:], res_s[:, dci * TOK + th * 512:dci * TOK + th * 512 + 512],
                                     ALU.add, inc="vec")
        ctx.barrier()
        emit_ln_fm(ctx, ph, C, r_fm, TOK, f"w{tag}", ln_sink_simple(ctx, r_fm, TOK))
        ctx.barrier()
    return r_fm



def build_full():
    nc = bass.Bass("TRN2", target_bir_lowering=False)
    dr = {}

    def ein(name, shape, dt):
        dr[name] = nc.dram_tensor(name, shape, dt, kind="ExternalInput").ap()
        return dr[name]

    for k, (shape, dt) in CONST_SPECS.items():
        ein(k, shape, dt)
    ein("x_fm", [D, SEQ], F32)
    ein("x_res", [D, TOK], F32)
    for nm in ("wq0", "wk0", "wv0"):
        ein(nm, [D, 512], F32)
    ein("wb0", [D, 4], F32)
    ein("wg1t", [D, 128], F32)
    ein("wg2sl", [128, 512], F32)
    ein("gnf", [128, 512], F32)
    ein("wo0f", [D, D], F32)
    ein("gate_w", [D, 8], F32)
    ein("ltmask", [128, 64], F32)
    ein("wg_f", [8 * D, 1408], F32)
    ein("wu_f", [8 * D, 1408], F32)
    ein("wd_f", [8 * 1408, D], F32)
    ein("swg_f", [D, 2816], F32)
    ein("swu_f", [D, 2816], F32)
    ein("swd_f", [2816, D], F32)
    for nm in ("wq1", "wk1", "wv1"):
        ein(nm, [D, 512], F32)
    ein("wo1f", [D, D], F32)
    ein("wg8f", [D, 8192], I8)
    ein("wu8f", [D, 8192], I8)
    ein("wd8f", [8192, D], I8)
    ein("sel", [128, 4], F32)
    ein("sg_t", [128, 64], F32)
    ein("su_t", [128, 64], F32)
    ein("sd_t", [128, 16], F32)
    out = nc.dram_tensor("out", [TOK, D], F32, kind="ExternalOutput").ap()

    cc1_inA = nc.dram_tensor("cc1_inA", [1, 1024 * TOK], BF16)
    cc1_inB = nc.dram_tensor("cc1_inB", [1, 1024 * TOK], BF16)
    cc1_outA = nc.dram_tensor("cc1_outA", [4, 1024 * TOK], BF16)
    cc1_outB = nc.dram_tensor("cc1_outB", [4, 1024 * TOK], BF16)
    cc2_in = nc.dram_tensor("cc2_in", [1, D * TOK], BF16)
    cc2_out = nc.dram_tensor("cc2_out", [4, D * TOK], BF16)
    cc3_inA = nc.dram_tensor("cc3_inA", [1, 2 * 516 * TOK], BF16)
    cc3_inB = nc.dram_tensor("cc3_inB", [1, 2 * 516 * TOK], BF16)
    cc3_outA = nc.dram_tensor("cc3_outA", [4, 2 * 516 * TOK], BF16)
    cc3_outB = nc.dram_tensor("cc3_outB", [4, 2 * 516 * TOK], BF16)
    x2sav = nc.dram_tensor("x2sav", [D, TOK], BF16)
    cc1_inv = (cc1_inA.ap().rearrange("o (b t) -> (o b) t", t=TOK),
               cc1_inB.ap().rearrange("o (b t) -> (o b) t", t=TOK))
    cc3_inv = (cc3_inA.ap().rearrange("o (b t) -> (o b) t", t=TOK),
               cc3_inB.ap().rearrange("o (b t) -> (o b) t", t=TOK))
    cc2_inv = cc2_in.ap().rearrange("o (r t) -> (o r) t", t=TOK)
    cc2_outv = cc2_out.ap().rearrange("r (b t) -> (r b) t", t=TOK)

    def mk_outv(tA, tB):
        vA = tA.ap().rearrange("r (w b t) -> r w b t", w=2, t=TOK)
        vB = tB.ap().rearrange("r (w b t) -> r w b t", w=2, t=TOK)

        def f(r, w):
            return (vA if w < 2 else vB)[r, w % 2]
        return f

    cc1_outv = mk_outv(cc1_outA, cc1_outB)
    cc3_outv = mk_outv(cc3_outA, cc3_outB)

    G8 = [list(range(8))]
    G4 = [[0, 1, 2, 3], [4, 5, 6, 7]]

    es = ExitStack()
    ctx = Ctx(nc, es)
    C = emit_load_consts(ctx, es, dr)
    gws = sbt(ctx, es, "gws", [128, NCH * 8], BF16)
    ltm = sbt(ctx, es, "ltm", [128, 64], F32)
    sel = sbt(ctx, es, "sel_s", [128, 4], F32)
    ctx.dma("gpsimd", sel[:], dr["sel"], inc="ldc")
    ctx.dma("gpsimd", gws[:].rearrange("p (c t) -> p c t", t=8),
            dr["gate_w"].rearrange("(c p) t -> p c t", p=128), inc="ldc")
    ctx.dma("gpsimd", ltm[:], dr["ltmask"], inc="ldc")

    for e_ in ("tensor", "vector", "scalar", "sync", "gpsimd"):
        ctx.wait(e_, "ldc")

    cc1_vals = []

    def og_half(idx):
        ctx.wait("gpsimd", "d_og")
        t_in = [cc1_inA, cc1_inB][idx]
        t_out = [cc1_outA, cc1_outB][idx]
        ctx.op("gpsimd", lambda g, i=t_in, o=t_out: g.collective_compute(
            "AllGather", ALU.bypass, replica_groups=G4,
            ins=[i.ap().opt()], outs=[o.ap().opt()]), inc="cc", chain=False)
        cc1_vals.append(ctx.val("cc"))

    emit_deltanet(ctx, es, C, lambda g: dr["x_fm"][:, g * 512:(g + 1) * 512],
                  dr["wq0"], dr["wk0"], dr["wv0"], dr["wb0"], dr["wg1t"],
                  dr["wg2sl"], dr["gnf"], cc1_inv, half_done=og_half)

    es_x2 = ExitStack()
    x2buf = sbt(ctx, es_x2, "x2buf", [128, NCH * TOK], BF16)
    es_x1 = ExitStack()
    x1 = emit_wo_ln(ctx, es, es_x1, C, cc1_outv,
                    lambda k: dr["x_res"][k * 128:(k + 1) * 128, :],
                    dr["wo0f"], False, sel, "0", cc_halves=cc1_vals)

    x2 = emit_moe(ctx, es_x2, C, x1, gws, ltm, dr["wg_f"], dr["wu_f"],
                  dr["wd_f"], dr["swg_f"], dr["swu_f"], dr["swd_f"],
                  r_fm=x2buf)
    es_x1.close()
    x2v = ctx.dma("gpsimd", cc2_inv.rearrange("(c p) t -> p c t", p=128),
                  x2[:].rearrange("p (c t) -> p c t", t=TOK), inc="x2s")
    sv2 = ctx.dma("gpsimd", x2sav[:, :].rearrange("(c p) t -> p c t", p=128),
                  x2[:].rearrange("p (c t) -> p c t", t=TOK), inc="x2p")
    ctx.wait("gpsimd", "x2s", x2v)
    ctx.op("gpsimd", lambda g: g.collective_compute(
        "AllGather", ALU.bypass, replica_groups=G4,
        ins=[cc2_in.ap().opt()], outs=[cc2_out.ap().opt()]), inc="cc", chain=False)
    cc_n = ctx.val("cc")
    x2p_n = sv2
    ctx.wait("vector", "x2s", x2v)
    ctx.wait("vector", "x2p", sv2)
    ctx.barrier(exclude=("cc",))
    es_x2.close()

    cc3_vals = []

    def o1_half(idx):
        ctx.wait("gpsimd", "a_st")
        t_in = [cc3_inA, cc3_inB][idx]
        t_out = [cc3_outA, cc3_outB][idx]
        ctx.op("gpsimd", lambda g, i=t_in, o=t_out: g.collective_compute(
            "AllGather", ALU.bypass, replica_groups=G4,
            ins=[i.ap().opt()], outs=[o.ap().opt()]), inc="cc", chain=False)
        cc3_vals.append(ctx.val("cc"))

    emit_mha(ctx, es, C,
             lambda g: cc2_outv[(g // 2) * D:(g // 2) * D + D,
                                  (g % 2) * 512:(g % 2) * 512 + 512],
             dr["wq1"], dr["wk1"], dr["wv1"], cc3_inv,
             pre_x=lambda: ctx.wait("gpsimd", "cc", cc_n), half_done=o1_half)

    sgt = sbt(ctx, es, "sgt_s", [128, 64], F32)
    sut = sbt(ctx, es, "sut_s", [128, 64], F32)
    sdt = sbt(ctx, es, "sdt_s", [128, 16], F32)
    ctx.dma("gpsimd", sgt[:], dr["sg_t"])
    ctx.dma("gpsimd", sut[:], dr["su_t"])
    ctx.dma("gpsimd", sdt[:], dr["sd_t"])
    es_x3 = ExitStack()
    x3 = emit_wo_ln(ctx, es, es_x3, C, cc3_outv,
                    lambda k: x2sav.ap()[k * 128:(k + 1) * 128, :],
                    dr["wo1f"], True, sel, "1", cc_halves=cc3_vals)
    for e in ("tensor", "vector", "scalar"):
        ctx.wait(e, "ld")
    emit_l1ffn(ctx, es, C, x3, dr["wg8f"], dr["wu8f"], dr["wd8f"],
               sgt, sut, sdt, out)
    es_x3.close()
    ctx.wait("sync", "st")
    ctx.finalize()
    es.close()
    return nc


_NC_CACHE = []


def kernel(**inputs):
    g = dict(inputs)
    consts = host_consts()
    ii = np.arange(8)
    ltm = np.zeros((128, 64), np.float32)
    for j in range(8):
        ltm[:, j * 8:(j + 1) * 8] = (ii[None, :] > j)

    x = np.asarray(g["L0_attn_x"], np.float32)
    A = lambda a: np.ascontiguousarray(a)
    wo0f = A(np.asarray(g["L0_attn_Wo"]).T)
    wo1f = A(np.asarray(g["L1_attn_Wo"]).T)
    wg_f = A(np.asarray(g["L0_ffn_w_gate"]).reshape(8 * 2048, 1408))
    wu_f = A(np.asarray(g["L0_ffn_w_up"]).reshape(8 * 2048, 1408))
    wd_f = A(np.asarray(g["L0_ffn_w_down"]).reshape(8 * 1408, 2048))
    swg_f = A(np.asarray(g["L0_ffn_sw_gate"]))
    swu_f = A(np.asarray(g["L0_ffn_sw_up"]))
    swd_f = A(np.asarray(g["L0_ffn_sw_down"]))
    wg8f = A(np.asarray(g["L1_ffn_w_gate"]))
    wu8f = A(np.asarray(g["L1_ffn_w_up"]))
    wd8f = A(np.asarray(g["L1_ffn_w_down"]))
    gate_w = A(np.asarray(g["L0_ffn_gate_w"]))
    in_maps = []
    for c in range(8):
        beta, lane = c // 4, c % 4
        cs = slice(512 * lane, 512 * lane + 512)
        hs = slice(4 * lane, 4 * lane + 4)
        xT = x[beta].T
        m = dict(consts)
        m["ltmask"] = ltm
        m["x_fm"] = A(xT)
        m["x_res"] = A(xT[:, lane * 1024:lane * 1024 + 1024])
        m["wq0"] = A(np.asarray(g["L0_attn_Wq"])[cs, :].T)
        m["wk0"] = A(np.asarray(g["L0_attn_Wk"])[cs, :].T)
        m["wv0"] = A(np.asarray(g["L0_attn_Wv"])[cs, :].T)
        m["wb0"] = A(np.asarray(g["L0_attn_Wbeta"])[hs, :].T)
        m["wg1t"] = A(np.asarray(g["L0_attn_Wg1"]).T)
        m["wg2sl"] = A(np.asarray(g["L0_attn_Wg2"])[cs, :].T)
        gn_ = np.asarray(g["L0_attn_g_norm"])[cs].reshape(4, 128)
        m["gnf"] = A(np.concatenate([np.tile(gn_[h][:, None], (1, 128))
                                     for h in range(4)], axis=1))
        m["wo0f"] = wo0f
        m["gate_w"] = gate_w
        m["wg_f"] = wg_f
        m["wu_f"] = wu_f
        m["wd_f"] = wd_f
        m["swg_f"] = swg_f
        m["swu_f"] = swu_f
        m["swd_f"] = swd_f
        m["wq1"] = A(np.asarray(g["L1_attn_Wq"])[cs, :].T)
        m["wk1"] = A(np.asarray(g["L1_attn_Wk"])[cs, :].T)
        m["wv1"] = A(np.asarray(g["L1_attn_Wv"])[cs, :].T)
        m["wo1f"] = wo1f
        m["wg8f"] = wg8f
        m["wu8f"] = wu8f
        m["wd8f"] = wd8f
        selv = np.zeros((128, 4), np.float32)
        selv[:, lane] = 1.0
        m["sel"] = selv
        m["sg_t"] = A(np.asarray(g["L1_ffn_s_gate"], np.float32).reshape(64, 128).T)
        m["su_t"] = A(np.asarray(g["L1_ffn_s_up"], np.float32).reshape(64, 128).T)
        m["sd_t"] = A(np.asarray(g["L1_ffn_s_down"], np.float32).reshape(16, 128).T)
        in_maps.append(m)

    if not _NC_CACHE:
        _NC_CACHE.append(build_full())
    nc = _NC_CACHE[0]
    res = run_bass_kernel_spmd(nc, in_maps, core_ids=list(range(8)))
    out = np.concatenate([np.asarray(res.results[c]["out"]) for c in range(8)], axis=0)
    return out.reshape(2, SEQ, D)


# version 35 (final)
# speedup vs baseline: 1.5578x; 1.0065x over previous
import math
from contextlib import ExitStack

import numpy as np
import concourse.bass as bass
import concourse.mybir as mybir
from concourse.bass_utils import run_bass_kernel_spmd

F32 = mybir.dt.float32
BF16 = mybir.dt.bfloat16
I8 = mybir.dt.int8
I32 = mybir.dt.int32
AF = mybir.ActivationFunctionType
ALU = mybir.AluOpType
AX = mybir.AxisListType

D = 2048
NCH = D // 128
SEQ = 4096
TOK = 1024
HD = 128
H_PER = 4
EPS_DN = 1e-6
EPS_LN = 1e-5
INV_D = 1.0 / D


class Ctx:

    ENGINES = ("tensor", "vector", "scalar", "gpsimd", "sync")

    def __init__(self, nc, es):
        self.nc = nc
        self.es = es
        self.streams = {e: [] for e in self.ENGINES}
        self._h = {}
        self._n = {}
        for nm in ("ld", "mm", "vec", "scl", "gp", "sy", "st", "cc"):
            self._h[nm] = es.enter_context(nc.semaphore(nm))
            self._n[nm] = 0

    def dsem(self, nm):
        if nm not in self._h:
            self._h[nm] = self.es.enter_context(self.nc.semaphore(nm))
            self._n[nm] = 0
        return nm

    def val(self, nm):
        return self._n[nm]

    CHAIN = {"vector": "vec", "scalar": "scl", "gpsimd": "gp"}

    def op(self, eng, fn, inc=None, inck=1, chain=True):
        own = self.CHAIN.get(eng) if chain else None
        if own is not None and (inc is None or inc == own):
            h = self._h[own]
            prev = self._n[own]
            self.streams[eng].append(
                lambda e, fn=fn, h=h, p=prev: (e.wait_ge(h, p) if p else None,
                                               fn(e).then_inc(h, 1))[1])
            self._n[own] += 1
            return self._n[own]
        if inc is None:
            self.streams[eng].append(fn)
            return None
        h = self._h[inc]
        if own is not None:
            ho = self._h[own]
            prev = self._n[own]
            self.streams[eng].append(
                lambda e, fn=fn, h=h, k=inck, ho=ho, p=prev: (
                    e.wait_ge(ho, p) if p else None, fn(e).then_inc(h, k))[1])
        else:
            self.streams[eng].append(lambda e, fn=fn, h=h, k=inck: fn(e).then_inc(h, k))
        self._n[inc] += inck
        return self._n[inc]

    def wait(self, eng, sem, val=None):
        v = self._n[sem] if val is None else val
        if v <= 0:
            return
        h = self._h[sem]
        self.streams[eng].append(lambda e, h=h, v=v: e.wait_ge(h, v))

    def barrier(self, exclude=()):
        vals = {nm: v for nm, v in self._n.items() if v > 0 and nm not in exclude}
        for eng in self.ENGINES:
            for sem, v in vals.items():
                self.wait(eng, sem, v)

    def mm(self, o, l, r, start, stop, inc=None):
        return self.op("tensor",
                       lambda t, o=o, l=l, r=r, a=start, b=stop: t.matmul(o, l, r, start=a, stop=b),
                       inc=inc)

    def trans(self, o, i, ident, inc=None):
        return self.op("tensor", lambda t, o=o, i=i, d=ident: t.transpose(o, i, d), inc=inc)

    def tt(self, eng, o, a, b, alu, inc=None):
        return self.op(eng, lambda v, o=o, a=a, b=b, u=alu: v.tensor_tensor(o, a, b, u), inc=inc)

    def ts(self, eng, o, a, s1, s2, op0, op1, inc=None):
        return self.op(eng, lambda v, o=o, a=a, s1=s1, s2=s2, p0=op0, p1=op1:
                       v.tensor_scalar(o, a, s1, s2, p0, p1), inc=inc)

    def stt(self, eng, o, a, s, b, op0, op1, inc=None):
        return self.op(eng, lambda v, o=o, a=a, s=s, b=b, p0=op0, p1=op1:
                       v.scalar_tensor_tensor(o, a, s, b, p0, p1), inc=inc)

    def act(self, o, a, fname, bias=None, scale=None, accum=None, inc=None):
        def f(s, o=o, a=a, fn=fname, bi=bias, sc=scale, ac=accum):
            kw = {}
            if bi is not None:
                kw["bias"] = bi
            if sc is not None:
                kw["scale"] = sc
            if ac is not None:
                kw["accum_out"] = ac
            return s.activation(o, a, fn, **kw)
        return self.op("scalar", f, inc=inc)

    def copy(self, eng, o, a, inc=None):
        if eng == "scalar":
            return self.op(eng, lambda s, o=o, a=a: s.copy(o, a), inc=inc)
        return self.op(eng, lambda v, o=o, a=a: v.tensor_copy(o, a), inc=inc)

    def dma(self, eng, o, a, inc="ld", transpose=False):
        if inc not in self._h:
            self.dsem(inc)
        if transpose:
            return self.op(eng, lambda e, o=o, a=a: e.dma_start(out=o, in_=a, transpose=True),
                           inc=inc, inck=16, chain=False)
        return self.op(eng, lambda e, o=o, a=a: e.dma_start(out=o, in_=a), inc=inc, inck=16,
                       chain=False)

    def finalize(self):
        blk = self.es.enter_context(self.nc.Block())
        streams = self.streams

        @blk.tensor
        def _(t):
            for f in streams["tensor"]:
                f(t)

        @blk.vector
        def _(v):
            for f in streams["vector"]:
                f(v)

        @blk.scalar
        def _(s):
            for f in streams["scalar"]:
                f(s)

        @blk.gpsimd
        def _(g):
            for f in streams["gpsimd"]:
                f(g)

        @blk.sync
        def _(sy):
            for f in streams["sync"]:
                f(sy)


def sbt(ctx, es, name, shape, dtype):
    return es.enter_context(ctx.nc.sbuf_tensor(name, shape, dtype))


def pst(ctx, es, name, shape, dtype=F32):
    elems = 512 if dtype == F32 else 1024
    shape = [shape[0], max(shape[1], elems)]
    return es.enter_context(ctx.nc.psum_tensor(name, shape, dtype))


CONST_SPECS = {
    "ident_bf": ([128, 128], BF16),
    "ident_f32": ([128, 128], F32),
    "ones_row_bf": ([1, 128], BF16),
    "ones_row_f32": ([1, 128], F32),
    "ones_col_bf": ([128, 1], BF16),
    "mask_bd4n": ([128, 512], BF16),
    "mask_of4": ([128, 512], BF16),
    "mask_iU4": ([128, 512], BF16),
    "ones_t": ([128, 128], BF16),
    "ident4": ([128, 512], BF16),
    "diagmask": ([128, 4 * 512], BF16),
}


def host_consts():
    import ml_dtypes
    ii = np.arange(128)
    c = {}
    c["ident_bf"] = np.eye(128, dtype=np.float32)
    c["ident_f32"] = np.eye(128, dtype=np.float32)
    c["ones_row_bf"] = np.ones((1, 128), np.float32)
    c["ones_row_f32"] = np.ones((1, 128), np.float32)
    c["ones_col_bf"] = np.ones((128, 1), np.float32)
    c["mask_sL"] = (ii[:, None] > ii[None, :]).astype(np.float32)
    c["mask_iU"] = (ii[:, None] <= ii[None, :]).astype(np.float32)
    c["mask_bd"] = (((ii[:, None] // 32) == (ii[None, :] // 32))
                    & (ii[:, None] > ii[None, :])).astype(np.float32)
    c["mask_of"] = ((ii[:, None] // 32) > (ii[None, :] // 32)).astype(np.float32)
    c["mask_bd4n"] = np.tile(-c["mask_bd"], (1, 4))
    c["mask_of4"] = np.tile(c["mask_of"], (1, 4))
    c["mask_iU4"] = np.tile(c["mask_iU"], (1, 4))
    c["ones_t"] = np.ones((128, 128), np.float32)
    c["ident4"] = np.tile(np.eye(128, dtype=np.float32), (1, 4))
    dm = np.zeros((128, 4 * 512), np.float32)
    for j in range(4):
        dm[:, j * 512:(j + 1) * 512] = ((j * 128 + ii[:, None]) <= np.arange(512)[None, :])
    c["diagmask"] = dm
    out = {}
    for k, (shape, dt) in CONST_SPECS.items():
        a = np.ascontiguousarray(c[k].reshape(shape))
        out[k] = a.astype(ml_dtypes.bfloat16) if dt == BF16 else a.astype(np.float32)
    return out


def emit_load_consts(ctx, es, dr):
    C = {}
    for k, (shape, dt) in CONST_SPECS.items():
        C[k] = sbt(ctx, es, "c_" + k, shape, dt)
        ctx.dma("gpsimd", C[k][:], dr[k], inc="ldc")
    return C


def emit_ln_fm(ctx, es, C, r_fm, T, tag, sink):
    with ExitStack() as ph:
        sq = sbt(ctx, ph, f"lnsq{tag}", [128, 2 * 512], BF16)
        rows = sbt(ctx, ph, f"lnrw{tag}", [1, 6 * 512], F32)
        p_sum = pst(ctx, ph, f"lnps{tag}", [1, 512])
        p_sq = pst(ctx, ph, f"lnpq{tag}", [1, 512])
        p_a = pst(ctx, ph, f"lnpa{tag}", [128, 512])
        p_b = pst(ctx, ph, f"lnpb{tag}", [128, 512])

        def R(i):
            return rows[:, i * 512:(i + 1) * 512]

        applied = 0
        for g in range(T // 512):
            rsl = [r_fm[:, k * T + g * 512:k * T + g * 512 + 512] for k in range(NCH)]
            ctx.wait("tensor", "vec", applied)
            for k in range(NCH):
                ctx.mm(p_sum[:], C["ones_col_bf"][:], rsl[k], k == 0, k == NCH - 1, inc="mm")
            mmv = {}
            for k in range(NCH):
                if k >= 2:
                    ctx.wait("scalar", "mm", mmv[k - 2])
                sqs = sq[:, (k % 2) * 512:(k % 2) * 512 + 512]
                sv = ctx.act(sqs, rsl[k], AF.Square, inc="scl")
                ctx.wait("tensor", "scl", sv)
                mmv[k] = ctx.mm(p_sq[:], C["ones_col_bf"][:], sqs, k == 0, k == NCH - 1, inc="mm")
            ctx.wait("vector", "mm")
            ctx.ts("vector", R(0), p_sum[:], INV_D, 1.0, ALU.mult, ALU.mult)
            ctx.ts("vector", R(1), p_sq[:], INV_D, 1.0, ALU.mult, ALU.mult)
            ctx.tt("vector", R(4), R(0), R(0), ALU.mult)
            ctx.tt("vector", R(1), R(1), R(4), ALU.subtract)
            vv = ctx.ts("vector", R(1), R(1), 1.0, EPS_LN, ALU.mult, ALU.add, inc="vec")
            ctx.wait("scalar", "vec", vv)
            sv = ctx.act(R(4), R(1), AF.Sqrt, inc="scl")
            ctx.wait("vector", "scl", sv)
            ctx.op("vector", lambda v, o=R(2), a=R(4): v.reciprocal(o, a))
            ctx.tt("vector", R(3), R(0), R(2), ALU.mult)
            vv = ctx.ts("vector", R(3), R(3), -1.0, 1.0, ALU.mult, ALU.mult, inc="vec")
            ctx.wait("tensor", "vec", vv)
            ctx.mm(p_a[:], C["ones_row_f32"][:], R(2), True, True)
            bcv = ctx.mm(p_b[:], C["ones_row_f32"][:], R(3), True, True, inc="mm")
            ctx.wait("vector", "mm", bcv)
            for k in range(NCH):
                sink(g, k, rsl[k], p_a[:], p_b[:])
            applied = ctx.val("vec")


def ln_sink_simple(ctx, out_fm, T):
    def sink(g, k, rsl, a, b):
        osl = out_fm[:, k * T + g * 512:k * T + g * 512 + 512]
        ctx.tt("vector", osl, rsl, a, ALU.mult)
        ctx.tt("vector", osl, osl, b, ALU.add, inc="vec")
    return sink


def emit_l1ffn(ctx, es, C, x_fm, wg8, wu8, wd8, sg_t, su_t, sd_t, out_dram):
    with ExitStack() as ph:
        r_fm = sbt(ctx, ph, "f_r", [128, NCH * TOK], BF16)
        inner = ExitStack()
        fbuf = sbt(ctx, inner, "f_F", [128, 32 * 512], BF16)
        wgt = sbt(ctx, inner, "f_wg", [128, 2 * 16 * 512], BF16)
        wut = sbt(ctx, inner, "f_wu", [128, 2 * 16 * 512], BF16)
        wdt = sbt(ctx, inner, "f_wd", [128, 2 * 32 * 256], BF16)
        gtmp = sbt(ctx, inner, "f_gt", [128, 2 * 512], BF16)
        p_g = [pst(ctx, inner, f"f_pg{i}", [128, 512]) for i in range(2)]
        p_u = [pst(ctx, inner, f"f_pu{i}", [128, 512]) for i in range(2)]
        p_d = [pst(ctx, inner, f"f_pd{i}", [128, 512]) for i in range(2)]

        def colslice_dma(dst, src, c0, ncol, row0, nrow, sem, slot=0):
            sz = (nrow // 128) * ncol
            o = dst[:, slot * sz:(slot + 1) * sz].rearrange("p (c n) -> p c n", n=ncol)
            i = src[row0:row0 + nrow, c0:c0 + ncol].rearrange("(c p) n -> p c n", p=128)
            return ctx.dma("gpsimd", o, i, inc=sem)

        for th in range(2):
            tok0 = th * 512
            xsl = [x_fm[:, k * TOK + tok0:k * TOK + tok0 + 512] for k in range(NCH)]
            for hh in range(2):
                ctx.wait("vector", "mm")
                prev_mm = ctx.val("mm")
                fdv = {0: 0, 1: 0}
                sdv = {0: 0, 1: 0}
                wmm = {0: 0, 1: 0}
                wld = {}
                for hs in range(8):
                    ws = hs % 2
                    if hs == 0:
                        wld[0] = (colslice_dma(wgt, wg8, hh * 4096, 512, 0, D, "s_wg0", 0),
                                  colslice_dma(wut, wu8, hh * 4096, 512, 0, D, "s_wu0", 0))
                    if hs + 1 < 8:
                        wn = (hs + 1) % 2
                        ctx.wait("gpsimd", "mm", wmm[wn])
                        wld[hs + 1] = (
                            colslice_dma(wgt, wg8, hh * 4096 + (hs + 1) * 512, 512, 0, D,
                                         f"s_wg{wn}", wn),
                            colslice_dma(wut, wu8, hh * 4096 + (hs + 1) * 512, 512, 0, D,
                                         f"s_wu{wn}", wn))
                    ctx.wait("tensor", f"s_wg{ws}", wld[hs][0])
                    ctx.wait("tensor", f"s_wu{ws}", wld[hs][1])
                    for hc in range(4):
                        sl_ = hc % 2
                        hci = hh * 32 + hs * 4 + hc
                        fi = hs * 4 + hc
                        fsl = fbuf[:, fi * 512:fi * 512 + 512]
                        gsl = gtmp[:, sl_ * 512:sl_ * 512 + 512]
                        ctx.wait("tensor", "vec", fdv[sl_])
                        ctx.wait("tensor", "scl", sdv[sl_])
                        wbase = ws * 8192
                        mg = mu = None
                        for k in range(NCH):
                            mg = ctx.mm(p_g[sl_][:],
                                        wgt[:, wbase + (k * 4 + hc) * 128:wbase + (k * 4 + hc) * 128 + 128],
                                        xsl[k], k == 0, k == NCH - 1,
                                        inc="mm" if k == NCH - 1 else None)
                        for k in range(NCH):
                            mu = ctx.mm(p_u[sl_][:],
                                        wut[:, wbase + (k * 4 + hc) * 128:wbase + (k * 4 + hc) * 128 + 128],
                                        xsl[k], k == 0, k == NCH - 1,
                                        inc="mm" if k == NCH - 1 else None)
                        prev_mm = mu
                        wmm[ws] = mu
                        ctx.wait("vector", "mm", mg)
                        gv = ctx.ts("vector", gsl, p_g[sl_][:], sg_t[:, hci:hci + 1], 1.0,
                                    ALU.mult, ALU.mult, inc="vec")
                        ctx.wait("scalar", "vec", gv)
                        sv = ctx.act(gsl, gsl, AF.Silu, inc="scl")
                        sdv[sl_] = sv
                        ctx.wait("vector", "mm", mu)
                        ctx.ts("vector", fsl, p_u[sl_][:], su_t[:, hci:hci + 1], 1.0,
                               ALU.mult, ALU.mult)
                        ctx.wait("vector", "scl", sv)
                        fdv[sl_] = ctx.tt("vector", fsl, fsl, gsl, ALU.mult, inc="vec")
                ctx.wait("vector", "scl")
                fdone = ctx.val("vec")
                prev_dmm = ctx.val("mm")
                ddv = {0: 0, 1: 0}
                dmm = {0: 0, 1: 0}
                dld = {}
                for ds in range(8):
                    if ds == 0:
                        dld[0] = colslice_dma(wdt, wd8, 0, 256, hh * 4096, 4096, "s_wd0", 0)
                    if ds + 1 < 8:
                        dn_ = (ds + 1) % 2
                        ctx.wait("gpsimd", "mm", dmm[dn_])
                        dld[ds + 1] = colslice_dma(wdt, wd8, (ds + 1) * 256, 256,
                                                   hh * 4096, 4096, f"s_wd{dn_}", dn_)
                    ctx.wait("tensor", f"s_wd{ds % 2}", dld[ds])
                    ctx.wait("tensor", "vec", fdone)
                    for dc in range(2):
                        sl_ = dc % 2
                        dci = ds * 2 + dc
                        pd = p_d[sl_]
                        gsl = gtmp[:, sl_ * 512:sl_ * 512 + 512]
                        ctx.wait("tensor", "vec", ddv[sl_])
                        dbase = (ds % 2) * 8192
                        mv = None
                        for hc in range(32):
                            mv = ctx.mm(pd[:],
                                        wdt[:, dbase + (hc * 2 + dc) * 128:dbase + (hc * 2 + dc) * 128 + 128],
                                        fbuf[:, hc * 512:hc * 512 + 512], hc == 0, hc == 31,
                                        inc="mm" if hc == 31 else None)
                        prev_dmm = mv
                        dmm[ds % 2] = mv
                        ctx.wait("vector", "mm", mv)
                        rsl = r_fm[:, dci * TOK + tok0:dci * TOK + tok0 + 512]
                        ctx.ts("vector", gsl, pd[:], sd_t[:, dci:dci + 1], 1.0,
                               ALU.mult, ALU.mult)
                        if hh == 0:
                            ddv[sl_] = ctx.tt("vector", rsl, gsl, xsl[dci], ALU.add, inc="vec")
                        else:
                            ddv[sl_] = ctx.tt("vector", rsl, rsl, gsl, ALU.add, inc="vec")
        ctx.barrier()
        inner.close()
        y32 = sbt(ctx, ph, "f_y32", [128, NCH * 512], F32)
        ytm = sbt(ctx, ph, "f_ytm", [128, 2048], F32)
        p_t = [pst(ctx, ph, f"f_pt{i}", [128, 512]) for i in range(2)]

        def sink(g, k, rsl, a, b):
            osl = y32[:, k * 512:k * 512 + 512]
            ctx.tt("vector", osl, rsl, a, ALU.mult)
            ctx.tt("vector", osl, osl, b, ALU.add, inc="vec")
            if k == NCH - 1:
                _store_group(ctx, C, y32, ytm, p_t, out_dram, g)

        emit_ln_fm(ctx, ph, C, r_fm, TOK, "f", sink)
        ctx.wait("sync", "st")


def _store_group(ctx, C, y32, ytm, p_t, out_dram, g):
    yv = ctx.val("vec")
    ctx.wait("tensor", "vec", yv)
    evv = {}
    for tc in range(4):
        stv = ctx.val("st")
        for k in range(NCH):
            i = tc * NCH + k
            pt = p_t[i % 2]
            if i >= 2:
                ctx.wait("tensor", "vec", evv[i - 2])
            mv = ctx.trans(pt[:, 0:128], y32[:, k * 512 + tc * 128:k * 512 + tc * 128 + 128],
                           C["ident_f32"][:], inc="mm")
            ctx.wait("vector", "mm", mv)
            if k == 0 and stv > 0:
                ctx.wait("vector", "st", stv)
            evv[i] = ctx.copy("vector", ytm[:, k * 128:k * 128 + 128], pt[:, 0:128], inc="vec")
        ev = ctx.val("vec")
        ctx.wait("sync", "vec", ev)
        row0 = g * 512 + tc * 128
        ctx.dma("sync", out_dram[row0:row0 + 128, :], ytm[:], inc="st")


def kernel(**inputs):
    raise NotImplementedError("full build assembled incrementally; see scratch drivers")


def emit_moe(ctx, es, C, x_fm, gate_w_s, ltmask, wg_dr, wu_dr, wd_dr,
             swg_dr, swu_dr, swd_dr, dbg_comb=None, r_fm=None):
    if r_fm is None:
        r_fm = sbt(ctx, es, "m_r", [128, NCH * TOK], BF16)
    with ExitStack() as ph:
        comb_t = sbt(ctx, ph, "m_combt", [8, 8 * 128], F32)
        comb_r = sbt(ctx, ph, "m_combr", [1, 8 * 8 * 128], BF16)
        inner = ExitStack()
        with ExitStack() as rp:
            p_r = pst(ctx, rp, "m_pr", [128, 512])
            p_tr = pst(ctx, rp, "m_ptr", [128, 512])
            sc = sbt(ctx, rp, "m_sc", [128, 8], F32)
            rk = sbt(ctx, rp, "m_rk", [128, 8], F32)
            t8 = sbt(ctx, rp, "m_t8", [128, 8], F32)
            mrow = sbt(ctx, rp, "m_mrow", [128, 2], F32)
            for tc in range(8):
                xch = [x_fm[:, k * TOK + tc * 128:k * TOK + tc * 128 + 128] for k in range(NCH)]
                ctx.wait("tensor", "vec")
                ctx.wait("tensor", "scl")
                mv = None
                for k in range(NCH):
                    mv = ctx.mm(p_r[:, 0:8], xch[k], gate_w_s[:, k * 8:k * 8 + 8],
                                k == 0, k == NCH - 1, inc="mm" if k == NCH - 1 else None)
                ctx.wait("vector", "mm", mv)
                ctx.op("vector", lambda v, o=mrow[:, 0:1], a=p_r[:, 0:8]:
                       v.reduce_max(o, a, axis=AX.X))
                ctx.ts("vector", mrow[:, 1:2], mrow[:, 0:1], -1.0, 1.0, ALU.mult, ALU.mult,
                       inc="vec")
                ctx.wait("scalar", "vec")
                sv = ctx.act(sc[:], p_r[:, 0:8], AF.Exp, bias=mrow[:, 1:2], inc="scl")
                ctx.wait("vector", "scl", sv)
                ctx.op("vector", lambda v, o=mrow[:, 0:1], a=sc[:]:
                       v.reduce_sum(o, a, axis=AX.X))
                ctx.op("vector", lambda v, o=mrow[:, 1:2], a=mrow[:, 0:1]: v.reciprocal(o, a))
                ctx.ts("vector", sc[:], sc[:], mrow[:, 1:2], 1.0, ALU.mult, ALU.mult)
                ctx.ts("vector", rk[:], sc[:], sc[:, 0:1], 0.0, ALU.is_lt, ALU.add)
                for j in range(1, 8):
                    ctx.ts("vector", t8[:], sc[:], sc[:, j:j + 1], 1.0, ALU.is_lt, ALU.mult)
                    ctx.tt("vector", rk[:], rk[:], t8[:], ALU.add)
                for j in range(7):
                    ctx.ts("vector", t8[:], sc[:], sc[:, j:j + 1], 1.0, ALU.is_equal, ALU.mult)
                    ctx.tt("vector", t8[:], t8[:], ltmask[:, j * 8:j * 8 + 8], ALU.mult)
                    ctx.tt("vector", rk[:], rk[:], t8[:], ALU.add)
                ctx.ts("vector", rk[:], rk[:], 2.0, 1.0, ALU.is_lt, ALU.mult)
                cv = ctx.tt("vector", sc[:], sc[:], rk[:], ALU.mult, inc="vec")
                ctx.wait("tensor", "vec", cv)
                tv = ctx.trans(p_tr[0:8, 0:128], sc[:], C["ident_f32"][:], inc="mm")
                ctx.wait("vector", "mm", tv)
                ctx.copy("vector", comb_t[:, tc * 128:tc * 128 + 128], p_tr[0:8, 0:128],
                         inc="vec")
            ctx.wait("gpsimd", "vec")
            for e_ in range(8):
                ctx.dma("gpsimd", comb_r[:, e_ * 1024:(e_ + 1) * 1024],
                        comb_t[e_:e_ + 1, :], inc="m_cr")
            ctx.barrier()
            ctx.wait("tensor", "m_cr")
            if dbg_comb is not None:
                ctx.dma("gpsimd", dbg_comb, comb_r[:], inc="m_dbg")
                ctx.wait("gpsimd", "m_dbg")
        wgt = sbt(ctx, inner, "m_wg", [128, 2 * 16 * 512], BF16)
        wut = sbt(ctx, inner, "m_wu", [128, 2 * 16 * 512], BF16)
        wdt = sbt(ctx, inner, "m_wd", [128, 2 * 11 * 512], BF16)
        gbuf = sbt(ctx, inner, "m_G", [128, 11 * TOK], BF16)
        gtmp = sbt(ctx, inner, "m_gt", [128, 2 * 512], BF16)
        p_g = [pst(ctx, inner, f"m_pg{i}", [128, 512]) for i in range(2)]
        p_u = [pst(ctx, inner, f"m_pu{i}", [128, 512]) for i in range(2)]
        p_cb = [pst(ctx, inner, f"m_pcb{i}", [128, 512]) for i in range(2)]
        p_d = [pst(ctx, inner, f"m_pd{i}", [128, 512]) for i in range(2)]

        def colslice(dst, slot, slotsz, src, r0, nr, c0, ncol, sem):
            o = dst[:, slot * slotsz:slot * slotsz + (nr // 128) * ncol]
            o = o.rearrange("p (c n) -> p c n", n=ncol)
            i = src[r0:r0 + nr, c0:c0 + ncol].rearrange("(c p) n -> p c n", p=128)
            return ctx.dma("gpsimd", o, i, inc=sem)

        moe_wld = {}
        moe_wmm = {0: 0, 1: 0}
        for e in range(10):
            routed = e < 8
            Hh = 1408
            nhc = Hh // 128
            supers = [(0, 512), (512, 512), (1024, 384)]
            if routed:
                g_src = wg_dr[e * 2048:(e + 1) * 2048, :]
                u_src = wu_dr[e * 2048:(e + 1) * 2048, :]
                d_src = wd_dr[e * 1408:(e + 1) * 1408, :]
            else:
                sh = (e - 8) * 1408
                g_src = swg_dr[:, sh:sh + 1408]
                u_src = swu_dr[:, sh:sh + 1408]
                d_src = swd_dr[sh:sh + 1408, :]
            if routed:
                ctx.wait("tensor", "vec")
                for half in range(2):
                    ctx.mm(p_cb[half][:], C["ones_row_bf"][:],
                           comb_r[:, e * 1024 + half * 512:e * 1024 + half * 512 + 512],
                           True, True, inc="mm")
            gdv = {0: ctx.val("vec"), 1: ctx.val("vec")}
            sdv = {0: ctx.val("scl"), 1: ctx.val("scl")}
            for si, (h0, w) in enumerate(supers):
                gi0 = e * 3 + si
                slot = gi0 % 2
                if gi0 == 0:
                    moe_wld[0] = (
                        colslice(wgt, 0, 16 * 512, g_src, 0, 2048, h0, w, "m_swg0"),
                        colslice(wut, 0, 16 * 512, u_src, 0, 2048, h0, w, "m_swu0"))
                nsi = gi0 + 1
                if nsi < 30:
                    ne, nsl = nsi // 3, nsi % 3
                    nh0, nw = supers[nsl]
                    if ne < 8:
                        ng = wg_dr[ne * 2048:(ne + 1) * 2048, :]
                        nu = wu_dr[ne * 2048:(ne + 1) * 2048, :]
                    else:
                        nsh = (ne - 8) * 1408
                        ng = swg_dr[:, nsh:nsh + 1408]
                        nu = swu_dr[:, nsh:nsh + 1408]
                    ctx.wait("gpsimd", "mm", moe_wmm[nsi % 2])
                    moe_wld[nsi] = (
                        colslice(wgt, nsi % 2, 16 * 512, ng, 0, 2048, nh0, nw,
                                 f"m_swg{nsi % 2}"),
                        colslice(wut, nsi % 2, 16 * 512, nu, 0, 2048, nh0, nw,
                                 f"m_swu{nsi % 2}"))
                ctx.wait("tensor", f"m_swg{slot}", moe_wld[gi0][0])
                ctx.wait("tensor", f"m_swu{slot}", moe_wld[gi0][1])
                for hc in range(w // 128):
                    gi = h0 // 128 + hc
                    for half in range(2):
                        sl_ = (hc * 2 + half) % 2
                        xsl = [x_fm[:, k * TOK + half * 512:k * TOK + half * 512 + 512]
                               for k in range(NCH)]
                        gsl = gbuf[:, gi * TOK + half * 512:gi * TOK + half * 512 + 512]
                        gts = gtmp[:, sl_ * 512:sl_ * 512 + 512]
                        ctx.wait("tensor", "vec", gdv[sl_])
                        ctx.wait("tensor", "scl", sdv[sl_])
                        mg = mu = None
                        for k in range(NCH):
                            off = slot * 16 * 512 + (k * (w // 128) + hc) * 128
                            mg = ctx.mm(p_g[sl_][:], wgt[:, off:off + 128], xsl[k],
                                        k == 0, k == NCH - 1,
                                        inc="mm" if k == NCH - 1 else None)
                        for k in range(NCH):
                            off = slot * 16 * 512 + (k * (w // 128) + hc) * 128
                            mu = ctx.mm(p_u[sl_][:], wut[:, off:off + 128], xsl[k],
                                        k == 0, k == NCH - 1,
                                        inc="mm" if k == NCH - 1 else None)
                        moe_wmm[slot] = mu
                        ctx.wait("scalar", "mm", mg)
                        sv = ctx.act(gts, p_g[sl_][:], AF.Silu, inc="scl")
                        sdv[sl_] = sv
                        ctx.wait("vector", "mm", mu)
                        ctx.wait("vector", "scl", sv)
                        if routed:
                            ctx.tt("vector", gsl, gts, p_u[sl_][:], ALU.mult)
                            gdv[sl_] = ctx.tt("vector", gsl, gsl, p_cb[half][:], ALU.mult,
                                              inc="vec")
                        else:
                            gdv[sl_] = ctx.tt("vector", gsl, gts, p_u[sl_][:], ALU.mult,
                                              inc="vec")
            gdone = ctx.val("vec")
            dn_slot = {0: ctx.val("mm"), 1: ctx.val("mm")}
            pdv = {0: ctx.val("vec"), 1: ctx.val("vec")}
            dslotsz = 11 * 512
            for ds in range(4):
                slot = ds % 2
                ctx.wait("gpsimd", "mm", dn_slot[slot])
                d_ld = colslice(wdt, slot, dslotsz, d_src, 0, Hh, ds * 512, 512, f"m_swd{slot}")
                ctx.wait("tensor", f"m_swd{slot}", d_ld)
                ctx.wait("tensor", "vec", gdone)
                for dc in range(4):
                    dci = ds * 4 + dc
                    for half in range(2):
                        sl_ = (dc * 2 + half) % 2
                        pd = p_d[sl_]
                        ctx.wait("tensor", "vec", pdv[sl_])
                        mv = None
                        for hc in range(nhc):
                            off = slot * dslotsz + hc * 512 + dc * 128
                            mv = ctx.mm(pd[:], wdt[:, off:off + 128],
                                        gbuf[:, hc * TOK + half * 512:hc * TOK + half * 512 + 512],
                                        hc == 0, hc == nhc - 1,
                                        inc="mm" if hc == nhc - 1 else None)
                        dn_slot[slot] = mv
                        ctx.wait("vector", "mm", mv)
                        rsl = r_fm[:, dci * TOK + half * 512:dci * TOK + half * 512 + 512]
                        if e == 0:
                            pdv[sl_] = ctx.tt("vector", rsl, pd[:],
                                              x_fm[:, dci * TOK + half * 512:dci * TOK + half * 512 + 512],
                                              ALU.add, inc="vec")
                        else:
                            pdv[sl_] = ctx.tt("vector", rsl, rsl, pd[:], ALU.add, inc="vec")
        ctx.barrier()
        inner.close()
        emit_ln_fm(ctx, ph, C, r_fm, TOK, "m", ln_sink_simple(ctx, r_fm, TOK))
        ctx.barrier()
    return r_fm


DN_GROUPS = 8
DN_CHUNKS = 4
DN_HORNER = 3
MHA_STAGE = 4


def emit_mha(ctx, es, C, xsrc, wq_dr, wk_dr, wv_dr, cc3_dr, pre_x=None, half_done=None):
    RT2 = 1.0 / math.sqrt(128.0)
    with ExitStack() as ph:
        wq_s = sbt(ctx, ph, "a_wq", [128, 16 * 512], BF16)
        wk_s = sbt(ctx, ph, "a_wk", [128, 16 * 512], BF16)
        wv_s = sbt(ctx, ph, "a_wv", [128, 16 * 512], BF16)
        q_fm = sbt(ctx, ph, "a_qfm", [128, 4 * 512], BF16)
        k_fm = sbt(ctx, ph, "a_kfm", [128, 4 * SEQ], BF16)
        v_tm = sbt(ctx, ph, "a_vtm", [128, 4 * SEQ], BF16)
        xg = sbt(ctx, ph, "a_xg", [128, 2 * 16 * 512], BF16)
        cosx = sbt(ctx, ph, "a_cos", [128, 32 * 128], BF16)
        sinx = sbt(ctx, ph, "a_sin", [128, 32 * 128], BF16)
        qrp = sbt(ctx, ph, "a_qrp", [128, 512], BF16)
        krp = sbt(ctx, ph, "a_krp", [128, 512], BF16)
        tmpab = sbt(ctx, ph, "a_tab", [128, 256], BF16)
        sqb = sbt(ctx, ph, "a_sqb", [128, 512], BF16)
        ptb = sbt(ctx, ph, "a_ptb", [128, 4 * 512], BF16)
        exps = sbt(ctx, ph, "a_exps", [128, 4 * 512], F32)
        obuf = sbt(ctx, ph, "a_obuf", [128, 512], BF16)
        mb_sb = sbt(ctx, ph, "a_mbsb", [128, 512], BF16)
        rows = sbt(ctx, ph, "a_rows", [1, 3 * 512], F32)
        qnr = sbt(ctx, ph, "a_qnr", [1, 4 * 512], BF16)
        runmax = sbt(ctx, ph, "a_rmx", [1, 8], F32)
        lrow = sbt(ctx, ph, "a_lrow", [1, 512], BF16)
        p_q = pst(ctx, ph, "a_pq", [128, 512])
        p_k = pst(ctx, ph, "a_pk", [128, 512])
        p_v = pst(ctx, ph, "a_pv", [128, 512])
        p_st = pst(ctx, ph, "a_pst", [128, 512])
        p_mb = pst(ctx, ph, "a_pmb", [128, 512])
        p_o = pst(ctx, ph, "a_po", [128, 512])
        p_l = pst(ctx, ph, "a_pl", [1, 512])

        def R(i):
            return rows[:, i * 512:(i + 1) * 512]

        ctx.dma("gpsimd", wq_s[:].rearrange("p (c n) -> p c n", n=512),
                wq_dr.rearrange("(c p) n -> p c n", p=128), inc="a_w")
        ctx.dma("gpsimd", wk_s[:].rearrange("p (c n) -> p c n", n=512),
                wk_dr.rearrange("(c p) n -> p c n", p=128), inc="a_w")
        wv_v = ctx.dma("gpsimd", wv_s[:].rearrange("p (c n) -> p c n", n=512),
                       wv_dr.rearrange("(c p) n -> p c n", p=128), inc="a_w")
        with ExitStack() as tp:
            itr = sbt(ctx, tp, "a_itr", [1, 128], F32)
            thr = sbt(ctx, tp, "a_thr", [1, 64], F32)
            ang = sbt(ctx, tp, "a_ang", [128, 64], F32)
            red = sbt(ctx, tp, "a_red", [128, 64], F32)
            ki = sbt(ctx, tp, "a_ki", [128, 64], I32)

            gv = ctx.op("gpsimd", lambda g, o=thr[:]: g.iota(
                o, pattern=[[1, 64]], base=0, channel_multiplier=0,
                allow_small_or_imprecise_dtypes=True), inc="gp")
            ctx.wait("scalar", "gp", gv)
            sv = ctx.act(thr[:], thr[:], AF.Exp, scale=float(-math.log(10000.0) / 64.0),
                         inc="scl")
            ctx.wait("tensor", "gp", gv)
            ctx.wait("tensor", "scl", sv)
            ang_mv = 0
            for tc in range(32):
                if ang_mv:
                    ctx.wait("gpsimd", "mm", ang_mv)
                gv = ctx.op("gpsimd", lambda g, o=itr[:], b=tc * 128: g.iota(
                    o, pattern=[[1, 128]], base=b, channel_multiplier=0,
                    allow_small_or_imprecise_dtypes=True), inc="gp")
                ctx.wait("tensor", "gp", gv)
                mv = ctx.mm(p_st[:, 0:64], itr[:], thr[:],
                            True, True, inc="mm")
                ang_mv = mv
                ctx.wait("vector", "mm", mv)
                for half, shift in ((0, 0.0), (1, 0.5 * math.pi)):
                    ctx.ts("vector", ang[:], p_st[:, 0:64], 1.0, shift, ALU.mult, ALU.add)
                    ctx.ts("vector", red[:], ang[:], float(1.0 / (2 * math.pi)), 1.0,
                           ALU.mult, ALU.mult)
                    ctx.copy("vector", ki[:], red[:])
                    ctx.copy("vector", red[:], ki[:])
                    ctx.stt("vector", ang[:], red[:], float(-2 * math.pi), ang[:],
                            ALU.mult, ALU.add)
                    ctx.ts("vector", red[:], ang[:], float(math.pi), 1.0, ALU.is_gt, ALU.mult)
                    vv = ctx.stt("vector", ang[:], red[:], float(-2 * math.pi), ang[:],
                                 ALU.mult, ALU.add, inc="vec")
                    ctx.wait("scalar", "vec", vv)
                    dst = sinx if half == 0 else cosx
                    s0 = ctx.act(dst[:, tc * 128:tc * 128 + 127:2], ang[:], AF.Sin,
                                 inc="scl")
                    s1 = ctx.act(dst[:, tc * 128 + 1:tc * 128 + 128:2], ang[:], AF.Sin,
                                 inc="scl")
                    ctx.wait("vector", "scl", s1)
                ctx.wait("tensor", "vec")
        ctx.wait("tensor", "a_w", wv_v)
        ctx.wait("vector", "a_w", wv_v)

        ctx.op("vector", lambda v, o=runmax[:]: v.memset(o, 0.0), inc="vec")

        xg_mm = {0: 0, 1: 0}
        for g in range(8 if MHA_STAGE >= 2 else 0):
            slot = g % 2
            if g == 0 and pre_x is not None:
                pre_x()
            ctx.wait("gpsimd", "mm", xg_mm[slot])
            xv = ctx.dma("gpsimd",
                         xg[:, slot * 8192:slot * 8192 + 8192].rearrange(
                             "p (c n) -> p c n", n=512),
                         xsrc(g).rearrange("(c p) n -> p c n", p=128), inc=f"a_x{slot}")
            ctx.wait("tensor", f"a_x{slot}", xv)
            for tc4 in range(4):
                tc = g * 4 + tc4
                xsl = [xg[:, slot * 8192 + k * 512 + tc4 * 128:
                          slot * 8192 + k * 512 + tc4 * 128 + 128] for k in range(NCH)]
                ctx.wait("tensor", "vec")
                mq = mk = mvv = None
                for k in range(NCH):
                    mq = ctx.mm(p_q[:], xsl[k], wq_s[:, k * 512:k * 512 + 512],
                                k == 0, k == NCH - 1, inc="mm" if k == NCH - 1 else None)
                for k in range(NCH):
                    mk = ctx.mm(p_k[:], xsl[k], wk_s[:, k * 512:k * 512 + 512],
                                k == 0, k == NCH - 1, inc="mm" if k == NCH - 1 else None)
                for k in range(NCH):
                    mvv = ctx.mm(p_v[:], xsl[k], wv_s[:, k * 512:k * 512 + 512],
                                 k == 0, k == NCH - 1, inc="mm" if k == NCH - 1 else None)
                xg_mm[slot] = mvv
                ctx.wait("vector", "mm", mvv)
                ctx.copy("vector", v_tm[:, tc * 512:tc * 512 + 512], p_v[:], inc="vec")
                if MHA_STAGE < 2.3:
                    continue
                ctx.wait("vector", "mm")
                cose = cosx[:, tc * 128:tc * 128 + 128]
                sine = sinx[:, tc * 128:tc * 128 + 128]
                for src_ps, dstbuf in ((p_q, qrp), (p_k, krp)):
                    for h in range(4):
                        psl = src_ps[:, h * 128:h * 128 + 128]
                        dsl = dstbuf[:, h * 128:h * 128 + 128]
                        ctx.tt("vector", tmpab[:, 0:128], psl, cose, ALU.mult)
                        ctx.tt("vector", tmpab[:, 128:256], psl, sine, ALU.mult)
                        ctx.tt("vector", dstbuf[:, h * 128:h * 128 + 127:2],
                               tmpab[:, 0:127:2], tmpab[:, 129:256:2], ALU.subtract)
                        ctx.tt("vector", dstbuf[:, h * 128 + 1:h * 128 + 128:2],
                               tmpab[:, 1:128:2], tmpab[:, 128:255:2], ALU.add,
                               inc="vec")
                if MHA_STAGE < 2.6:
                    continue
                ctx.wait("tensor", "vec")
                ptr_v = [p_q[:].bitcast(BF16), p_k[:].bitcast(BF16)]
                evv = {}
                for i, (srcbuf, dstfm) in enumerate(((qrp, q_fm), (krp, k_fm))):
                    for h in range(4):
                        j = i * 4 + h
                        if j >= 2:
                            ctx.wait("tensor", "vec", evv[j - 2])
                        tv = ctx.trans(ptr_v[j % 2][:, 0:128],
                                       srcbuf[:, h * 128:h * 128 + 128],
                                       C["ident_bf"][:], inc="mm")
                        ctx.wait("vector", "mm", tv)
                        dcol = (h * 512 + tc4 * 128) if i == 0 else (h * SEQ + tc * 128)
                        evv[j] = ctx.copy(
                            "vector",
                            dstfm[:, dcol:dcol + 128],
                            ptr_v[j % 2][:, 0:128], inc="vec")
            if MHA_STAGE < 3:
                continue
            for h in range(4):
                qw = h * 512
                kw = h * SEQ + g * 512
                kv = ctx.tt("vector", sqb[:], k_fm[:, kw:kw + 512], k_fm[:, kw:kw + 512],
                            ALU.mult, inc="vec")
                ctx.wait("tensor", "vec", kv)
                mv = ctx.mm(p_l[:], C["ones_col_bf"][:], sqb[:], True, True, inc="mm")
                ctx.wait("vector", "mm", mv)
                ctx.op("vector", lambda v, o=R(0)[:, 0:1], a=p_l[:, 0:512]:
                       v.reduce_max(o, a, axis=AX.X))
                ctx.tt("vector", runmax[:, h:h + 1], runmax[:, h:h + 1], R(0)[:, 0:1],
                       ALU.max, inc="vec")
                qv = ctx.tt("vector", sqb[:], q_fm[:, qw:qw + 512], q_fm[:, qw:qw + 512],
                            ALU.mult, inc="vec")
                ctx.wait("tensor", "vec", qv)
                mv = ctx.mm(p_l[:], C["ones_col_bf"][:], sqb[:], True, True, inc="mm")
                ctx.wait("scalar", "mm", mv)
                sv = ctx.act(R(1), p_l[:, 0:512], AF.Ln, inc="scl")
                sv = ctx.act(qnr[:, h * 512:h * 512 + 512], R(1), AF.Exp, scale=0.5,
                             inc="scl")
                ctx.wait("vector", "scl", sv)
            if MHA_STAGE < 4:
                continue
            sv0 = ctx.val("vec")
            ctx.wait("scalar", "vec", sv0)
            sv = ctx.act(runmax[:, 4:8], runmax[:, 0:4], AF.Ln, inc="scl")
            sv = ctx.act(runmax[:, 4:8], runmax[:, 4:8], AF.Exp, scale=0.5, inc="scl")
            ctx.wait("vector", "scl", sv)
            for h in range(4):
                qw = h * 512
                mrv = ctx.ts("vector", R(2), qnr[:, h * 512:h * 512 + 512],
                             runmax[:, 4 + h:5 + h], RT2, ALU.mult, ALU.mult, inc="vec")
                ctx.wait("tensor", "vec", mrv)
                mbv = ctx.mm(p_mb[:], C["ones_row_f32"][:], R(2), True, True, inc="mm")
                ctx.wait("vector", "mm", mbv)
                ctx.copy("vector", mb_sb[:], p_mb[:], inc="vec")
                nk = 4 * (g + 1)
                lv = None
                stslots = [p_st, p_q, p_k, p_v]
                NS = 4
                stv_s = {i: 0 for i in range(NS)}
                rdy = {}
                ptv = {i: 0 for i in range(NS)}
                svv = {i: 0 for i in range(NS)}
                pvv = {i: 0 for i in range(NS)}
                LAG = 2
                for kc in range(nk + LAG):
                    if kc < nk:
                        ps = kc % NS
                        if ptv[ps]:
                            ctx.wait("tensor", "vec", ptv[ps])
                        stv_s[ps] = ctx.mm(
                            stslots[ps][:],
                            k_fm[:, h * SEQ + kc * 128:h * SEQ + kc * 128 + 128],
                            q_fm[:, qw:qw + 512], True, True, inc="mm")
                    if kc >= LAG:
                        kp = kc - LAG
                        pc = kp % NS
                        ctx.wait("tensor", "vec" if (kp - 4 * g) >= 0 else "scl",
                                 rdy[kp])
                        ctx.mm(p_o[:], v_tm[:, (kp * 4 + h) * 128:
                                              (kp * 4 + h) * 128 + 128],
                               ptb[:, pc * 512:pc * 512 + 512], kp == 0,
                               kp == nk - 1, inc="mm")
                        lv = ctx.mm(p_l[:], C["ones_col_bf"][:],
                                    ptb[:, pc * 512:pc * 512 + 512], kp == 0,
                                    kp == nk - 1, inc="mm")
                        pvv[pc] = lv
                    if kc < nk:
                        ps = kc % NS
                        ctx.wait("vector", "mm", stv_s[ps])
                        if svv[ps]:
                            ctx.wait("vector", "scl", svv[ps])
                        ev = ctx.stt("vector", exps[:, ps * 512:ps * 512 + 512],
                                     stslots[ps][:], RT2, mb_sb[:], ALU.mult,
                                     ALU.subtract, inc="vec")
                        ptv[ps] = ev
                        ctx.wait("scalar", "vec", ev)
                        if pvv[ps]:
                            ctx.wait("scalar", "mm", pvv[ps])
                        sv = ctx.act(ptb[:, ps * 512:ps * 512 + 512],
                                     exps[:, ps * 512:ps * 512 + 512], AF.Exp, inc="scl")
                        svv[ps] = sv
                        rdy[kc] = sv
                        dj = kc - 4 * g
                        if dj >= 0:
                            ctx.wait("vector", "scl", sv)
                            rdy[kc] = ctx.tt("vector", ptb[:, ps * 512:ps * 512 + 512],
                                             ptb[:, ps * 512:ps * 512 + 512],
                                             C["diagmask"][:, dj * 512:dj * 512 + 512],
                                             ALU.mult, inc="vec")
                ctx.wait("vector", "mm", lv)
                ov = ctx.copy("vector", obuf[:], p_o[:], inc="vec")
                lvv = ctx.copy("vector", lrow[:], p_l[:, 0:512], inc="vec")
                ctx.wait("sync", "vec", lvv)
                halfd = cc3_dr[0] if g < 4 else cc3_dr[1]
                r0 = ((g // 2) % 2) * 516
                c0 = (g % 2) * 512
                ctx.dma("sync", halfd[r0 + h * 128:r0 + h * 128 + 128, c0:c0 + 512],
                        obuf[:], inc="a_st")
                sov = ctx.dma("sync", halfd[r0 + 512 + h:r0 + 513 + h, c0:c0 + 512],
                              lrow[:], inc="a_st")
                ctx.wait("vector", "a_st", sov)
            if half_done is not None and g in (3, 7):
                half_done(g // 4)
        ctx.barrier()


def emit_deltanet(ctx, es, C, xsrc, wq_dr, wk_dr, wv_dr, wb_dr, wg1_dr, wg2_dr,
                  gnf_dr, cc1_dr, half_done=None):
    with ExitStack() as ph:
        wq_s = sbt(ctx, ph, "d_wq", [128, 16 * 512], BF16)
        wk_s = sbt(ctx, ph, "d_wk", [128, 16 * 512], BF16)
        wv_s = sbt(ctx, ph, "d_wv", [128, 16 * 512], BF16)
        wb_s = sbt(ctx, ph, "d_wb", [128, 16 * 4], BF16)
        wg1_s = sbt(ctx, ph, "d_wg1", [128, 16 * 128], BF16)
        wg2_s = sbt(ctx, ph, "d_wg2", [128, 512], BF16)
        gnf4 = sbt(ctx, ph, "d_gnf", [128, 512], F32)
        xg = sbt(ctx, ph, "d_xg", [128, 2 * 16 * 512], BF16)
        qf = sbt(ctx, ph, "d_qf", [128, 2 * 2048], BF16)
        kf = sbt(ctx, ph, "d_kf", [128, 2 * 2048], BF16)
        kt = sbt(ctx, ph, "d_kt", [128, 2 * 2048], BF16)
        vt = sbt(ctx, ph, "d_vt", [128, 2 * 2048], BF16)
        gate = sbt(ctx, ph, "d_gate", [128, 2 * 2048], BF16)
        xg1 = sbt(ctx, ph, "d_xg1", [128, 2 * 512], BF16)
        bt = sbt(ctx, ph, "d_bt", [128, 2 * 16], F32)
        ksum = sbt(ctx, ph, "d_ksum", [128, 2 * 16], F32)
        qg_all = sbt(ctx, ph, "d_qga", [128, 2 * 2048], BF16)
        ktm_g = sbt(ctx, ph, "d_ktmg", [128, 2048], F32)
        ogb = sbt(ctx, ph, "d_ogb", [128, 2048], BF16)
        sgt = sbt(ctx, ph, "d_sgt", [128, 512], F32)
        rowb = sbt(ctx, ph, "d_rowb", [1, 2048], F32)
        nb4 = sbt(ctx, ph, "d_nb", [128, 512], BF16)
        ntb4 = sbt(ctx, ph, "d_ntb", [128, 2 * 512], BF16)
        mqb4 = sbt(ctx, ph, "d_mqb", [128, 2 * 512], BF16)
        spw4 = sbt(ctx, ph, "d_spw", [128, 512], BF16)
        spwt4 = sbt(ctx, ph, "d_spwt", [128, 512], BF16)
        ptb4 = sbt(ctx, ph, "d_pt", [128, 2 * 512], BF16)
        rb4 = sbt(ctx, ph, "d_rb", [128, 512], BF16)
        fb4 = sbt(ctx, ph, "d_fb", [128, 512], BF16)
        zb4 = sbt(ctx, ph, "d_zb", [128, 512], BF16)
        ub4 = sbt(ctx, ph, "d_ub", [128, 512], BF16)
        btp4 = sbt(ctx, ph, "d_btp", [128, 2 * 512], BF16)
        s32 = sbt(ctx, ph, "d_s32", [128, 2 * 4 * 128], F32)
        sbf = sbt(ctx, ph, "d_sbf", [128, 2 * 4 * 128], BF16)
        pA = pst(ctx, ph, "d_pA", [128, 512])
        pB = pst(ctx, ph, "d_pB", [128, 512])
        pC = pst(ctx, ph, "d_pC", [128, 512])
        pD = pst(ctx, ph, "d_pD", [128, 512])
        pE = pst(ctx, ph, "d_pE", [128, 512])
        pF = pst(ctx, ph, "d_pF", [128, 512])
        pG = pst(ctx, ph, "d_pG", [128, 512])
        pH = pst(ctx, ph, "d_pH", [128, 512])

        for dst, src, n in ((wq_s, wq_dr, 512), (wk_s, wk_dr, 512), (wv_s, wv_dr, 512),
                            (wb_s, wb_dr, 4), (wg1_s, wg1_dr, 128)):
            ctx.dma("gpsimd", dst[:].rearrange("p (c n) -> p c n", n=n),
                    src.rearrange("(c p) n -> p c n", p=128), inc="d_w")
        ctx.dma("gpsimd", wg2_s[:], wg2_dr, inc="d_w")
        wv_v = ctx.dma("gpsimd", gnf4[:], gnf_dr, inc="d_w")
        for e in ("tensor", "vector", "scalar"):
            ctx.wait(e, "d_w", wv_v)

        sgtb_all = sgt[:].bitcast(BF16)
        sgt4 = sgtb_all[:, 0:512]
        sgt4b = sgtb_all[:, 512:1024]
        pcfree = {0: 0, 1: 0}
        xg_mm = {0: 0, 1: 0}
        for g in range(DN_GROUPS):
            slot = g % 2
            so = slot * 2048
            ctx.wait("gpsimd", "mm", xg_mm[slot])
            xv = ctx.dma("gpsimd",
                         xg[:, slot * 8192:slot * 8192 + 8192].rearrange(
                             "p (c n) -> p c n", n=512),
                         xsrc(g).rearrange("(c p) n -> p c n", p=128), inc=f"d_x{slot}")
            ctx.wait("tensor", f"d_x{slot}", xv)
            for tc4 in range(4):
                xsl = [xg[:, slot * 8192 + k * 512 + tc4 * 128:
                          slot * 8192 + k * 512 + tc4 * 128 + 128] for k in range(NCH)]
                ctx.wait("tensor", "vec")
                ctx.wait("tensor", "scl")
                mvq = mv1 = mvv_ = mvb = mv2 = None
                for k in range(NCH):
                    mvq = ctx.mm(pA[:], xsl[k], wq_s[:, k * 512:k * 512 + 512],
                                 k == 0, k == NCH - 1, inc="mm" if k == NCH - 1 else None)
                for k in range(NCH):
                    mv1 = ctx.mm(pB[:], xsl[k], wk_s[:, k * 512:k * 512 + 512],
                                 k == 0, k == NCH - 1, inc="mm" if k == NCH - 1 else None)
                for k in range(NCH):
                    mvv_ = ctx.mm(pC[:], xsl[k], wv_s[:, k * 512:k * 512 + 512],
                                  k == 0, k == NCH - 1, inc="mm" if k == NCH - 1 else None)
                for k in range(NCH):
                    mvb = ctx.mm(pD[:, 0:4], xsl[k], wb_s[:, k * 4:k * 4 + 4],
                                 k == 0, k == NCH - 1, inc="mm" if k == NCH - 1 else None)
                for k in range(NCH):
                    mv2 = ctx.mm(pE[:, 0:128], wg1_s[:, k * 128:k * 128 + 128], xsl[k],
                                 k == 0, k == NCH - 1, inc="mm" if k == NCH - 1 else None)
                ctx.wait("scalar", "mm", mvq)
                ctx.wait("scalar", "vec")
                sv = ctx.act(sgt[:], pA[:], AF.Sigmoid, inc="scl")
                ctx.wait("vector", "scl", sv)
                qtv = ctx.tt("vector", qg_all[:, so + tc4 * 512:so + tc4 * 512 + 512],
                             pA[:], sgt[:], ALU.mult, inc="vec")
                ctx.wait("scalar", "mm", mv1)
                ctx.wait("scalar", "vec", qtv)
                sv = ctx.act(sgt[:], pB[:], AF.Sigmoid, inc="scl")
                ctx.wait("vector", "scl", sv)
                ctx.tt("vector", ktm_g[:, tc4 * 512:tc4 * 512 + 512], pB[:], sgt[:],
                       ALU.mult, inc="vec")
                ctx.wait("scalar", "mm", mvb)
                bv = ctx.act(bt[:, slot * 16 + tc4 * 4:slot * 16 + tc4 * 4 + 4],
                             pD[:, 0:4], AF.Sigmoid, inc="scl")
                ctx.wait("vector", "scl", bv)
                ctx.ts("vector", bt[:, slot * 16 + tc4 * 4:slot * 16 + tc4 * 4 + 4],
                       bt[:, slot * 16 + tc4 * 4:slot * 16 + tc4 * 4 + 4], 2.0, 1.0,
                       ALU.mult, ALU.mult, inc="vec")
                ctx.wait("vector", "mm", mvv_)
                ctx.copy("vector", vt[:, so + tc4 * 512:so + tc4 * 512 + 512], pC[:],
                         inc="vec")
                ctx.wait("vector", "mm", mv2)
                xv1 = ctx.copy("vector", xg1[:, slot * 512 + tc4 * 128:
                                             slot * 512 + tc4 * 128 + 128],
                               pE[:, 0:128], inc="vec")
                ctx.wait("tensor", "vec", xv1)
                gmv = None
                for h in range(4):
                    gmv = ctx.mm(pF[:, h * 128:h * 128 + 128],
                                 wg2_s[:, h * 128:h * 128 + 128],
                                 xg1[:, slot * 512 + tc4 * 128:slot * 512 + tc4 * 128 + 128],
                                 True, True, inc="mm" if h == 3 else None)
                ctx.wait("scalar", "mm", gmv)
                ctx.act(gate[:, so + tc4 * 512:so + tc4 * 512 + 512], pF[:],
                        AF.Sigmoid, inc="scl")
                for h in range(4):
                    ctx.tt("vector", sgt[:, 0:128],
                           ktm_g[:, tc4 * 512 + h * 128:tc4 * 512 + h * 128 + 128],
                           ktm_g[:, tc4 * 512 + h * 128:tc4 * 512 + h * 128 + 128],
                           ALU.mult)
                    ctx.op("vector", lambda v,
                           o=ksum[:, slot * 16 + tc4 * 4 + h:slot * 16 + tc4 * 4 + h + 1],
                           a=sgt[:, 0:128]: v.reduce_sum(o, a, axis=AX.X), inc="vec")
            vv = ctx.ts("vector", ksum[:, slot * 16:slot * 16 + 16],
                        ksum[:, slot * 16:slot * 16 + 16], 1.0, EPS_DN, ALU.mult,
                        ALU.add, inc="vec")
            ctx.wait("scalar", "vec", vv)
            sv = ctx.act(ksum[:, slot * 16:slot * 16 + 16],
                         ksum[:, slot * 16:slot * 16 + 16], AF.Sqrt, inc="scl")
            ctx.wait("vector", "scl", sv)
            ctx.op("vector", lambda v, o=ksum[:, slot * 16:slot * 16 + 16],
                   a=ksum[:, slot * 16:slot * 16 + 16]: v.reciprocal(o, a), inc="vec")
            pv = [pA[:].bitcast(BF16), pB[:].bitcast(BF16)]
            evv = {}
            ctx.wait("tensor", "vec")
            for tc4 in range(4):
                for h in range(4):
                    i = tc4 * 4 + h
                    ctx.ts("vector", kt[:, so + i * 128:so + i * 128 + 128],
                           ktm_g[:, tc4 * 512 + h * 128:tc4 * 512 + h * 128 + 128],
                           ksum[:, slot * 16 + i:slot * 16 + i + 1], 1.0,
                           ALU.mult, ALU.mult, inc="vec")
            ktv = ctx.val("vec")
            ctx.wait("tensor", "vec", ktv)
            for j in range(32):
                srcb = qg_all if j < 16 else kt
                dstb = qf if j < 16 else kf
                i = j % 16
                if j >= 2:
                    ctx.wait("tensor", "vec", evv[j - 2])
                tv = ctx.trans(pv[j % 2][:, 0:128], srcb[:, so + i * 128:so + i * 128 + 128],
                               C["ident_bf"][:], inc="mm")
                ctx.wait("vector", "mm", tv)
                evv[j] = ctx.copy("vector", dstb[:, so + i * 128:so + i * 128 + 128],
                                  pv[j % 2][:, 0:128], inc="vec")
            ctx.wait("vector", "scl")

            def pc_gen(tc4):
                sl_ = tc4 % 2
                kfs = [kf[:, so + (tc4 * 4 + h) * 128:so + (tc4 * 4 + h) * 128 + 128]
                       for h in range(4)]
                qfs = [qf[:, so + (tc4 * 4 + h) * 128:so + (tc4 * 4 + h) * 128 + 128]
                       for h in range(4)]
                bcol = [bt[:, slot * 16 + tc4 * 4 + h:slot * 16 + tc4 * 4 + h + 1]
                        for h in range(4)]
                ptb = ptb4[:, sl_ * 512:sl_ * 512 + 512]
                ntb = ntb4[:, sl_ * 512:sl_ * 512 + 512]
                mqb = mqb4[:, sl_ * 512:sl_ * 512 + 512]
                btp = btp4[:, sl_ * 512:sl_ * 512 + 512]

                def B4(p, h):
                    return p[:, h * 128:h * 128 + 128]

                ctx.wait("tensor", "vec", pcfree[sl_])
                ctx.wait("tensor", "scl")
                m1 = None
                for h in range(4):
                    ctx.mm(B4(pF, h), kfs[h], kfs[h], True, True)
                for h in range(4):
                    m1 = ctx.mm(B4(pG, h), kfs[h], qfs[h], True, True,
                                inc="mm" if h == 3 else None)
                for h in range(4):
                    ctx.ts("vector", btp[:, h * 128:h * 128 + 128], C["ones_t"][:],
                           bcol[h], 1.0, ALU.mult, ALU.mult)
                yield
                ctx.wait("vector", "mm", m1)
                ctx.tt("vector", sgt4[:], pF[:], btp[:], ALU.mult)
                ctx.tt("vector", nb4[:], sgt4[:], C["mask_bd4n"][:], ALU.mult)
                ctx.tt("vector", fb4[:], sgt4[:], C["mask_of4"][:], ALU.mult)
                v1 = ctx.tt("vector", mqb, pG[:], C["mask_iU4"][:], ALU.mult, inc="vec")
                yield
                ctx.wait("tensor", "vec", v1)
                tv = None
                for h in range(2):
                    tv = ctx.trans([pF, pG][h][:].bitcast(BF16)[:, 0:128],
                                   nb4[:, h * 128:h * 128 + 128], C["ident_bf"][:],
                                   inc="mm" if h == 1 else None)
                yield
                ctx.wait("vector", "mm", tv)
                for h in range(2):
                    ctx.copy("vector", spwt4[:, h * 128:h * 128 + 128],
                             [pF, pG][h][:].bitcast(BF16)[:, 0:128],
                             inc="vec" if h == 1 else None)
                ctx.wait("tensor", "vec")
                tv = None
                for h in range(2, 4):
                    tv = ctx.trans([pF, pG][h - 2][:].bitcast(BF16)[:, 0:128],
                                   nb4[:, h * 128:h * 128 + 128], C["ident_bf"][:],
                                   inc="mm" if h == 3 else None)
                yield
                ctx.wait("vector", "mm", tv)
                for h in range(2, 4):
                    ctx.copy("vector", spwt4[:, h * 128:h * 128 + 128],
                             [pF, pG][h - 2][:].bitcast(BF16)[:, 0:128])
                ctx.copy("vector", spw4[:], nb4[:])
                v1 = ctx.tt("vector", ptb, spwt4[:], C["ident4"][:], ALU.add, inc="vec")
                yield
                for lvl in range(1, 5):
                    ctx.wait("tensor", "vec", v1)
                    mv = None
                    for h in range(4):
                        ctx.mm(B4(pF, h), spwt4[:, h * 128:h * 128 + 128],
                               spw4[:, h * 128:h * 128 + 128], True, True)
                    if lvl < 4:
                        for h in range(4):
                            mv = ctx.mm(B4(pG, h), spw4[:, h * 128:h * 128 + 128],
                                        spwt4[:, h * 128:h * 128 + 128], True, True,
                                        inc="mm" if h == 3 else None)
                    else:
                        mv = ctx.mm(B4(pF, 3), spwt4[:, 384:512], spw4[:, 384:512],
                                    True, True, inc="mm")
                    yield
                    ctx.wait("vector", "mm", mv)
                    ctx.copy("vector", spw4[:], pF[:])
                    if lvl < 4:
                        v1 = ctx.copy("vector", spwt4[:], pG[:], inc="vec")
                    else:
                        v1 = ctx.copy("vector", spw4[:, 0:1], spw4[:, 0:1], inc="vec")
                    yield
                    ctx.wait("tensor", "vec", v1)
                    mv = None
                    for h in range(4):
                        mv = ctx.mm(B4(pH, h), spw4[:, h * 128:h * 128 + 128],
                                    ptb[:, h * 128:h * 128 + 128], True, True,
                                    inc="mm" if h == 3 else None)
                    yield
                    ctx.wait("vector", "mm", mv)
                    v1 = ctx.tt("vector", ptb, pH[:], ptb, ALU.add, inc="vec")
                yield
                ctx.wait("tensor", "vec", v1)
                mv = None
                for h in range(4):
                    mv = ctx.mm(B4(pH, h), fb4[:, h * 128:h * 128 + 128],
                                ptb[:, h * 128:h * 128 + 128], True, True,
                                inc="mm" if h == 3 else None)
                yield
                ctx.wait("vector", "mm", mv)
                pcfree[sl_] = ctx.copy("vector", ntb, pH[:], inc="vec")
                yield

            def serial_gen(tc4):
                sl_ = tc4 % 2
                first = (g == 0 and tc4 == 0)
                par = (g * 4 + tc4) % 2
                kfs = [kf[:, so + (tc4 * 4 + h) * 128:so + (tc4 * 4 + h) * 128 + 128]
                       for h in range(4)]
                qfs = [qf[:, so + (tc4 * 4 + h) * 128:so + (tc4 * 4 + h) * 128 + 128]
                       for h in range(4)]
                kts = [kt[:, so + (tc4 * 4 + h) * 128:so + (tc4 * 4 + h) * 128 + 128]
                       for h in range(4)]
                soldbf = [sbf[:, par * 512 + h * 128:par * 512 + h * 128 + 128]
                          for h in range(4)]
                ptb = ptb4[:, sl_ * 512:sl_ * 512 + 512]
                ntb = ntb4[:, sl_ * 512:sl_ * 512 + 512]
                mqb = mqb4[:, sl_ * 512:sl_ * 512 + 512]
                btp = btp4[:, sl_ * 512:sl_ * 512 + 512]
                vt_w = vt[:, so + tc4 * 512:so + tc4 * 512 + 512]

                def B4(p, h):
                    return p[:, h * 128:h * 128 + 128]

                ctx.wait("tensor", "vec")
                if not first:
                    mv = None
                    for h in range(4):
                        mv = ctx.mm(B4(pA, h), kfs[h], soldbf[h], True, True,
                                    inc="mm" if h == 3 else None)
                    yield
                    ctx.wait("vector", "mm", mv)
                    ctx.stt("vector", sgt4b[:], pA[:], -1.0, vt_w, ALU.mult, ALU.add)
                    rv = ctx.tt("vector", rb4[:], sgt4b[:], btp, ALU.mult, inc="vec")
                else:
                    rv = ctx.tt("vector", rb4[:], vt_w, btp, ALU.mult, inc="vec")
                yield
                ctx.wait("tensor", "vec", rv)
                mv = None
                for h in range(4):
                    mv = ctx.mm(B4(pB, h), ptb[:, h * 128:h * 128 + 128],
                                rb4[:, h * 128:h * 128 + 128], True, True,
                                inc="mm" if h == 3 else None)
                yield
                ctx.wait("vector", "mm", mv)
                zv = ctx.copy("vector", zb4[:], pB[:], inc="vec")
                yield
                usrc4 = zb4
                for it in range(3):
                    udst4 = ub4 if it == 2 else (rb4 if it % 2 == 0 else ub4)
                    ctx.wait("tensor", "vec", zv)
                    mv = None
                    for h in range(4):
                        mv = ctx.mm(B4(pC, h), ntb[:, h * 128:h * 128 + 128],
                                    usrc4[:, h * 128:h * 128 + 128], True, True,
                                    inc="mm" if h == 3 else None)
                    yield
                    ctx.wait("vector", "mm", mv)
                    zv = ctx.stt("vector", udst4[:], pC[:], -1.0, zb4[:], ALU.mult,
                                 ALU.add, inc="vec")
                    usrc4 = udst4
                yield
                ctx.wait("tensor", "vec", zv)
                mv = None
                for h in range(4):
                    mv = ctx.mm(B4(pD, h), kts[h], ub4[:, h * 128:h * 128 + 128],
                                True, True, inc="mm" if h == 3 else None)
                yield
                ctx.wait("vector", "mm", mv)
                s32o = s32[:, par * 512:par * 512 + 512]
                s32n = s32[:, (1 - par) * 512:(1 - par) * 512 + 512]
                sbfn = sbf[:, (1 - par) * 512:(1 - par) * 512 + 512]
                if not first:
                    ctx.tt("vector", s32n, pD[:], s32o, ALU.add)
                else:
                    ctx.copy("vector", s32n, pD[:])
                sv_ = ctx.copy("vector", sbfn, s32n, inc="vec")
                yield
                ctx.wait("tensor", "vec", sv_)
                mv = None
                for h in range(4):
                    if not first:
                        ctx.mm(B4(pE, h), soldbf[h], qfs[h], True, False)
                        mv = ctx.mm(B4(pE, h), ub4[:, h * 128:h * 128 + 128],
                                    mqb[:, h * 128:h * 128 + 128], False, True,
                                    inc="mm" if h == 3 else None)
                    else:
                        mv = ctx.mm(B4(pE, h), ub4[:, h * 128:h * 128 + 128],
                                    mqb[:, h * 128:h * 128 + 128], True, True,
                                    inc="mm" if h == 3 else None)
                yield
                ctx.wait("vector", "mm", mv)
                ogw = ogb[:, tc4 * 512:tc4 * 512 + 512]
                ctx.tt("vector", ogw, pE[:], gate[:, so + tc4 * 512:so + tc4 * 512 + 512],
                       ALU.mult)
                gv = ctx.tt("vector", sgt4b[:], ogw, ogw, ALU.mult, inc="vec")
                yield
                ctx.wait("tensor", "vec", gv)
                mv = ctx.mm(pA[0:1, 0:512], C["ones_col_bf"][:], sgt4b[:], True, True,
                            inc="mm")
                yield
                ctx.wait("vector", "mm", mv)
                ctx.op("vector", lambda v, o=rowb[:, tc4 * 512:tc4 * 512 + 512],
                       a=pA[0:1, 0:512]: v.tensor_copy(o, a), inc="vec")
                yield

            for t in range(DN_CHUNKS + 1):
                gens = []
                if t < DN_CHUNKS:
                    gens.append(pc_gen(t))
                if t > 0:
                    gens.append(serial_gen(t - 1))
                while gens:
                    nxt = []
                    for gg in gens:
                        try:
                            next(gg)
                            nxt.append(gg)
                        except StopIteration:
                            pass
                    gens = nxt
            if DN_CHUNKS < 4:
                continue
            vv = ctx.val("vec")
            ctx.wait("scalar", "vec", vv)
            sv = ctx.ts("vector", rowb[:], rowb[:], 1.0 / 128.0, EPS_DN, ALU.mult,
                        ALU.add, inc="vec")
            ctx.wait("scalar", "vec", sv)
            sv = ctx.act(rowb[:], rowb[:], AF.Sqrt, inc="scl")
            ctx.wait("vector", "scl", sv)
            rv = ctx.op("vector", lambda v, o=rowb[:], a=rowb[:]: v.reciprocal(o, a),
                        inc="vec")
            ctx.wait("tensor", "vec", rv)
            for tc4 in range(4):
                bv = ctx.mm(pH[:], C["ones_row_f32"][:],
                            rowb[:, tc4 * 512:tc4 * 512 + 512], True, True, inc="mm")
                ctx.wait("vector", "mm", bv)
                ogw = ogb[:, tc4 * 512:tc4 * 512 + 512]
                ctx.tt("vector", ogw, ogw, pH[:], ALU.mult)
                ov = ctx.tt("vector", ogw, ogw, gnf4[:], ALU.mult, inc="vec")
                ctx.wait("tensor", "vec", ov)
                pcfree[0] = pcfree[1] = ov
            ovv = ctx.val("vec")
            ctx.wait("sync", "vec", ovv)
            half = cc1_dr[0] if g < 4 else cc1_dr[1]
            for h in range(4):
                r0 = ((g // 2) % 2) * 512 + h * 128
                c0 = (g % 2) * 512
                sv_ = ctx.dma("sync",
                              half[r0:r0 + 128, c0:c0 + 512]
                              .rearrange("q (a n) -> q a n", a=4),
                              ogb[:].rearrange("p (a c n) -> p a c n", a=4, c=4)[:, :, h, :],
                              inc="d_og")
            ctx.wait("vector", "d_og")
            if half_done is not None and g in (3, 7):
                half_done(g // 4)
            xg_mm[slot] = ctx.val("mm")
        ctx.barrier()


def emit_wo_ln(ctx, es, out_es, C, og_dr, resid_ap, wo_dr, l_norm, sel, tag,
               cc_halves=None):
    BLK = 516 if l_norm else 512
    r_fm = sbt(ctx, out_es, f"w_r{tag}", [128, NCH * TOK], BF16)
    with ExitStack() as ph:
        og_s = sbt(ctx, ph, f"w_og{tag}", [128, NCH * TOK], BF16)
        nslot = 2
        og4 = sbt(ctx, ph, f"w_og4{tag}", [128, nslot * 2 * TOK], BF16)
        res_s = sbt(ctx, ph, f"w_res{tag}", [128, NCH * TOK], BF16)
        wot = sbt(ctx, ph, f"w_wo{tag}", [128, 2 * 16 * 512], BF16)
        if l_norm:
            lr16 = sbt(ctx, ph, f"w_lr16{tag}", [16, 4 * TOK], BF16)
            lrs = sbt(ctx, ph, f"w_lrs{tag}", [16, TOK], F32)
            lr0 = sbt(ctx, ph, f"w_lr0{tag}", [1, 16 * TOK], BF16)
        p_w = [pst(ctx, ph, f"w_pw{tag}{i}", [128, 512]) for i in range(2)]
        p_rl = pst(ctx, ph, f"w_prl{tag}", [128, 512])
        for k in range(NCH):
            ov = ctx.dma("gpsimd", res_s[:, k * TOK:(k + 1) * TOK], resid_ap(k),
                         inc=f"w_ld{tag}")
        sumv = {0: 0, 1: 0}
        ov_ll = 0
        for hf in range(2):
            if cc_halves is not None:
                ctx.wait("gpsimd", "cc", cc_halves[hf])
                ctx.wait("sync", "cc", cc_halves[hf])
            wA, wB = 2 * hf, 2 * hf + 1
            for k in range(NCH):
                r_, lc = k // 4, k % 4
                sl_ = k % nslot
                o4 = og4[:, sl_ * 2 * TOK:(sl_ + 1) * 2 * TOK]
                ctx.wait("gpsimd", "vec", sumv[sl_])
                ctx.wait("sync", "vec", sumv[sl_])
                for wi, w in enumerate((wA, wB)):
                    eng = "gpsimd" if wi == 0 else "sync"
                    ldv = ctx.dma(eng, o4[:, wi * TOK:(wi + 1) * TOK],
                                  og_dr(r_, w)[lc * 128:lc * 128 + 128, :],
                                  inc=f"w_g{tag}{eng[0]}{sl_}")
                    ctx.wait("vector", f"w_g{tag}{eng[0]}{sl_}", ldv)
                osl = og_s[:, k * TOK:(k + 1) * TOK]
                if hf == 0:
                    ctx.ts("vector", osl, o4[:, 0:TOK], sel[:, 0:1], 1.0,
                           ALU.mult, ALU.mult)
                else:
                    ctx.stt("vector", osl, o4[:, 0:TOK], sel[:, 2:3], osl,
                            ALU.mult, ALU.add)
                sumv[sl_] = ctx.stt("vector", osl, o4[:, TOK:2 * TOK],
                                    sel[:, wB:wB + 1], osl, ALU.mult, ALU.add,
                                    inc="vec")
            if l_norm:
                for w in (wA, wB):
                    for r_ in range(4):
                        ov_ll = ctx.dma("gpsimd",
                                        lr16[r_ * 4:r_ * 4 + 4, w * TOK:(w + 1) * TOK],
                                        og_dr(r_, w)[512:516, :], inc=f"w_ll{tag}")
        ogv = ctx.val("vec")
        if l_norm:
            ctx.wait("vector", f"w_ll{tag}", ov_ll)
            ctx.ts("vector", lrs[:], lr16[:, 0:TOK], sel[0:16, 0:1], 1.0,
                   ALU.mult, ALU.mult)
            for w in range(1, 4):
                ctx.stt("vector", lrs[:], lr16[:, w * TOK:(w + 1) * TOK],
                        sel[0:16, w:w + 1], lrs[:], ALU.mult, ALU.add)
            rv = ctx.op("vector", lambda v, o=lrs[:], a=lrs[:]:
                        v.reciprocal(o, a), inc="vec")
            ctx.wait("gpsimd", "vec", rv)
            for k in range(16):
                ov = ctx.dma("gpsimd", lr0[:, k * TOK:(k + 1) * TOK],
                             lrs[k:k + 1, :], inc=f"w_lr{tag}")
            ctx.wait("tensor", f"w_lr{tag}", ov)
            ctx.wait("tensor", f"w_ld{tag}")
            ctx.wait("tensor", "vec", ogv)
            for k in range(NCH):
                for th in range(2):
                    bv = ctx.mm(p_rl[:], C["ones_row_bf"][:],
                                lr0[:, k * TOK + th * 512:k * TOK + th * 512 + 512],
                                True, True, inc="mm")
                    ctx.wait("vector", "mm", bv)
                    osl = og_s[:, k * TOK + th * 512:k * TOK + th * 512 + 512]
                    nv = ctx.tt("vector", osl, osl, p_rl[:], ALU.mult, inc="vec")
                    ctx.wait("tensor", "vec", nv)
        else:
            ctx.wait("tensor", f"w_ld{tag}")
            ctx.wait("tensor", "vec", ogv)
        wo_mm = {0: 0, 1: 0}
        wev = {0: ctx.val("vec"), 1: ctx.val("vec")}
        for ds in range(4):
            slot = ds % 2
            ctx.wait("gpsimd", "mm", wo_mm[slot])
            o = wot[:, slot * 8192:slot * 8192 + 8192].rearrange("p (c n) -> p c n", n=512)
            i = wo_dr[:, ds * 512:ds * 512 + 512].rearrange("(c p) n -> p c n", p=128)
            wv = ctx.dma("gpsimd", o, i, inc=f"w_w{tag}{slot}")
            ctx.wait("tensor", f"w_w{tag}{slot}", wv)
            for dc in range(4):
                dci = ds * 4 + dc
                for th in range(2):
                    pw = p_w[th]
                    ctx.wait("tensor", "vec", wev[th])
                    mv = None
                    for k in range(NCH):
                        off = slot * 8192 + (k * 4 + dc) * 128
                        mv = ctx.mm(pw[:], wot[:, off:off + 128],
                                    og_s[:, k * TOK + th * 512:k * TOK + th * 512 + 512],
                                    k == 0, k == NCH - 1,
                                    inc="mm" if k == NCH - 1 else None)
                    wo_mm[slot] = mv
                    ctx.wait("vector", "mm", mv)
                    wev[th] = ctx.tt("vector",
                                     r_fm[:, dci * TOK + th * 512:dci * TOK + th * 512 + 512],
                                     pw[:], res_s[:, dci * TOK + th * 512:dci * TOK + th * 512 + 512],
                                     ALU.add, inc="vec")
        ctx.barrier()
        emit_ln_fm(ctx, ph, C, r_fm, TOK, f"w{tag}", ln_sink_simple(ctx, r_fm, TOK))
        ctx.barrier()
    return r_fm



def build_full():
    nc = bass.Bass("TRN2", target_bir_lowering=False)
    dr = {}

    def ein(name, shape, dt):
        dr[name] = nc.dram_tensor(name, shape, dt, kind="ExternalInput").ap()
        return dr[name]

    for k, (shape, dt) in CONST_SPECS.items():
        ein(k, shape, dt)
    ein("x_fm", [D, SEQ], F32)
    ein("x_res", [D, TOK], F32)
    for nm in ("wq0", "wk0", "wv0"):
        ein(nm, [D, 512], F32)
    ein("wb0", [D, 4], F32)
    ein("wg1t", [D, 128], F32)
    ein("wg2sl", [128, 512], F32)
    ein("gnf", [128, 512], F32)
    ein("wo0f", [D, D], F32)
    ein("gate_w", [D, 8], F32)
    ein("ltmask", [128, 64], F32)
    ein("wg_f", [8 * D, 1408], F32)
    ein("wu_f", [8 * D, 1408], F32)
    ein("wd_f", [8 * 1408, D], F32)
    ein("swg_f", [D, 2816], F32)
    ein("swu_f", [D, 2816], F32)
    ein("swd_f", [2816, D], F32)
    for nm in ("wq1", "wk1", "wv1"):
        ein(nm, [D, 512], F32)
    ein("wo1f", [D, D], F32)
    ein("wg8f", [D, 8192], I8)
    ein("wu8f", [D, 8192], I8)
    ein("wd8f", [8192, D], I8)
    ein("sel", [128, 4], F32)
    ein("sg_t", [128, 64], F32)
    ein("su_t", [128, 64], F32)
    ein("sd_t", [128, 16], F32)
    out = nc.dram_tensor("out", [TOK, D], F32, kind="ExternalOutput").ap()

    cc1_inA = nc.dram_tensor("cc1_inA", [1, 1024 * TOK], BF16)
    cc1_inB = nc.dram_tensor("cc1_inB", [1, 1024 * TOK], BF16)
    cc1_outA = nc.dram_tensor("cc1_outA", [4, 1024 * TOK], BF16)
    cc1_outB = nc.dram_tensor("cc1_outB", [4, 1024 * TOK], BF16)
    cc2_in = nc.dram_tensor("cc2_in", [1, D * TOK], BF16)
    cc2_out = nc.dram_tensor("cc2_out", [4, D * TOK], BF16)
    cc3_inA = nc.dram_tensor("cc3_inA", [1, 2 * 516 * TOK], BF16)
    cc3_inB = nc.dram_tensor("cc3_inB", [1, 2 * 516 * TOK], BF16)
    cc3_outA = nc.dram_tensor("cc3_outA", [4, 2 * 516 * TOK], BF16)
    cc3_outB = nc.dram_tensor("cc3_outB", [4, 2 * 516 * TOK], BF16)
    x2sav = nc.dram_tensor("x2sav", [D, TOK], BF16)
    cc1_inv = (cc1_inA.ap().rearrange("o (b t) -> (o b) t", t=TOK),
               cc1_inB.ap().rearrange("o (b t) -> (o b) t", t=TOK))
    cc3_inv = (cc3_inA.ap().rearrange("o (b t) -> (o b) t", t=TOK),
               cc3_inB.ap().rearrange("o (b t) -> (o b) t", t=TOK))
    cc2_inv = cc2_in.ap().rearrange("o (r t) -> (o r) t", t=TOK)
    cc2_outv = cc2_out.ap().rearrange("r (b t) -> (r b) t", t=TOK)

    def mk_outv(tA, tB):
        vA = tA.ap().rearrange("r (w b t) -> r w b t", w=2, t=TOK)
        vB = tB.ap().rearrange("r (w b t) -> r w b t", w=2, t=TOK)

        def f(r, w):
            return (vA if w < 2 else vB)[r, w % 2]
        return f

    cc1_outv = mk_outv(cc1_outA, cc1_outB)
    cc3_outv = mk_outv(cc3_outA, cc3_outB)

    G8 = [list(range(8))]
    G4 = [[0, 1, 2, 3], [4, 5, 6, 7]]

    es = ExitStack()
    ctx = Ctx(nc, es)
    C = emit_load_consts(ctx, es, dr)
    gws = sbt(ctx, es, "gws", [128, NCH * 8], BF16)
    ltm = sbt(ctx, es, "ltm", [128, 64], F32)
    sel = sbt(ctx, es, "sel_s", [128, 4], F32)
    ctx.dma("gpsimd", sel[:], dr["sel"], inc="ldc")
    ctx.dma("gpsimd", gws[:].rearrange("p (c t) -> p c t", t=8),
            dr["gate_w"].rearrange("(c p) t -> p c t", p=128), inc="ldc")
    ctx.dma("gpsimd", ltm[:], dr["ltmask"], inc="ldc")

    for e_ in ("tensor", "vector", "scalar", "sync", "gpsimd"):
        ctx.wait(e_, "ldc")

    cc1_vals = []

    def og_half(idx):
        ctx.wait("gpsimd", "d_og")
        t_in = [cc1_inA, cc1_inB][idx]
        t_out = [cc1_outA, cc1_outB][idx]
        ctx.op("gpsimd", lambda g, i=t_in, o=t_out: g.collective_compute(
            "AllGather", ALU.bypass, replica_groups=G4,
            ins=[i.ap().opt()], outs=[o.ap().opt()]), inc="cc", chain=False)
        cc1_vals.append(ctx.val("cc"))

    emit_deltanet(ctx, es, C, lambda g: dr["x_fm"][:, g * 512:(g + 1) * 512],
                  dr["wq0"], dr["wk0"], dr["wv0"], dr["wb0"], dr["wg1t"],
                  dr["wg2sl"], dr["gnf"], cc1_inv, half_done=og_half)

    es_x2 = ExitStack()
    x2buf = sbt(ctx, es_x2, "x2buf", [128, NCH * TOK], BF16)
    es_x1 = ExitStack()
    x1 = emit_wo_ln(ctx, es, es_x1, C, cc1_outv,
                    lambda k: dr["x_res"][k * 128:(k + 1) * 128, :],
                    dr["wo0f"], False, sel, "0", cc_halves=cc1_vals)

    x2 = emit_moe(ctx, es_x2, C, x1, gws, ltm, dr["wg_f"], dr["wu_f"],
                  dr["wd_f"], dr["swg_f"], dr["swu_f"], dr["swd_f"],
                  r_fm=x2buf)
    es_x1.close()
    x2v = ctx.dma("gpsimd", cc2_inv.rearrange("(c p) t -> p c t", p=128),
                  x2[:].rearrange("p (c t) -> p c t", t=TOK), inc="x2s")
    sv2 = ctx.dma("gpsimd", x2sav[:, :].rearrange("(c p) t -> p c t", p=128),
                  x2[:].rearrange("p (c t) -> p c t", t=TOK), inc="x2p")
    ctx.wait("gpsimd", "x2s", x2v)
    ctx.op("gpsimd", lambda g: g.collective_compute(
        "AllGather", ALU.bypass, replica_groups=G4,
        ins=[cc2_in.ap().opt()], outs=[cc2_out.ap().opt()]), inc="cc", chain=False)
    cc_n = ctx.val("cc")
    x2p_n = sv2
    ctx.wait("vector", "x2s", x2v)
    ctx.wait("vector", "x2p", sv2)
    ctx.barrier(exclude=("cc",))
    es_x2.close()

    cc3_vals = []

    def o1_half(idx):
        ctx.wait("gpsimd", "a_st")
        t_in = [cc3_inA, cc3_inB][idx]
        t_out = [cc3_outA, cc3_outB][idx]
        ctx.op("gpsimd", lambda g, i=t_in, o=t_out: g.collective_compute(
            "AllGather", ALU.bypass, replica_groups=G4,
            ins=[i.ap().opt()], outs=[o.ap().opt()]), inc="cc", chain=False)
        cc3_vals.append(ctx.val("cc"))

    emit_mha(ctx, es, C,
             lambda g: cc2_outv[(g // 2) * D:(g // 2) * D + D,
                                  (g % 2) * 512:(g % 2) * 512 + 512],
             dr["wq1"], dr["wk1"], dr["wv1"], cc3_inv,
             pre_x=lambda: ctx.wait("gpsimd", "cc", cc_n), half_done=o1_half)

    sgt = sbt(ctx, es, "sgt_s", [128, 64], F32)
    sut = sbt(ctx, es, "sut_s", [128, 64], F32)
    sdt = sbt(ctx, es, "sdt_s", [128, 16], F32)
    ctx.dma("gpsimd", sgt[:], dr["sg_t"])
    ctx.dma("gpsimd", sut[:], dr["su_t"])
    ctx.dma("gpsimd", sdt[:], dr["sd_t"])
    es_x3 = ExitStack()
    x3 = emit_wo_ln(ctx, es, es_x3, C, cc3_outv,
                    lambda k: x2sav.ap()[k * 128:(k + 1) * 128, :],
                    dr["wo1f"], True, sel, "1", cc_halves=cc3_vals)
    for e in ("tensor", "vector", "scalar"):
        ctx.wait(e, "ld")
    emit_l1ffn(ctx, es, C, x3, dr["wg8f"], dr["wu8f"], dr["wd8f"],
               sgt, sut, sdt, out)
    es_x3.close()
    ctx.wait("sync", "st")
    ctx.finalize()
    es.close()
    return nc


_NC_CACHE = []


def kernel(**inputs):
    g = dict(inputs)
    consts = host_consts()
    ii = np.arange(8)
    ltm = np.zeros((128, 64), np.float32)
    for j in range(8):
        ltm[:, j * 8:(j + 1) * 8] = (ii[None, :] > j)

    x = np.asarray(g["L0_attn_x"], np.float32)
    A = lambda a: np.ascontiguousarray(a)
    wo0f = A(np.asarray(g["L0_attn_Wo"]).T)
    wo1f = A(np.asarray(g["L1_attn_Wo"]).T)
    wg_f = A(np.asarray(g["L0_ffn_w_gate"]).reshape(8 * 2048, 1408))
    wu_f = A(np.asarray(g["L0_ffn_w_up"]).reshape(8 * 2048, 1408))
    wd_f = A(np.asarray(g["L0_ffn_w_down"]).reshape(8 * 1408, 2048))
    swg_f = A(np.asarray(g["L0_ffn_sw_gate"]))
    swu_f = A(np.asarray(g["L0_ffn_sw_up"]))
    swd_f = A(np.asarray(g["L0_ffn_sw_down"]))
    wg8f = A(np.asarray(g["L1_ffn_w_gate"]))
    wu8f = A(np.asarray(g["L1_ffn_w_up"]))
    wd8f = A(np.asarray(g["L1_ffn_w_down"]))
    gate_w = A(np.asarray(g["L0_ffn_gate_w"]))
    in_maps = []
    for c in range(8):
        beta, lane = c // 4, c % 4
        cs = slice(512 * lane, 512 * lane + 512)
        hs = slice(4 * lane, 4 * lane + 4)
        xT = x[beta].T
        m = dict(consts)
        m["ltmask"] = ltm
        m["x_fm"] = A(xT)
        m["x_res"] = A(xT[:, lane * 1024:lane * 1024 + 1024])
        m["wq0"] = A(np.asarray(g["L0_attn_Wq"])[cs, :].T)
        m["wk0"] = A(np.asarray(g["L0_attn_Wk"])[cs, :].T)
        m["wv0"] = A(np.asarray(g["L0_attn_Wv"])[cs, :].T)
        m["wb0"] = A(np.asarray(g["L0_attn_Wbeta"])[hs, :].T)
        m["wg1t"] = A(np.asarray(g["L0_attn_Wg1"]).T)
        m["wg2sl"] = A(np.asarray(g["L0_attn_Wg2"])[cs, :].T)
        gn_ = np.asarray(g["L0_attn_g_norm"])[cs].reshape(4, 128)
        m["gnf"] = A(np.concatenate([np.tile(gn_[h][:, None], (1, 128))
                                     for h in range(4)], axis=1))
        m["wo0f"] = wo0f
        m["gate_w"] = gate_w
        m["wg_f"] = wg_f
        m["wu_f"] = wu_f
        m["wd_f"] = wd_f
        m["swg_f"] = swg_f
        m["swu_f"] = swu_f
        m["swd_f"] = swd_f
        m["wq1"] = A(np.asarray(g["L1_attn_Wq"])[cs, :].T)
        m["wk1"] = A(np.asarray(g["L1_attn_Wk"])[cs, :].T)
        m["wv1"] = A(np.asarray(g["L1_attn_Wv"])[cs, :].T)
        m["wo1f"] = wo1f
        m["wg8f"] = wg8f
        m["wu8f"] = wu8f
        m["wd8f"] = wd8f
        selv = np.zeros((128, 4), np.float32)
        selv[:, lane] = 1.0
        m["sel"] = selv
        m["sg_t"] = A(np.asarray(g["L1_ffn_s_gate"], np.float32).reshape(64, 128).T)
        m["su_t"] = A(np.asarray(g["L1_ffn_s_up"], np.float32).reshape(64, 128).T)
        m["sd_t"] = A(np.asarray(g["L1_ffn_s_down"], np.float32).reshape(16, 128).T)
        in_maps.append(m)

    if not _NC_CACHE:
        _NC_CACHE.append(build_full())
    nc = _NC_CACHE[0]
    res = run_bass_kernel_spmd(nc, in_maps, core_ids=list(range(8)))
    out = np.concatenate([np.asarray(res.results[c]["out"]) for c in range(8)], axis=0)
    return out.reshape(2, SEQ, D)
